# version 05
# baseline (speedup 1.0000x reference)
import numpy as np
from contextlib import ExitStack
import concourse.bass as bass
import concourse.mybir as mybir
from concourse.bass_utils import run_bass_kernel_spmd

F32 = mybir.dt.float32
BF16 = mybir.dt.bfloat16
I32 = mybir.dt.int32
U32 = mybir.dt.uint32
AF = mybir.ActivationFunctionType
OP = mybir.AluOpType

C = 1024
T = 4096
NCH = 8
TG = 512
NTG = T // TG
LN_EPS = 1e-5
ATTN_EPS = 1e-6
NEGBIG = -1e30

RQ0, RQ1, RQ2 = 0.7765257306659091, -0.17066219872211436, 0.011377491746806564
RSQRT_C = 1.8369135901471134e-20

ENGINES = ("tensor", "vector", "scalar", "gpsimd", "sync")


class Prog:
    def __init__(self, nc):
        self.nc = nc
        self._ctx = []
        self.sem = {}
        self.cnt = {}
        self.bar_round = 0

    def keep(self, cm):
        v = cm.__enter__()
        self._ctx.append(cm)
        return v

    def add_sems(self, *names):
        for n in names:
            self.sem[n] = self.keep(self.nc.semaphore(n))
            self.cnt[n] = 0

    def close(self):
        for cm in reversed(self._ctx):
            cm.__exit__(None, None, None)

    def barrier(self):
        self.bar_round += 1
        target = self.bar_round * len(ENGINES)
        bar = self.sem["bar"]
        dummy = self.bar_dummy
        ones = self.bar_ones
        with self.nc.Block(f"bar{self.bar_round}") as blk:
            @blk.tensor
            def _(te):
                te.ldweights(ones[:, 0:1])
                te.drain().then_inc(bar, 1)
                te.wait_ge(bar, target)

            @blk.vector
            def _(ve):
                ve.memset(dummy[:, 0:1], 0.0)
                ve.drain().then_inc(bar, 1)
                ve.wait_ge(bar, target)

            @blk.scalar
            def _(se):
                se.activation(out=dummy[:, 1:2], in_=dummy[:, 1:2],
                              func=AF.Copy, bias=0.0, scale=1.0)
                se.drain().then_inc(bar, 1)
                se.wait_ge(bar, target)

            @blk.gpsimd
            def _(gp):
                gp.memset(dummy[:, 2:3], 0.0)
                gp.drain().then_inc(bar, 1)
                gp.wait_ge(bar, target)

            @blk.sync
            def _(sy):
                sy.sem_inc(bar, 1)
                sy.wait_ge(bar, target)
        self.cnt["bar"] = target


class _Rec:

    def __init__(self, eng):
        object.__setattr__(self, "_eng", eng)
        object.__setattr__(self, "last", None)

    def __getattr__(self, n):
        a = getattr(object.__getattribute__(self, "_eng"), n)
        if callable(a):
            def call(*args, **kw):
                r = a(*args, **kw)
                if hasattr(r, "then_inc"):
                    object.__setattr__(self, "last", r)
                return r
            return call
        return a


class Sched:

    def __init__(self, p, name):
        self.p = p
        self.name = name
        self.steps = {e: [] for e in ENGINES}
        self.labels = {}

    def add(self, eng, fn, waits=(), label=None, dma=None, n_dma=1):
        p = self.p
        sem_name = dma if dma else "e_" + eng
        inc = 16 * n_dma if dma else 1
        p.cnt[sem_name] += inc
        tgt = p.cnt[sem_name]
        if label is not None:
            assert label not in self.labels, label
            self.labels[label] = (sem_name, tgt)
        self.steps[eng].append((fn, tuple(waits), sem_name, tgt, dma, n_dma))

    def emit(self):
        p = self.p
        with p.nc.Block(self.name) as blk:
            for en in ENGINES:
                if not self.steps[en]:
                    continue

                def _mk(en):
                    def _(eng):
                        for fn, waits, sem_name, tgt, dma, n_dma in \
                                self.steps[en]:
                            for w in waits:
                                ws, wv = self.labels[w]
                                eng.wait_ge(p.sem[ws], wv)
                            if dma is not None:
                                instrs = fn(eng)
                                if not isinstance(instrs, (list, tuple)):
                                    instrs = [instrs]
                                assert len(instrs) == n_dma, \
                                    (self.name, n_dma, len(instrs))
                                for ins in instrs:
                                    ins.then_inc(p.sem[dma], 16)
                            elif en == "sync":
                                rec = _Rec(eng)
                                fn(rec)
                                last = object.__getattribute__(rec, "last")
                                if last is not None:
                                    last.then_inc(p.sem[sem_name], 1)
                                else:
                                    eng.sem_inc(p.sem[sem_name], 1)
                            else:
                                fn(eng)
                                eng.drain().then_inc(p.sem[sem_name], 1)
                    return _
                getattr(blk, en)(_mk(en))


def _width(ap):
    n = 1
    for d in ap.shape[1:]:
        n *= d
    return n


def emit_rsqrt(ve, out, v, z, a, xor_31, iters=3):
    gap = (lambda: ve.drain()) if _width(v) < 64 else (lambda: 0)
    ve.tensor_single_scalar(z.bitcast(I32), v.bitcast(I32), 1,
                            OP.logical_shift_right)
    gap()
    ve.tensor_scalar(z.bitcast(I32), z.bitcast(I32), xor_31, None,
                     OP.bitwise_xor)
    gap()
    ve.tensor_scalar_mul(z, z, RSQRT_C)
    for i in range(iters):
        last = i == iters - 1
        gap()
        ve.tensor_mul(a, z, z)
        gap()
        ve.tensor_mul(a, a, v)
        gap()
        ve.tensor_scalar(a, a, 3.0, -0.5, OP.subtract, OP.mult)
        gap()
        ve.tensor_mul(out if last else z, z, a)


def emit_recip(ve, out, x, r, u, xor_exp, iters=3):
    gap = (lambda: ve.drain()) if _width(x) < 64 else (lambda: 0)
    ve.tensor_scalar(r.bitcast(I32), x.bitcast(I32), xor_exp, None,
                     OP.bitwise_xor)
    gap()
    ve.tensor_mul(u, r, x)
    gap()
    ve.tensor_scalar(out, u, RQ2, RQ1, OP.mult, OP.add)
    gap()
    ve.tensor_mul(out, out, u)
    gap()
    ve.tensor_scalar_add(out, out, RQ0)
    gap()
    ve.tensor_mul(r, out, r)
    for i in range(iters):
        last = i == iters - 1
        gap()
        ve.tensor_mul(u, r, x)
        gap()
        ve.tensor_scalar(u, u, 2.0, -1.0, OP.subtract, OP.mult)
        gap()
        ve.tensor_mul(out if last else r, r, u)



def build_program(debug=(), phases=("ln", "attn", "f1", "f2")):
    nc = bass.Bass("TRN2", target_bir_lowering=False,
                   detect_race_conditions=False)
    p = Prog(nc)
    p.add_sems("bar", "e_tensor", "e_vector", "e_scalar", "e_gpsimd",
               "e_sync", "dxi", "dxn", "dwt", "dout", "dtr")

    d = {}

    def din(name, shape, dtype=F32):
        d[name] = nc.dram_tensor(name, shape, dtype, kind="ExternalInput").ap()

    din("x", [C, T])
    din("filt", [C, 16])
    din("gb", [C, 2])
    din("wq", [C, 512])
    din("keysT", [2, 4, 64, 64])
    din("dnT", [C, 4096])
    din("ue", [4096, C])
    din("wg", [C, 4096])
    din("wu", [C, 4096])
    din("wd", [4096, C])
    din("bgu", [4096, 2])
    din("bd", [C, 1])

    d["xn_c"] = nc.dram_tensor("xn_c", [C, T], BF16, kind="Internal").ap()
    d["s_c"] = nc.dram_tensor("s_c", [C, T], BF16, kind="Internal").ap()
    d["es_c"] = nc.dram_tensor("es_c", [C, T], BF16, kind="Internal").ap()
    d["out"] = nc.dram_tensor("out", [C, T], F32, kind="ExternalOutput").ap()

    dbg = {}
    for name, shape, dt_ in [
        ("dbg_xn", [C, T], BF16), ("dbg_h0a", [C, T // 2], BF16),
        ("dbg_g0", [C, T // 2], BF16), ("dbg_l1", [C, T // 4], BF16),
        ("dbg_g1", [C, T // 4], BF16), ("dbg_att", [C, T // 4], BF16),
        ("dbg_low1", [C, T // 2], BF16), ("dbg_s", [C, T], BF16),
        ("dbg_q", [C, T // 4], BF16), ("dbg_es", [C, T], BF16),
        ("dbg_kvz", [128, 16 * 65], BF16), ("dbg_attm", [128, 8 * 1024], BF16),
        ("dbg_dens", [128, 8 * 16], F32), ("dbg_qtm", [128, 8 * 1024], BF16),
        ("dbg_S", [128, 4096], BF16), ("dbg_siluP", [128, 4096], BF16),
        ("dbg_qret", [512, T], BF16), ("dbg_acc", [C, T], BF16),
        ("dbg_pnz", [128, 520], F32), ("dbg_rcp", [128, 128], F32),
    ]:
        if name in debug:
            dbg[name] = nc.dram_tensor(name, shape, dt_,
                                       kind="ExternalOutput").ap()

    p.bar_dummy = p.keep(nc.sbuf_tensor("bar_dummy", [1, 4], F32))
    consts = p.keep(nc.sbuf_tensor("consts", [128, 8], F32))
    ones128 = p.keep(nc.sbuf_tensor("ones128", [128, 128], BF16))
    ident = p.keep(nc.sbuf_tensor("ident", [128, 128], BF16))
    identi = p.keep(nc.sbuf_tensor("identi", [128, 128], I32))
    negbig64 = p.keep(nc.sbuf_tensor("negbig64", [128, 64], F32))
    onecol = p.keep(nc.sbuf_tensor("onecol", [128, 1], BF16))

    g = dict(consts=consts, ones128=ones128, ident=ident,
             negbig64=negbig64, onecol=onecol,
             XOR_EXP=consts.bitcast(I32)[:, 0:1],
             XOR_31=consts.bitcast(I32)[:, 1:2])

    with nc.Block("init") as blk:
        @blk.vector
        def _(ve):
            ve.memset(consts.bitcast(U32)[:, 0:1], 0x7F800000)
            ve.memset(consts.bitcast(U32)[:, 1:2], 0x7FFFFFFF)
            ve.memset(consts[:, 2:3], NEGBIG)
            ve.memset(ones128[:], 1.0)
            ve.memset(negbig64[:], NEGBIG)
            ve.memset(onecol[:], 1.0)
            ve.memset(p.bar_dummy[:], 0.0)
            ve.wait_ge(p.sem["e_gpsimd"], 1)
            ve.tensor_single_scalar(ident[:], identi[:], 0.0, OP.is_equal)

        @blk.gpsimd
        def _(gp):
            gp.iota(identi[:], [[1, 128]], channel_multiplier=-1,
                    allow_small_or_imprecise_dtypes=True
                    ).then_inc(p.sem["e_gpsimd"], 1)
    p.cnt["e_gpsimd"] += 1
    p.bar_ones = ones128
    p.barrier()

    xn_cm = nc.sbuf_tensor("xn_sb", [128, NCH * T], BF16)
    xn_sb = xn_cm.__enter__()
    xn3 = xn_sb[:].rearrange("p (k t) -> p k t", k=NCH)

    if "ln" in phases:
        emit_ln(p, d, g, xn3)
        p.barrier()
        if "dbg_xn" in dbg:
            dump_fm(p, dbg["dbg_xn"], xn3)

    if "attn" in phases:
        emit_attn(p, d, g, xn3, dbg)
        p.barrier()

    xn_cm.__exit__(None, None, None)

    if "f1" in phases:
        emit_f1(p, d, g, dbg)
        p.barrier()

    if "f2" in phases:
        emit_f2(p, d, g, dbg)
        p.barrier()

    p.close()
    return nc


def dump_fm(p, dram, sb3):
    nc = p.nc
    dr = dram.rearrange("(k p) t -> p k t", p=128)
    p.cnt["dtr"] += 16
    tgt = p.cnt["dtr"]
    p.dump_n = getattr(p, "dump_n", 0) + 1
    with nc.Block(f"dump{p.dump_n}") as blk:
        @blk.sync
        def _(sy):
            sy.dma_start(out=dr, in_=sb3).then_inc(p.sem["dtr"], 16)
            sy.wait_ge(p.sem["dtr"], tgt)
    p.barrier()


def dump_flat(p, dram, sb):
    nc = p.nc
    p.cnt["dtr"] += 16
    tgt = p.cnt["dtr"]
    p.dump_n = getattr(p, "dump_n", 0) + 1
    with nc.Block(f"dumpf{p.dump_n}") as blk:
        @blk.sync
        def _(sy):
            sy.dma_start(out=dram, in_=sb).then_inc(p.sem["dtr"], 16)
            sy.wait_ge(p.sem["dtr"], tgt)
    p.barrier()



def emit_ln(p, d, g, xn3):
    nc = p.nc
    x3 = d["x"].rearrange("(k p) t -> p k t", p=128)
    xnc3 = d["xn_c"].rearrange("(k p) t -> p k t", p=128)

    with (
        nc.sbuf_tensor("ln_x", [128, 2, NCH, TG], F32) as x_t,
        nc.sbuf_tensor("ln_xbf", [128, 2, NCH, TG], BF16) as xbf,
        nc.sbuf_tensor("ln_xsq", [128, 2, NCH, TG], BF16) as xsq,
        nc.sbuf_tensor("ln_mu", [128, 2, TG], F32) as mu_t,
        nc.sbuf_tensor("ln_rs", [128, TG], F32) as rs_t,
        nc.sbuf_tensor("ln_v", [128, TG], F32) as v_t,
        nc.sbuf_tensor("ln_s1", [128, TG], F32) as s1_t,
        nc.sbuf_tensor("ln_s2", [128, TG], F32) as s2_t,
        nc.psum_tensor("ln_ps", [128, 2, 2, TG], F32) as ps,
    ):
        s = Sched(p, "ln")
        for gi in range(NTG):
            pa = gi % 2

            def loadx(sy, gi=gi, pa=pa):
                return sy.dma_start(out=x_t[:, pa],
                                    in_=x3[:, :, gi * TG:(gi + 1) * TG])
            w = [f"ap{gi - 2}"] if gi >= 2 else []
            s.add("sync", loadx, waits=w, label=f"lx{gi}", dma="dxi")

            def xbf_f(ve, gi=gi, pa=pa):
                ve.tensor_copy(xbf[:, pa], x_t[:, pa])
            s.add("vector", xbf_f, waits=[f"lx{gi}"], label=f"xb{gi}")

            def xsq_f(se, gi=gi, pa=pa):
                se.activation(out=xsq[:, pa], in_=x_t[:, pa], func=AF.Square,
                              bias=0.0, scale=1.0)
            wq_ = [f"lx{gi}"] + ([f"st{gi - 2}"] if gi >= 2 else [])
            s.add("scalar", xsq_f, waits=wq_, label=f"xq{gi}")

            def stats(te, gi=gi, pa=pa):
                for k in range(NCH):
                    te.matmul(ps[:, pa, 0], g["ones128"][:], xbf[:, pa, k],
                              start=(k == 0), stop=(k == NCH - 1))
                for k in range(NCH):
                    te.matmul(ps[:, pa, 1], g["ones128"][:], xsq[:, pa, k],
                              start=(k == 0), stop=(k == NCH - 1))
            w = [f"xb{gi}", f"xq{gi}"] + ([f"mo{gi - 2}"] if gi >= 2 else [])
            s.add("tensor", stats, waits=w, label=f"st{gi}")

            def moments(ve, gi=gi, pa=pa):
                ve.tensor_scalar_mul(mu_t[:, pa], ps[:, pa, 0], 1.0 / C)
                ve.tensor_mul(v_t[:], mu_t[:, pa], mu_t[:, pa])
                ve.scalar_tensor_tensor(v_t[:], ps[:, pa, 1], 1.0 / C,
                                        v_t[:], OP.mult, OP.subtract)
                ve.tensor_scalar_add(v_t[:], v_t[:], LN_EPS)
            s.add("vector", moments, waits=[f"st{gi}"], label=f"mo{gi}")

            def apply(ve, gi=gi, pa=pa):
                emit_rsqrt(ve, rs_t[:], v_t[:], s1_t[:], s2_t[:],
                           g["XOR_31"])
                for k in range(NCH):
                    ve.tensor_sub(s1_t[:], x_t[:, pa, k], mu_t[:, pa])
                    ve.tensor_mul(xn3[:, k, gi * TG:(gi + 1) * TG],
                                  s1_t[:], rs_t[:])
            s.add("vector", apply, label=f"ap{gi}")

            def store(sy, gi=gi):
                return sy.dma_start(out=xnc3[:, :, gi * TG:(gi + 1) * TG],
                                    in_=xn3[:, :, gi * TG:(gi + 1) * TG])
            s.add("sync", store, waits=[f"ap{gi}"], label=f"sx{gi}",
                  dma="dxn")
        def lnfin(sy):
            sy.wait_ge(p.sem["dxn"], p.cnt["dxn"])
        s.add("sync", lnfin, label="lnfin")
        s.emit()



def emit_attn(p, d, g, xn3, dbg):
    nc = p.nc
    NT2, NT4 = T // 2, T // 4

    with (
        nc.sbuf_tensor("at_filt", [128, NCH, 16], F32) as filt,
        nc.sbuf_tensor("at_gb", [128, NCH, 2], F32) as gb,
        nc.sbuf_tensor("at_g0", [128, NCH, NT2], BF16) as g0,
        nc.sbuf_tensor("at_g1", [128, NCH, NT4], BF16) as g1,
        nc.sbuf_tensor("at_att", [128, NCH, NT4], BF16) as att,
    ):
        with nc.sbuf_tensor("at_l1", [128, NCH, NT4], BF16) as l1:
            with (
                nc.sbuf_tensor("at_h0a", [128, NCH, NT2], BF16) as h0a,
                nc.sbuf_tensor("at_scr", [128, NT2], F32) as scr,
            ):
                s = Sched(p, "attnA")

                def loadf(sy):
                    return [sy.dma_start(
                        out=filt[:], in_=d["filt"].rearrange(
                            "(k p) i -> p k i", p=128)),
                        sy.dma_start(out=gb[:], in_=d["gb"].rearrange(
                            "(k p) i -> p k i", p=128))]
                s.add("sync", loadf, label="lf", dma="dxi", n_dma=2)

                def conv(ve, out3, in3, fcol, Lin, k):
                    Lo = Lin // 2
                    zi = in3[:, k]
                    f = lambda i: filt[:, k, fcol + i:fcol + i + 1]
                    ve.tensor_scalar(scr[:, 0:Lo], zi[:, 0:Lin:2], f(1),
                                     None, OP.mult)
                    ve.scalar_tensor_tensor(scr[:, 0:Lo], zi[:, 1:Lin:2],
                                            f(2), scr[:, 0:Lo], OP.mult,
                                            OP.add)
                    ve.scalar_tensor_tensor(scr[:, 1:Lo], zi[:, 1:Lin - 2:2],
                                            f(0), scr[:, 1:Lo], OP.mult,
                                            OP.add)
                    ve.scalar_tensor_tensor(scr[:, 0:Lo - 1], zi[:, 2:Lin:2],
                                            f(3), scr[:, 0:Lo - 1], OP.mult,
                                            OP.add)
                    ve.tensor_copy(out3[:, k], scr[:, 0:Lo])

                for k in range(NCH):
                    s.add("vector",
                          lambda ve, k=k: conv(ve, h0a, xn3, 0, T, k),
                          waits=["lf"], label=f"h0a{k}")
                    s.add("vector",
                          lambda ve, k=k: conv(ve, g0, xn3, 4, T, k),
                          label=f"g0{k}")
                for k in range(NCH):
                    s.add("vector",
                          lambda ve, k=k: conv(ve, l1, h0a, 0, NT2, k),
                          label=f"l1{k}")
                    s.add("vector",
                          lambda ve, k=k: conv(ve, g1, h0a, 4, NT2, k),
                          label=f"g1{k}")
                s.emit()
                p.barrier()

                if "dbg_h0a" in dbg:
                    dump_fm(p, dbg["dbg_h0a"], h0a[:])
                if "dbg_g0" in dbg:
                    dump_fm(p, dbg["dbg_g0"], g0[:])
                if "dbg_l1" in dbg:
                    dump_fm(p, dbg["dbg_l1"], l1[:])
                if "dbg_g1" in dbg:
                    dump_fm(p, dbg["dbg_g1"], g1[:])

            emit_linattn(p, d, g, l1, att, dbg)
            p.barrier()
            if "dbg_att" in dbg:
                dump_fm(p, dbg["dbg_att"], att[:])

        emit_up_sln(p, d, g, xn3, att, g0, g1, gb, filt, dbg)
        p.barrier()
        if "dbg_s" in dbg:
            dump_fm(p, dbg["dbg_s"], xn3)


def emit_linattn(p, d, g, l1, att, dbg):
    nc = p.nc
    N = T // 4
    NTC = N // 128

    with (
        nc.sbuf_tensor("la_q", [128, NCH, N], BF16) as q,
        nc.sbuf_tensor("la_qtm", [128, NTC, C], BF16) as q_tm,
        nc.sbuf_tensor("la_vtm", [128, NTC, C], BF16) as v_tm,
        nc.sbuf_tensor("la_kvz", [128, 8, 130], BF16) as kvz,
        nc.sbuf_tensor("la_dens", [128, 8, 16], F32) as dens,
        nc.sbuf_tensor("la_rcp", [128, 8, 16], F32) as rcp,
        nc.sbuf_tensor("la_r1", [128, 8, 16], F32) as r1,
        nc.sbuf_tensor("la_r2", [128, 8, 16], F32) as r2,
        nc.sbuf_tensor("la_scA", [128, N], F32) as scA,
        nc.sbuf_tensor("la_scB", [128, N], F32) as scB,
        nc.psum_tensor("la_pst", [128, 2, 1024], BF16) as pst,
        nc.psum_tensor("la_kv", [64, 2, 512], F32) as pkv,
        nc.psum_tensor("la_nz", [128, 4, 512], F32) as pnz,
    ):
        att_tm = v_tm
        s = Sched(p, "linattn")

        def zkvz(ve):
            ve.memset(kvz[:], 0.0)
        s.add("vector", zkvz, label="zkvz")

        for k in range(NCH):
            s.add("vector",
                  lambda ve, k=k: ve.tensor_scalar_min(scA[:], l1[:, k], 0.0),
                  label=f"qm{k}", waits=[f"qq{k - 1}"] if k else [])
            s.add("scalar",
                  lambda se, k=k: se.activation(out=scB[:], in_=scA[:],
                                                func=AF.Exp, bias=0.0,
                                                scale=1.0),
                  waits=[f"qm{k}"], label=f"qe{k}")
            s.add("vector",
                  lambda ve, k=k: ve.scalar_tensor_tensor(
                      q[:, k], l1[:, k], 0.0, scB[:], OP.max, OP.add),
                  waits=[f"qe{k}"], label=f"qq{k}")

        w = 0
        for src, dst, nm in ((q, q_tm, "q"), (l1, v_tm, "v")):
            for tc in range(NTC):
                for k in range(NCH):
                    w += 1
                    waits = []
                    if nm == "q":
                        waits.append(f"qq{k}")
                    if w > 2:
                        waits.append(f"tev{w - 2}")
                    s.add("tensor",
                          lambda te, src=src, tc=tc, k=k, w=w: te.transpose(
                              pst[:, w % 2, 0:128],
                              src[:, k, tc * 128:(tc + 1) * 128],
                              g["ident"][:]),
                          waits=waits, label=f"tt{w}")
                    s.add("vector",
                          lambda ve, dst=dst, tc=tc, k=k, w=w: ve.tensor_copy(
                              dst[:, tc, k * 128:(k + 1) * 128],
                              pst[:, w % 2, 0:128]),
                          waits=[f"tt{w}"], label=f"tev{w}")
        ntr = w

        for h in range(16):
            pa = h % 2
            waits = [f"tev{ntr}"] if h == 0 else []
            if h >= 2:
                waits.append(f"kve{h - 2}")
            def kv_f(te, h=h, pa=pa):
                for tc in range(NTC):
                    te.matmul(pkv[:, pa, 0:64], q_tm[:, tc, h * 64:(h + 1) * 64],
                              v_tm[:, tc, h * 64:(h + 1) * 64],
                              start=(tc == 0), stop=(tc == NTC - 1))
                for tc in range(NTC):
                    te.matmul(pkv[:, pa, 64:65], q_tm[:, tc, h * 64:(h + 1) * 64],
                              g["onecol"][:], start=(tc == 0),
                              stop=(tc == NTC - 1))
            s.add("tensor", kv_f, waits=waits, label=f"kv{h}")

            def kve_f(ve, h=h, pa=pa):
                j, half = h // 2, h % 2
                ve.tensor_copy(kvz[half * 64:(half + 1) * 64, j,
                                   half * 65:half * 65 + 64],
                               pkv[:, pa, 0:64])
                ve.tensor_copy(kvz[half * 64:(half + 1) * 64, j,
                                   half * 65 + 64:half * 65 + 65],
                               pkv[:, pa, 64:65])
            s.add("vector", kve_f, waits=[f"kv{h}"], label=f"kve{h}")

        nn = 0
        for tc in range(NTC):
            for j in range(8):
                nn += 1
                waits = [f"kve{2 * j + 1}"]
                if nn > 4:
                    waits.append(f"nze{nn - 4}")

                def nz_f(te, tc=tc, j=j, nn=nn):
                    te.matmul(pnz[:, nn % 4, 0:130],
                              q[:, j, tc * 128:(tc + 1) * 128],
                              kvz[:, j], start=True, stop=True)
                s.add("tensor", nz_f, waits=waits, label=f"nz{nn}")

                def nze_f(ve, tc=tc, j=j, nn=nn):
                    ve.tensor_scalar_add(dens[:, tc, 2 * j:2 * j + 1],
                                         pnz[:, nn % 4, 64:65], ATTN_EPS)
                    ve.tensor_scalar_add(dens[:, tc, 2 * j + 1:2 * j + 2],
                                         pnz[:, nn % 4, 129:130], ATTN_EPS)
                    ve.tensor_scalar_add(att_tm[:, tc, 2 * j * 64:
                                                 (2 * j + 1) * 64],
                                         pnz[:, nn % 4, 0:64], 0.0)
                    ve.tensor_scalar_add(att_tm[:, tc, (2 * j + 1) * 64:
                                                 (2 * j + 2) * 64],
                                         pnz[:, nn % 4, 65:129], 0.0)
                    if "dbg_pnz" in dbg and nn in (1, 5):
                        ix = 0 if nn == 1 else 1
                        ve.tensor_copy(scA[:, ix * 260:ix * 260 + 130],
                                       pnz[:, nn % 4, 0:130])
                        ve.tensor_copy(
                            scA[:, ix * 260 + 130:ix * 260 + 258],
                            att_tm[:, tc, 2 * j * 64:(2 * j + 2) * 64])
                s.add("vector", nze_f, waits=[f"nz{nn}"],
                      label=f"nze{nn}")

        s.add("vector",
              lambda ve: emit_recip(ve, rcp[:], dens[:], r1[:], r2[:],
                                    g["XOR_EXP"]),
              label="rcpa")
        for tc in range(NTC):
            def norm_f(ve, tc=tc):
                for h in range(16):
                    ve.tensor_scalar(att_tm[:, tc, h * 64:(h + 1) * 64],
                                     att_tm[:, tc, h * 64:(h + 1) * 64],
                                     rcp[:, tc, h:h + 1], None, OP.mult)
            s.add("vector", norm_f, waits=["rcpa"], label=f"nrm{tc}")

        w2 = 0
        for tc in range(NTC):
            for k in range(NCH):
                w2 += 1
                waits = [f"nrm{tc}"]
                if w2 > 2:
                    waits.append(f"tae{w2 - 2}")
                s.add("tensor",
                      lambda te, tc=tc, k=k, w2=w2: te.transpose(
                          pst[:, w2 % 2, 0:128],
                          att_tm[:, tc, k * 128:(k + 1) * 128],
                          g["ident"][:]),
                      waits=waits, label=f"ta{w2}")
                s.add("vector",
                      lambda ve, tc=tc, k=k, w2=w2: ve.tensor_copy(
                          att[:, k, tc * 128:(tc + 1) * 128],
                          pst[:, w2 % 2, 0:128]),
                      waits=[f"ta{w2}"], label=f"tae{w2}")
        if "dbg_kvz" in dbg:
            s.add("sync", lambda sy: sy.dma_start(
                out=dbg["dbg_kvz"].rearrange("p (h e) -> p h e", h=8),
                in_=kvz[:]), waits=["kve15"], label="dkvz", dma="dtr")
        if "dbg_q" in dbg:
            s.add("sync", lambda sy: sy.dma_start(
                out=dbg["dbg_q"].rearrange("(k p) t -> p k t", p=128),
                in_=q[:]), waits=["qq7"], label="dq", dma="dtr")
        if "dbg_attm" in dbg:
            s.add("sync", lambda sy: sy.dma_start(
                out=dbg["dbg_attm"].rearrange("p (a b) -> p a b", a=8),
                in_=att_tm[:]), waits=[f"nrm{NTC - 1}"], label="dattm",
                dma="dtr")
        if "dbg_rcp" in dbg:
            s.add("sync", lambda sy: sy.dma_start(
                out=dbg["dbg_rcp"].rearrange("p (a b) -> p a b", a=8),
                in_=rcp[:]),
                waits=[f"nrm{NTC - 1}"], label="drcp", dma="dtr")
        if "dbg_pnz" in dbg:
            s.add("sync", lambda sy: sy.dma_start(
                out=dbg["dbg_pnz"], in_=scA[:, 0:520]),
                waits=["nze5"], label="dpnz", dma="dtr")
        if "dbg_dens" in dbg:
            s.add("sync", lambda sy: sy.dma_start(
                out=dbg["dbg_dens"].rearrange("p (a b) -> p a b", a=8),
                in_=dens[:]),
                waits=[f"nrm{NTC - 1}"], label="ddens", dma="dtr")
        if "dbg_qtm" in dbg:
            s.add("sync", lambda sy: sy.dma_start(
                out=dbg["dbg_qtm"].rearrange("p (a b) -> p a b", a=8),
                in_=q_tm[:]), waits=["tev128"], label="dqtm", dma="dtr")
        s.emit()


def emit_up_sln(p, d, g, xn3, att, g0, g1, gb, filt, dbg):
    nc = p.nc
    NT2, NT4 = T // 2, T // 4
    sc3 = d["s_c"].rearrange("(k p) t -> p k t", p=128)

    with (
        nc.sbuf_tensor("up_comb", [128, NCH, 10], F32) as comb,
        nc.sbuf_tensor("up_diag", [128, 2, 10, 128], BF16) as diag,
        nc.sbuf_tensor("up_low1", [128, NCH, NT2], BF16) as low1,
        nc.sbuf_tensor("up_rsq", [128, 2, NCH, TG], BF16) as rsq,
        nc.sbuf_tensor("up_mu", [128, 2, TG], F32) as mu_t,
        nc.sbuf_tensor("up_rs", [128, TG], F32) as rs_t,
        nc.sbuf_tensor("up_s1", [128, TG], F32) as s1_t,
        nc.sbuf_tensor("up_s2", [128, TG], F32) as s2_t,
        nc.sbuf_tensor("up_v", [128, TG], F32) as v_t,
        nc.psum_tensor("up_ps", [128, 4, 512], F32) as ps,
        nc.psum_tensor("up_st", [128, 2, 2, TG], F32) as pstat,
    ):
        s = Sched(p, "upconv")

        def combs(ve):
            for k in range(NCH):
                for base, col in ((8, 0), (12, 5)):
                    f = lambda i: filt[:, k, base + i:base + i + 1]
                    cs = lambda j: comb[:, k, col + j:col + j + 1]
                    ve.tensor_copy(cs(0), f(0))
                    ve.tensor_add(cs(1), f(1), f(2))
                    ve.tensor_copy(cs(2), f(3))
                    ve.tensor_add(cs(3), f(0), f(1))
                    ve.tensor_add(cs(4), f(2), f(3))
            ve.drain()
        s.add("vector", combs, label="cmb")

        def add_diag(tag, k, waits):
            def f(ve, k=k):
                for j in range(10):
                    ve.tensor_scalar(diag[:, k % 2, j], g["ident"][:],
                                     comb[:, k, j:j + 1], None, OP.mult)
            s.add("vector", f, waits=waits, label=tag)

        def up_pass(passname, vin1, vin2, Lh, nslots_used, evac_eng,
                    evac_fn, prev_diag_user):
            ngr = Lh // 512
            wv = 0
            for k in range(NCH):
                dwaits = ["cmb"]
                if k >= 2:
                    dwaits.append(f"{passname}mm{k - 2}_last")
                add_diag(f"{passname}dg{k}", k, dwaits)
                for ph in range(2):
                    for tg2 in range(ngr):
                        wv += 1
                        lo = tg2 * 512
                        slot = (wv - 1) % 4
                        waits = [f"{passname}dg{k}"]
                        if wv > 4:
                            waits.append(f"{passname}ev{wv - 4}")

                        def mm_f(te, k=k, ph=ph, lo=lo, slot=slot,
                                 vin1=vin1, vin2=vin2, Lh=Lh):
                            seq = []
                            for (src, base) in ((vin1, 0), (vin2, 5)):
                                if ph == 0:
                                    seq += [(src, base + 1, 0),
                                            (src, base + 0, -1),
                                            (src, base + 2, +1)]
                                else:
                                    seq += [(src, base + 3, 0),
                                            (src, base + 4, +1)]
                            for idx, (src, tap, off) in enumerate(seq):
                                a, b = lo + off, lo + off + 512
                                pslice = ps[:, slot, 0:512]
                                if a < 0:
                                    pslice = ps[:, slot, 1:512]
                                    a, b = 0, 511
                                elif b > Lh:
                                    pslice = ps[:, slot, 0:512 - (b - Lh)]
                                    b = Lh
                                te.matmul(pslice, diag[:, k % 2, tap],
                                          src[:, k, a:b],
                                          start=(idx == 0),
                                          stop=(idx == len(seq) - 1))
                        lbl = f"{passname}mm{wv}"
                        s.add("tensor", mm_f, waits=waits, label=lbl)
                        if ph == 1 and tg2 == ngr - 1:
                            s.labels[f"{passname}mm{k}_last"] = \
                                s.labels[lbl]
                        s.add(evac_eng,
                              lambda e, k=k, ph=ph, tg2=tg2, slot=slot:
                              evac_fn(e, k, ph, tg2, slot),
                              waits=[lbl], label=f"{passname}ev{wv}")
            return wv

        def ev1(se, k, ph, tg2, slot):
            se.activation(out=low1[:, k, tg2 * 1024 + ph:
                                   (tg2 + 1) * 1024:2],
                          in_=ps[:, slot, 0:512], func=AF.Copy, bias=0.0,
                          scale=1.0)
        n1 = up_pass("u1", att, g1, NT4, 4, "scalar", ev1, None)

        def ev0(ve, k, ph, tg2, slot):
            ve.tensor_add(xn3[:, k, tg2 * 1024 + ph:(tg2 + 1) * 1024:2],
                          ps[:, slot, 0:512],
                          xn3[:, k, tg2 * 1024 + ph:(tg2 + 1) * 1024:2])
        def up0():
            ngr = NT2 // 512
            wv = 0
            for k in range(NCH):
                dwaits = [f"u1ev{n1}"] if k == 0 else []
                dwaits.append("cmb")
                if k >= 2:
                    dwaits.append(f"u0mm{k - 2}_last")
                add_diag(f"u0dg{k}", k, dwaits)
                for ph in range(2):
                    for tg2 in range(ngr):
                        wv += 1
                        lo = tg2 * 512
                        slot = (wv - 1) % 4
                        waits = [f"u0dg{k}"]
                        if wv > 4:
                            waits.append(f"u0ev{wv - 4}")

                        def mm_f(te, k=k, ph=ph, lo=lo, slot=slot):
                            seq = []
                            for (src, base) in ((low1, 0), (g0, 5)):
                                if ph == 0:
                                    seq += [(src, base + 1, 0),
                                            (src, base + 0, -1),
                                            (src, base + 2, +1)]
                                else:
                                    seq += [(src, base + 3, 0),
                                            (src, base + 4, +1)]
                            for idx, (src, tap, off) in enumerate(seq):
                                a, b = lo + off, lo + off + 512
                                pslice = ps[:, slot, 0:512]
                                if a < 0:
                                    pslice = ps[:, slot, 1:512]
                                    a, b = 0, 511
                                elif b > NT2:
                                    pslice = ps[:, slot, 0:512 - (b - NT2)]
                                    b = NT2
                                te.matmul(pslice, diag[:, k % 2, tap],
                                          src[:, k, a:b],
                                          start=(idx == 0),
                                          stop=(idx == len(seq) - 1))
                        lbl = f"u0mm{wv}"
                        s.add("tensor", mm_f, waits=waits, label=lbl)
                        if ph == 1 and tg2 == ngr - 1:
                            s.labels[f"u0mm{k}_last"] = s.labels[lbl]
                        s.add("vector",
                              lambda ve, k=k, ph=ph, tg2=tg2, slot=slot:
                              ev0(ve, k, ph, tg2, slot),
                              waits=[lbl], label=f"u0ev{wv}")
            return wv
        n0 = up0()

        for gi in range(NTG):
            pa = gi % 2

            def sq_f(se, gi=gi, pa=pa):
                for k in range(NCH):
                    se.activation(out=rsq[:, pa, k],
                                  in_=xn3[:, k, gi * TG:(gi + 1) * TG],
                                  func=AF.Square, bias=0.0, scale=1.0)
            waits = [f"u0ev{n0}"] if gi == 0 else []
            if gi >= 2:
                waits.append(f"sap{gi - 2}")
            s.add("scalar", sq_f, waits=waits, label=f"ssq{gi}")

            def sstat(te, gi=gi, pa=pa):
                for k in range(NCH):
                    te.matmul(pstat[:, pa, 0], g["ones128"][:],
                              xn3[:, k, gi * TG:(gi + 1) * TG],
                              start=(k == 0), stop=(k == NCH - 1))
                for k in range(NCH):
                    te.matmul(pstat[:, pa, 1], g["ones128"][:],
                              rsq[:, pa, k], start=(k == 0),
                              stop=(k == NCH - 1))
            w = [f"ssq{gi}"] + ([f"smo{gi - 2}"] if gi >= 2 else [])
            s.add("tensor", sstat, waits=w, label=f"sst{gi}")

            def smom(ve, gi=gi, pa=pa):
                ve.tensor_scalar_mul(mu_t[:, pa], pstat[:, pa, 0], 1.0 / C)
                ve.tensor_mul(v_t[:], mu_t[:, pa], mu_t[:, pa])
                ve.scalar_tensor_tensor(v_t[:], pstat[:, pa, 1], 1.0 / C,
                                        v_t[:], OP.mult, OP.subtract)
                ve.tensor_scalar_add(v_t[:], v_t[:], LN_EPS)
            s.add("vector", smom, waits=[f"sst{gi}"], label=f"smo{gi}")

            def sap(ve, gi=gi, pa=pa):
                emit_rsqrt(ve, rs_t[:], v_t[:], s1_t[:], s2_t[:],
                           g["XOR_31"])
                for k in range(NCH):
                    ve.tensor_sub(s1_t[:], xn3[:, k, gi * TG:(gi + 1) * TG],
                                  mu_t[:, pa])
                    ve.tensor_mul(s1_t[:], s1_t[:], rs_t[:])
                    ve.tensor_scalar(xn3[:, k, gi * TG:(gi + 1) * TG],
                                     s1_t[:], gb[:, k, 0:1], gb[:, k, 1:2],
                                     OP.mult, OP.add)
            s.add("vector", sap, label=f"sap{gi}")

            s.add("sync",
                  lambda sy, gi=gi: sy.dma_start(
                      out=sc3[:, :, gi * TG:(gi + 1) * TG],
                      in_=xn3[:, :, gi * TG:(gi + 1) * TG]),
                  waits=[f"sap{gi}"], label=f"ssx{gi}", dma="dxn")
        s.add("sync", lambda sy: sy.wait_ge(p.sem["dxn"], p.cnt["dxn"]),
              label="upfin")
        if "dbg_low1" in dbg:
            s.add("sync", lambda sy: sy.dma_start(
                out=dbg["dbg_low1"].rearrange("(k p) t -> p k t", p=128),
                in_=low1[:]), waits=[f"u1ev{n1}"], label="dlow1", dma="dtr")
        s.emit()



def emit_f1(p, d, g, dbg):
    nc = p.nc
    xnc3 = d["xn_c"].rearrange("(k p) t -> p k t", p=128)
    esc3 = d["es_c"].rearrange("(k p) t -> p k t", p=128)
    ue3 = d["ue"].rearrange("(ec p) dd -> p ec dd", p=128)

    es_ = ExitStack()
    def sb(name, shape, dt_):
        return es_.enter_context(nc.sbuf_tensor(name, shape, dt_))
    def psb(name, shape, dt_):
        return es_.enter_context(nc.psum_tensor(name, shape, dt_))
    dn = sb("f1_dn", [128, NCH, 4096], BF16)
    keys = sb("f1_keys", [128, 4, 128], BF16)
    wq = sb("f1_wq", [128, NCH, 512], BF16)
    xn_g = sb("f1_xn", [128, 2, NCH, TG], BF16)
    qret = sb("f1_qret", [128, 4, TG], BF16)
    Bm = sb("f1_Bm", [128, 4, 8, 64], F32)
    mx = sb("f1_mx", [128, 4, 8, 8], F32)
    tnew = sb("f1_tnew", [128, 8, 64], F32)
    msk64 = sb("f1_msk", [128, 8, 64], mybir.dt.uint8)
    grid_a = sb("f1_grid", [128, 16, 64], F32)
    g8a = sb("f1_g8", [128, 16, 8], F32)
    negm = sb("f1_nm", [128, 16], F32)
    theta = sb("f1_th", [128, 16], F32)
    e8a = sb("f1_e8", [128, 16, 8], F32)
    Zb = sb("f1_Z", [128, 64], F32)
    rZ = sb("f1_rZ", [128, 64], F32)
    r1 = sb("f1_r1", [128, 64], F32)
    r2 = sb("f1_r2", [128, 64], F32)
    D = sb("f1_D", [128, 4096], F32)
    W = sb("f1_W", [128, 4096], BF16)
    M = sb("f1_M", [128, 4096], BF16)
    S = sb("f1_S", [128, 4096], BF16)
    sP = sb("f1_sP", [128, 4096], BF16)
    Sfm = sb("f1_Sfm", [128, 32, TG], BF16)
    ue_t = sb("f1_ue", [128, 2, 512], BF16)
    es_t = sb("f1_es", [128, NCH, TG], BF16)
    psim = psb("f1_psim", [128, 2, 512], F32)
    ppd = psb("f1_pd", [128, 2, 512], F32)
    pes = psb("f1_pes", [128, 4, 512], F32)
    if True:
        D3 = D[:].rearrange("p (i j) -> p i j", i=64)
        s = Sched(p, "f1")

        kr = d["keysT"].rearrange("q h n k -> n (q h) k")

        def zkeys(ve):
            ve.memset(keys[:], 0.0)
            ve.memset(Zb[:], 1.0)
        s.add("vector", zkeys, label="zk")

        def loadw(gp):
            out = [
                gp.dma_start(out=dn[:], in_=d["dnT"].rearrange(
                    "(k p) e -> p k e", p=128)),
                gp.dma_start(out=wq[:], in_=d["wq"].rearrange(
                    "(k p) r -> p k r", p=128)),
            ]
            for rc in range(4):
                out.append(gp.dma_start(out=keys[0:64, rc, 0:64],
                                        in_=kr[:, 2 * rc]))
                out.append(gp.dma_start(out=keys[64:128, rc, 64:128],
                                        in_=kr[:, 2 * rc + 1]))
            return out
        s.add("gpsimd", loadw, waits=["zk"], label="lw", dma="dwt",
              n_dma=10)

        for gi in range(NTG):
            pa = gi % 2

            waits = ["lw"] if gi == 0 else [f"pdn{gi - 1}_3_7"]
            s.add("sync",
                  lambda sy, gi=gi, pa=pa: sy.dma_start(
                      out=xn_g[:, pa], in_=xnc3[:, :, gi * TG:(gi + 1) * TG]),
                  waits=waits, label=f"lxn{gi}", dma="dxi")

            for rc in range(4):
                waits = [f"lxn{gi}"]
                if gi > 0:
                    waits.append(f"slu{gi - 1}_3_{6 + rc % 2}")
                if rc >= 2:
                    waits.append(f"qe{gi}_{rc - 2}")

                def qr_f(te, gi=gi, pa=pa, rc=rc):
                    for k in range(NCH):
                        te.matmul(ppd[:, rc % 2],
                                  wq[:, k, rc * 128:(rc + 1) * 128],
                                  xn_g[:, pa, k], start=(k == 0),
                                  stop=(k == NCH - 1))
                s.add("tensor", qr_f, waits=waits, label=f"qr{gi}_{rc}")
                s.add("vector",
                      lambda ve, rc=rc: ve.tensor_copy(qret[:, rc],
                                                       ppd[:, rc % 2]),
                      waits=[f"qr{gi}_{rc}"], label=f"qe{gi}_{rc}")

            for tc in range(4):
                waits = [f"qe{gi}_3"]
                if gi > 0 or tc >= 2:
                    waits.append(f"st1{gi}_{tc - 2}" if tc >= 2 else
                                 f"st1{gi - 1}_{2 + tc}")

                def sim_f(te, gi=gi, tc=tc):
                    for rc in range(4):
                        te.matmul(psim[:, tc % 2, rc * 128:(rc + 1) * 128],
                                  qret[:, rc, tc * 128:(tc + 1) * 128],
                                  keys[:, rc], start=True, stop=True)
                s.add("tensor", sim_f, waits=waits, label=f"sim{gi}_{tc}")

                def st1_f(ve, gi=gi, tc=tc):
                    def sl(ph):
                        rc, off = ph // 2, (ph % 2) * 64
                        return psim[:, tc % 2, rc * 128 + off:
                                    rc * 128 + off + 64]
                    for ph in range(8):
                        ve.max(out=mx[:, tc, ph], in_=sl(ph))
                    for ph in range(8):
                        ve.match_replace(out=tnew[:, ph],
                                         in_to_replace=mx[:, tc, ph],
                                         in_values=sl(ph),
                                         imm_value=NEGBIG)
                    for ph in range(8):
                        ve.tensor_scalar(msk64[:, ph], tnew[:, ph], NEGBIG,
                                         None, OP.is_equal)
                    for ph in range(8):
                        ve.select(Bm[:, tc, ph], msk64[:, ph], sl(ph),
                                  g["negbig64"][:])
                s.add("vector", st1_f, waits=[f"sim{gi}_{tc}"],
                      label=f"st1{gi}_{tc}")

            def zg_f(ve, gi=gi):
                for i in range(16):
                    tc, h = i // 4, i % 4
                    ve.tensor_add(
                        grid_a[:, i].rearrange("p (a b) -> p a b", a=8),
                        mx[:, tc, h, :].unsqueeze(2).broadcast_to(
                            [128, 8, 8]),
                        mx[:, tc, 4 + h, :].unsqueeze(1).broadcast_to(
                            [128, 8, 8]))
                for i in range(16):
                    ve.max(out=g8a[:, i], in_=grid_a[:, i])
                for i in range(16):
                    ve.tensor_scalar(negm[:, i:i + 1], g8a[:, i, 0:1],
                                     -1.0, None, OP.mult)
                for i in range(16):
                    ve.tensor_copy(theta[:, i:i + 1], g8a[:, i, 7:8])
                for i in range(16):
                    ve.tensor_scalar(e8a[:, i], g8a[:, i],
                                     negm[:, i:i + 1], None, OP.add)
            s.add("vector", zg_f, waits=[f"st1{gi}_3"], label=f"zg{gi}")

            s.add("scalar",
                  lambda se: se.activation(
                      out=e8a[:].rearrange("p a b -> p (a b)"),
                      in_=e8a[:].rearrange("p a b -> p (a b)"),
                      func=AF.Exp, bias=0.0, scale=1.0),
                  waits=[f"zg{gi}"], label=f"ze{gi}")

            def zs_f(ve, gi=gi):
                for i in range(16):
                    ve.reduce_sum(Zb[:, i:i + 1], e8a[:, i],
                                  axis=mybir.AxisListType.X)
                ve.drain()
                emit_recip(ve, rZ[:], Zb[:], r1[:], r2[:], g["XOR_EXP"])
            s.add("vector", zs_f, waits=[f"ze{gi}"], label=f"zs{gi}")

            for tc in range(4):
                for h in range(4):
                    waits = [f"st1{gi}_{tc}"]
                    if h > 0:
                        waits.append(f"dm{gi}_{tc}_{h - 1}")
                    elif not (gi == 0 and tc == 0):
                        waits.append(f"dm{gi}_{tc - 1}_3" if tc > 0 else
                                     f"dm{gi - 1}_3_3")
                        waits.append(f"tr{gi}_{tc - 1}" if tc > 0 else
                                     f"tr{gi - 1}_3")

                    def dD_f(ve, gi=gi, tc=tc, h=h):
                        ve.tensor_add(
                            D3,
                            Bm[:, tc, h, :].unsqueeze(2).broadcast_to(
                                [128, 64, 64]),
                            Bm[:, tc, 4 + h, :].unsqueeze(1).broadcast_to(
                                [128, 64, 64]))
                    s.add("vector", dD_f, waits=waits,
                          label=f"dD{gi}_{tc}_{h}")

                    def sW_f(se, tc=tc, h=h):
                        se.activation(out=W[:], in_=D[:], func=AF.Exp,
                                      bias=negm[:, tc * 4 + h:
                                                tc * 4 + h + 1], scale=1.0)
                    s.add("scalar", sW_f, waits=[f"dD{gi}_{tc}_{h}"],
                          label=f"sW{gi}_{tc}_{h}")

                    def dM_f(ve, gi=gi, tc=tc, h=h):
                        i = tc * 4 + h
                        ve.tensor_scalar(M[:], D[:], theta[:, i:i + 1],
                                         None, OP.is_ge)
                        if h == 0:
                            ve.scalar_tensor_tensor(
                                S[:], W[:], rZ[:, i:i + 1], M[:],
                                OP.mult, OP.mult)
                        else:
                            ve.scalar_tensor_tensor(
                                M[:], W[:], rZ[:, i:i + 1], M[:],
                                OP.mult, OP.mult)
                            ve.tensor_add(S[:], S[:], M[:])
                    s.add("vector", dM_f, waits=[f"sW{gi}_{tc}_{h}"],
                          label=f"dm{gi}_{tc}_{h}")

                for eg in range(8):
                    waits = []
                    if eg >= 2:
                        waits.append(f"slu{gi}_{tc}_{eg - 2}")
                    elif not (gi == 0 and tc == 0):
                        waits.append(f"slu{gi}_{tc - 1}_{6 + eg}" if tc > 0
                                     else f"qe{gi}_{2 + eg}")
                    if eg < 2 and not (gi == 0 and tc == 0) and tc == 0:
                        pass
                    if eg == 0:
                        waits.append(f"lxn{gi}")

                    def pd_f(te, gi=gi, pa=pa, tc=tc, eg=eg):
                        for k in range(NCH):
                            te.matmul(ppd[:, eg % 2],
                                      xn_g[:, pa, k, tc * 128:(tc + 1) * 128],
                                      dn[:, k, eg * 512:(eg + 1) * 512],
                                      start=(k == 0), stop=(k == NCH - 1))
                    s.add("tensor", pd_f, waits=waits,
                          label=f"pdn{gi}_{tc}_{eg}")

                    wsl = [f"pdn{gi}_{tc}_{eg}"]
                    if eg == 0 and not (gi == 0 and tc == 0):
                        wsl.append(f"smul{gi}_{tc - 1}" if tc > 0 else
                                   f"smul{gi - 1}_3")
                    s.add("scalar",
                          lambda se, eg=eg: se.activation(
                              out=sP[:, eg * 512:(eg + 1) * 512],
                              in_=ppd[:, eg % 2], func=AF.Silu, bias=0.0,
                              scale=1.0),
                          waits=wsl, label=f"slu{gi}_{tc}_{eg}")

                s.add("vector",
                      lambda ve: ve.tensor_mul(S[:], S[:], sP[:]),
                      waits=[f"dm{gi}_{tc}_3", f"slu{gi}_{tc}_7"],
                      label=f"smul{gi}_{tc}")

                if "dbg_S" in dbg and gi == 0 and tc == 0:
                    s.add("sync", lambda sy: sy.dma_start(
                        out=dbg["dbg_S"], in_=S[:]),
                        waits=[f"smul{gi}_{tc}"], label="dS", dma="dtr")
                if "dbg_siluP" in dbg and gi == 0 and tc == 0:
                    s.add("sync", lambda sy: sy.dma_start(
                        out=dbg["dbg_siluP"], in_=sP[:]),
                        waits=[f"slu{gi}_{tc}_7"], label="dsP", dma="dtr")

                waits = [f"smul{gi}_{tc}"]
                if gi > 0 and tc == 0:
                    waits.append(f"upm{gi - 1}_1_31")

                def tr_f(sy, gi=gi, tc=tc):
                    out = []
                    for ec in range(32):
                        out.append(sy.dma_start(
                            out=Sfm[:, ec, tc * 128:(tc + 1) * 128],
                            in_=S[:, ec * 128:(ec + 1) * 128],
                            transpose=True))
                    return out
                s.add("sync", tr_f, waits=waits, label=f"tr{gi}_{tc}",
                      dma="dtr", n_dma=32)

            for pp in range(2):
                for ec in range(32):
                    waits = []
                    if ec >= 2:
                        waits.append(f"upm{gi}_{pp}_{ec - 2}")
                    elif pp == 1:
                        waits.append(f"upm{gi}_0_{30 + ec}")
                    elif gi > 0:
                        waits.append(f"upm{gi - 1}_1_{30 + ec}")
                    s.add("gpsimd",
                          lambda gp, gi=gi, pp=pp, ec=ec: gp.dma_start(
                              out=ue_t[:, ec % 2],
                              in_=ue3[:, ec, pp * 512:(pp + 1) * 512]),
                          waits=waits, label=f"lue{gi}_{pp}_{ec}", dma="dwt")

                    waits = [f"lue{gi}_{pp}_{ec}", f"tr{gi}_3"]
                    if ec == 0:
                        waits.append(f"ese{gi - 1}_7" if (pp == 0 and gi > 0)
                                     else (f"ese{gi}_3" if pp == 1 else
                                           "lw"))

                    def up_f(te, gi=gi, pp=pp, ec=ec):
                        for d4 in range(4):
                            te.matmul(pes[:, d4],
                                      ue_t[:, ec % 2,
                                           d4 * 128:(d4 + 1) * 128],
                                      Sfm[:, ec], start=(ec == 0),
                                      stop=(ec == 31))
                    s.add("tensor", up_f, waits=waits,
                          label=f"upm{gi}_{pp}_{ec}")

                for d4 in range(4):
                    dc = pp * 4 + d4
                    waits = [f"upm{gi}_{pp}_31"]
                    if gi > 0 and d4 == 0 and pp == 0:
                        waits.append(f"esx{gi - 1}")
                    s.add("vector",
                          lambda ve, dc=dc, d4=d4: ve.tensor_copy(
                              es_t[:, dc], pes[:, d4]),
                          waits=waits, label=f"ese{gi}_{pp * 4 + d4}")
            s.add("sync",
                  lambda sy, gi=gi: sy.dma_start(
                      out=esc3[:, :, gi * TG:(gi + 1) * TG], in_=es_t[:]),
                  waits=[f"ese{gi}_7"], label=f"esx{gi}", dma="dxn")

        def f1fin(sy):
            sy.wait_ge(p.sem["dxn"], p.cnt["dxn"])
        s.add("sync", f1fin, label="f1fin")
        s.emit()
    es_.close()



def emit_f2(p, d, g, dbg):
    nc = p.nc
    xnc3 = d["xn_c"].rearrange("(k p) t -> p k t", p=128)
    esc3 = d["es_c"].rearrange("(k p) t -> p k t", p=128)
    sc3 = d["s_c"].rearrange("(k p) t -> p k t", p=128)
    x3 = d["x"].rearrange("(k p) t -> p k t", p=128)
    o3 = d["out"].rearrange("(k p) t -> p k t", p=128)
    wg4 = d["wg"].rearrange("(k p) cq -> p k cq", p=128)
    wu4 = d["wu"].rearrange("(k p) cq -> p k cq", p=128)
    wd3 = d["wd"].rearrange("(c p) n -> p c n", p=128)
    bgu3 = d["bgu"].rearrange("(c p) i -> p c i", p=128)
    bd2 = d["bd"].rearrange("(k p) i -> p k i", p=128)

    NCG, CPG = 4, 8

    with (
        nc.sbuf_tensor("f2_acc", [128, NCH, T], BF16) as acc,
        nc.sbuf_tensor("f2_bd", [128, NCH, 1], F32) as bd_t,
    ):
        with (
            nc.sbuf_tensor("f2_wg", [128, 2, CPG, NCH, 128], BF16) as wgt_g,
            nc.sbuf_tensor("f2_wu", [128, 2, CPG, NCH, 128], BF16) as wgt_u,
            nc.sbuf_tensor("f2_wd", [128, 2, CPG, 1024], BF16) as wgt_d,
            nc.sbuf_tensor("f2_bgu", [128, 2, CPG, 2], F32) as bgu_t,
            nc.sbuf_tensor("f2_xn", [128, 2, NCH, TG], BF16) as xn_g,
            nc.sbuf_tensor("f2_tg", [128, 512], BF16) as t_g,
            nc.sbuf_tensor("f2_mid", [128, CPG, TG], BF16) as mid,
            nc.psum_tensor("f2_pg", [128, 2, 512], F32) as pg,
            nc.psum_tensor("f2_pu", [128, 2, 512], F32) as pu,
            nc.psum_tensor("f2_po", [128, 2, 512], F32) as po,
        ):
            s = Sched(p, "f2a")
            s.add("sync", lambda sy: sy.dma_start(out=acc[:], in_=esc3),
                  label="lacc", dma="dxi")
            s.add("gpsimd", lambda gp: gp.dma_start(out=bd_t[:], in_=bd2, allow_slow_non_contiguous=True),
                  label="lbd", dma="dwt")

            for cg in range(NCG):
                waits = []
                if cg >= 2:
                    waits.append(f"wdone{cg - 2}")

                def lw_f(gp, cg=cg):
                    pa = cg % 2
                    out = []
                    for k in range(NCH):
                        out.append(gp.dma_start(
                            out=wgt_g[:, pa, :, k, :],
                            in_=wg4[:, k, cg * 1024:(cg + 1) * 1024]))
                        out.append(gp.dma_start(
                            out=wgt_u[:, pa, :, k, :],
                            in_=wu4[:, k, cg * 1024:(cg + 1) * 1024]))
                    out.append(gp.dma_start(
                        out=wgt_d[:, pa],
                        in_=wd3[:, cg * CPG:(cg + 1) * CPG]))
                    out.append(gp.dma_start(
                        out=bgu_t[:, pa],
                        in_=bgu3[:, cg * CPG:(cg + 1) * CPG]))
                    return out
                s.add("gpsimd", lw_f, waits=waits, label=f"lwg{cg}",
                      dma="dwt", n_dma=18)

            for cg in range(NCG):
                pa = cg % 2
                for gi in range(NTG):
                    gpa = gi % 2
                    waits = []
                    if cg > 0 or gi >= 2:
                        waits.append(f"mid{cg}_{gi - 2}_7" if gi >= 2 else
                                     f"mid{cg - 1}_{6 + gi}_7")
                    s.add("sync",
                          lambda sy, cg=cg, gi=gi, gpa=gpa: sy.dma_start(
                              out=xn_g[:, gpa],
                              in_=xnc3[:, :, gi * TG:(gi + 1) * TG]),
                          waits=waits, label=f"f2x{cg}_{gi}", dma="dxi")

                    for ci in range(CPG):
                        waits = [f"f2x{cg}_{gi}", f"lwg{cg}"]
                        if not (cg == 0 and gi == 0 and ci < 2):
                            waits.append(
                                f"mid{cg}_{gi}_{ci - 2}" if ci >= 2 else
                                (f"mid{cg}_{gi - 1}_{6 + ci}" if gi > 0 else
                                 f"mid{cg - 1}_7_{6 + ci}"))

                        def ag_f(te, cg=cg, pa=pa, gi=gi, gpa=gpa, ci=ci):
                            for k in range(NCH):
                                te.matmul(pg[:, ci % 2],
                                          wgt_g[:, pa, ci, k],
                                          xn_g[:, gpa, k], start=(k == 0),
                                          stop=(k == NCH - 1))
                            for k in range(NCH):
                                te.matmul(pu[:, ci % 2],
                                          wgt_u[:, pa, ci, k],
                                          xn_g[:, gpa, k], start=(k == 0),
                                          stop=(k == NCH - 1))
                        s.add("tensor", ag_f, waits=waits,
                              label=f"agu{cg}_{gi}_{ci}")

                        wtg = [f"agu{cg}_{gi}_{ci}"]
                        if not (cg == 0 and gi == 0 and ci == 0):
                            wtg.append(
                                f"mid{cg}_{gi}_{ci - 1}" if ci > 0 else
                                (f"mid{cg}_{gi - 1}_7" if gi > 0 else
                                 f"mid{cg - 1}_7_7"))
                        s.add("scalar",
                              lambda se, pa=pa, ci=ci: se.activation(
                                  out=t_g[:], in_=pg[:, ci % 2],
                                  func=AF.Silu,
                                  bias=bgu_t[:, pa, ci, 0:1], scale=1.0),
                              waits=wtg, label=f"tg{cg}_{gi}_{ci}")

                        s.add("vector",
                              lambda ve, pa=pa, ci=ci:
                              ve.scalar_tensor_tensor(
                                  mid[:, ci], pu[:, ci % 2],
                                  bgu_t[:, pa, ci, 1:2], t_g[:],
                                  OP.add, OP.mult),
                              waits=[f"tg{cg}_{gi}_{ci}"],
                              label=f"mid{cg}_{gi}_{ci}")

                    for oc in range(8):
                        waits = [f"mid{cg}_{gi}_7"]
                        if not (cg == 0 and gi == 0 and oc < 2):
                            waits.append(
                                f"oacc{cg}_{gi}_{oc - 2}" if oc >= 2 else
                                (f"oacc{cg}_{gi - 1}_{6 + oc}" if gi > 0
                                 else f"oacc{cg - 1}_7_{6 + oc}"))

                        def wd_f(te, cg=cg, pa=pa, gi=gi, oc=oc):
                            for ci in range(CPG):
                                te.matmul(po[:, oc % 2],
                                          wgt_d[:, pa, ci,
                                                oc * 128:(oc + 1) * 128],
                                          mid[:, ci], start=(ci == 0),
                                          stop=(ci == CPG - 1))
                        s.add("tensor", wd_f, waits=waits,
                              label=f"wd{cg}_{gi}_{oc}")

                        wacc = [f"wd{cg}_{gi}_{oc}"]
                        if cg == 0 and gi == 0 and oc == 0:
                            wacc.append("lacc")

                        def acc_f(ve, gi=gi, oc=oc):
                            ve.tensor_add(acc[:, oc, gi * TG:(gi + 1) * TG],
                                          acc[:, oc, gi * TG:(gi + 1) * TG],
                                          po[:, oc % 2])
                        s.add("vector", acc_f, waits=wacc,
                              label=f"oacc{cg}_{gi}_{oc}")
                s.labels[f"wdone{cg}"] = s.labels[f"oacc{cg}_7_7"]
            s.emit()
        p.barrier()

        if "dbg_acc" in dbg:
            dump_fm(p, dbg["dbg_acc"], acc[:])

        with (
            nc.sbuf_tensor("f2_x", [128, 2, NCH, TG], F32) as x_t,
            nc.sbuf_tensor("f2_s", [128, 2, NCH, TG], BF16) as s_t,
            nc.sbuf_tensor("f2_gk", [128, TG], BF16) as gk,
            nc.sbuf_tensor("f2_o", [128, 2, NCH, TG], F32) as o_t,
        ):
            s = Sched(p, "f2b")
            for gi in range(NTG):
                gpa = gi % 2
                waits = []
                if gi >= 2:
                    waits.append(f"cmb{gi - 2}")
                    waits.append(f"co{gi - 2}")

                def lx_f(sy, gi=gi, gpa=gpa):
                    return [
                        sy.dma_start(out=x_t[:, gpa],
                                     in_=x3[:, :, gi * TG:(gi + 1) * TG]),
                        sy.dma_start(out=s_t[:, gpa],
                                     in_=sc3[:, :, gi * TG:(gi + 1) * TG]),
                    ]
                s.add("sync", lx_f, waits=waits, label=f"cx{gi}",
                      dma="dxi", n_dma=2)

                for k in range(NCH):
                    wgk = [f"cx{gi}"]
                    if k > 0:
                        wgk.append(f"cmk{gi}_{k - 1}")
                    elif gi > 0:
                        wgk.append(f"cmk{gi - 1}_7")
                    s.add("scalar",
                          lambda se, gi=gi, k=k: se.activation(
                              out=gk[:],
                              in_=acc[:, k, gi * TG:(gi + 1) * TG],
                              func=AF.Silu, bias=bd_t[:, k, 0:1],
                              scale=1.0),
                          waits=wgk, label=f"cgk{gi}_{k}")

                    def cm_f(ve, gi=gi, gpa=gpa, k=k):
                        ve.tensor_mul(gk[:], s_t[:, gpa, k], gk[:])
                        ve.tensor_add(o_t[:, gpa, k], x_t[:, gpa, k], gk[:])
                    s.add("vector", cm_f, waits=[f"cgk{gi}_{k}"],
                          label=f"cmk{gi}_{k}")
                s.labels[f"cmb{gi}"] = s.labels[f"cmk{gi}_7"]

                s.add("sync",
                      lambda sy, gi=gi, gpa=gpa: sy.dma_start(
                          out=o3[:, :, gi * TG:(gi + 1) * TG],
                          in_=o_t[:, gpa]),
                      waits=[f"cmb{gi}"], label=f"co{gi}", dma="dout")

            def f2fin(sy):
                sy.wait_ge(p.sem["dout"], p.cnt["dout"])
            s.add("sync", f2fin, label="f2fin")
            s.emit()



def stage_inputs(inputs):
    x = np.asarray(inputs["L0_attn_x"])
    filt = np.ascontiguousarray(np.concatenate(
        [np.asarray(inputs["L0_attn_h_filter"]).T,
         np.asarray(inputs["L0_attn_g_filter"]).T,
         np.asarray(inputs["L0_attn_h_synth"]).T,
         np.asarray(inputs["L0_attn_g_synth"]).T], axis=1)).astype(np.float32)
    gb = np.ascontiguousarray(np.stack(
        [np.asarray(inputs["L0_attn_norm_gamma"]),
         np.asarray(inputs["L0_attn_norm_beta"])], axis=1)).astype(np.float32)
    keysT = np.ascontiguousarray(
        np.transpose(np.asarray(inputs["L0_ffn_keys"]), (2, 0, 3, 1))
    ).astype(np.float32)
    dnT = np.ascontiguousarray(
        np.asarray(inputs["L0_ffn_down_embed"]).T).astype(np.float32)
    bgu = np.ascontiguousarray(np.stack(
        [np.asarray(inputs["L0_ffn_bg"]),
         np.asarray(inputs["L0_ffn_bu"])], axis=1)).astype(np.float32)
    shared = {
        "filt": filt, "gb": gb,
        "wq": np.ascontiguousarray(np.asarray(inputs["L0_ffn_Wq"],
                                              np.float32)),
        "keysT": keysT, "dnT": dnT,
        "ue": np.ascontiguousarray(np.asarray(inputs["L0_ffn_up_embed"],
                                              np.float32)),
        "wg": np.ascontiguousarray(np.asarray(inputs["L0_ffn_Wg"],
                                              np.float32)),
        "wu": np.ascontiguousarray(np.asarray(inputs["L0_ffn_Wu"],
                                              np.float32)),
        "wd": np.ascontiguousarray(np.asarray(inputs["L0_ffn_Wd"],
                                              np.float32)),
        "bgu": bgu,
        "bd": np.ascontiguousarray(
            np.asarray(inputs["L0_ffn_bd"], np.float32)[:, None]),
    }
    per_core = []
    for b in range(8):
        m = dict(shared)
        m["x"] = np.ascontiguousarray(x[b].T).astype(np.float32)
        per_core.append(m)
    return per_core


def kernel(**inputs):
    nc = build_program()
    in_maps = stage_inputs(inputs)
    res = run_bass_kernel_spmd(nc, in_maps, core_ids=list(range(8)))
    out = np.stack([np.asarray(r["out"]).T for r in res.results])
    return out.astype(np.float32)


# version 06
# speedup vs baseline: 1.0444x; 1.0444x over previous
import numpy as np
from contextlib import ExitStack
import concourse.bass as bass
import concourse.mybir as mybir
from concourse.bass_utils import run_bass_kernel_spmd

F32 = mybir.dt.float32
BF16 = mybir.dt.bfloat16
I32 = mybir.dt.int32
U32 = mybir.dt.uint32
AF = mybir.ActivationFunctionType
OP = mybir.AluOpType

C = 1024
T = 4096
NCH = 8
TG = 512
NTG = T // TG
LN_EPS = 1e-5
ATTN_EPS = 1e-6
NEGBIG = -1e30

RQ0, RQ1, RQ2 = 0.7765257306659091, -0.17066219872211436, 0.011377491746806564
RSQRT_C = 1.8369135901471134e-20

ENGINES = ("tensor", "vector", "scalar", "gpsimd", "sync")


class Prog:
    def __init__(self, nc):
        self.nc = nc
        self._ctx = []
        self.sem = {}
        self.cnt = {}
        self.bar_round = 0

    def keep(self, cm):
        v = cm.__enter__()
        self._ctx.append(cm)
        return v

    def add_sems(self, *names):
        for n in names:
            self.sem[n] = self.keep(self.nc.semaphore(n))
            self.cnt[n] = 0

    def close(self):
        for cm in reversed(self._ctx):
            cm.__exit__(None, None, None)

    def barrier(self):
        self.bar_round += 1
        target = self.bar_round * len(ENGINES)
        bar = self.sem["bar"]
        dummy = self.bar_dummy
        ones = self.bar_ones
        with self.nc.Block(f"bar{self.bar_round}") as blk:
            @blk.tensor
            def _(te):
                te.ldweights(ones[:, 0:1])
                te.drain().then_inc(bar, 1)
                te.wait_ge(bar, target)

            @blk.vector
            def _(ve):
                ve.memset(dummy[:, 0:1], 0.0)
                ve.drain().then_inc(bar, 1)
                ve.wait_ge(bar, target)

            @blk.scalar
            def _(se):
                se.activation(out=dummy[:, 1:2], in_=dummy[:, 1:2],
                              func=AF.Copy, bias=0.0, scale=1.0)
                se.drain().then_inc(bar, 1)
                se.wait_ge(bar, target)

            @blk.gpsimd
            def _(gp):
                gp.memset(dummy[:, 2:3], 0.0)
                gp.drain().then_inc(bar, 1)
                gp.wait_ge(bar, target)

            @blk.sync
            def _(sy):
                sy.sem_inc(bar, 1)
                sy.wait_ge(bar, target)
        self.cnt["bar"] = target


class _Rec:

    def __init__(self, eng):
        object.__setattr__(self, "_eng", eng)
        object.__setattr__(self, "last", None)

    def __getattr__(self, n):
        a = getattr(object.__getattribute__(self, "_eng"), n)
        if callable(a):
            def call(*args, **kw):
                r = a(*args, **kw)
                if hasattr(r, "then_inc"):
                    object.__setattr__(self, "last", r)
                return r
            return call
        return a


class Sched:

    def __init__(self, p, name):
        self.p = p
        self.name = name
        self.steps = {e: [] for e in ENGINES}
        self.labels = {}

    def add(self, eng, fn, waits=(), label=None, dma=None, n_dma=1):
        p = self.p
        sem_name = dma if dma else "e_" + eng
        inc = 16 * n_dma if dma else 1
        p.cnt[sem_name] += inc
        tgt = p.cnt[sem_name]
        if label is not None:
            assert label not in self.labels, label
            self.labels[label] = (sem_name, tgt)
        self.steps[eng].append((fn, tuple(waits), sem_name, tgt, dma, n_dma))

    def emit(self):
        p = self.p
        with p.nc.Block(self.name) as blk:
            for en in ENGINES:
                if not self.steps[en]:
                    continue

                def _mk(en):
                    def _(eng):
                        for fn, waits, sem_name, tgt, dma, n_dma in \
                                self.steps[en]:
                            for w in waits:
                                ws, wv = self.labels[w]
                                eng.wait_ge(p.sem[ws], wv)
                            if dma is not None:
                                instrs = fn(eng)
                                if not isinstance(instrs, (list, tuple)):
                                    instrs = [instrs]
                                assert len(instrs) == n_dma, \
                                    (self.name, n_dma, len(instrs))
                                for ins in instrs:
                                    ins.then_inc(p.sem[dma], 16)
                            elif en == "sync":
                                rec = _Rec(eng)
                                fn(rec)
                                last = object.__getattribute__(rec, "last")
                                if last is not None:
                                    last.then_inc(p.sem[sem_name], 1)
                                else:
                                    eng.sem_inc(p.sem[sem_name], 1)
                            else:
                                fn(eng)
                                eng.drain().then_inc(p.sem[sem_name], 1)
                    return _
                getattr(blk, en)(_mk(en))


def _width(ap):
    n = 1
    for d in ap.shape[1:]:
        n *= d
    return n


def emit_rsqrt(ve, out, v, z, a, xor_31, iters=3):
    gap = (lambda: ve.drain()) if _width(v) < 64 else (lambda: 0)
    ve.tensor_single_scalar(z.bitcast(I32), v.bitcast(I32), 1,
                            OP.logical_shift_right)
    gap()
    ve.tensor_scalar(z.bitcast(I32), z.bitcast(I32), xor_31, None,
                     OP.bitwise_xor)
    gap()
    ve.tensor_scalar_mul(z, z, RSQRT_C)
    for i in range(iters):
        last = i == iters - 1
        gap()
        ve.tensor_mul(a, z, z)
        gap()
        ve.tensor_mul(a, a, v)
        gap()
        ve.tensor_scalar(a, a, 3.0, -0.5, OP.subtract, OP.mult)
        gap()
        ve.tensor_mul(out if last else z, z, a)


def emit_recip(ve, out, x, r, u, xor_exp, iters=3):
    gap = (lambda: ve.drain()) if _width(x) < 64 else (lambda: 0)
    ve.tensor_scalar(r.bitcast(I32), x.bitcast(I32), xor_exp, None,
                     OP.bitwise_xor)
    gap()
    ve.tensor_mul(u, r, x)
    gap()
    ve.tensor_scalar(out, u, RQ2, RQ1, OP.mult, OP.add)
    gap()
    ve.tensor_mul(out, out, u)
    gap()
    ve.tensor_scalar_add(out, out, RQ0)
    gap()
    ve.tensor_mul(r, out, r)
    for i in range(iters):
        last = i == iters - 1
        gap()
        ve.tensor_mul(u, r, x)
        gap()
        ve.tensor_scalar(u, u, 2.0, -1.0, OP.subtract, OP.mult)
        gap()
        ve.tensor_mul(out if last else r, r, u)



def build_program(debug=(), phases=("ln", "attn", "f1", "f2")):
    nc = bass.Bass("TRN2", target_bir_lowering=False,
                   detect_race_conditions=False)
    p = Prog(nc)
    p.add_sems("bar", "e_tensor", "e_vector", "e_scalar", "e_gpsimd",
               "e_sync", "dxi", "dxn", "dwt", "dout", "dtr")

    d = {}

    def din(name, shape, dtype=F32):
        d[name] = nc.dram_tensor(name, shape, dtype, kind="ExternalInput").ap()

    din("x", [C, T])
    din("filt", [C, 16])
    din("gb", [C, 2])
    din("wq", [C, 512])
    din("keysT", [2, 4, 64, 64])
    din("dnT", [C, 4096])
    din("ue", [4096, C])
    din("wg", [C, 4096])
    din("wu", [C, 4096])
    din("wd", [4096, C])
    din("bgu", [4096, 2])
    din("bd", [C, 1])

    d["xn_c"] = nc.dram_tensor("xn_c", [C, T], BF16, kind="Internal").ap()
    d["s_c"] = nc.dram_tensor("s_c", [C, T], BF16, kind="Internal").ap()
    d["es_c"] = nc.dram_tensor("es_c", [C, T], BF16, kind="Internal").ap()
    d["out"] = nc.dram_tensor("out", [C, T], F32, kind="ExternalOutput").ap()

    dbg = {}
    for name, shape, dt_ in [
        ("dbg_xn", [C, T], BF16), ("dbg_h0a", [C, T // 2], BF16),
        ("dbg_g0", [C, T // 2], BF16), ("dbg_l1", [C, T // 4], BF16),
        ("dbg_g1", [C, T // 4], BF16), ("dbg_att", [C, T // 4], BF16),
        ("dbg_low1", [C, T // 2], BF16), ("dbg_s", [C, T], BF16),
        ("dbg_q", [C, T // 4], BF16), ("dbg_es", [C, T], BF16),
        ("dbg_kvz", [128, 16 * 65], BF16), ("dbg_attm", [128, 8 * 1024], BF16),
        ("dbg_dens", [128, 8 * 16], F32), ("dbg_qtm", [128, 8 * 1024], BF16),
        ("dbg_S", [128, 4096], BF16), ("dbg_siluP", [128, 4096], BF16),
        ("dbg_qret", [512, T], BF16), ("dbg_acc", [C, T], BF16),
        ("dbg_pnz", [128, 520], F32), ("dbg_rcp", [128, 128], F32),
    ]:
        if name in debug:
            dbg[name] = nc.dram_tensor(name, shape, dt_,
                                       kind="ExternalOutput").ap()

    p.bar_dummy = p.keep(nc.sbuf_tensor("bar_dummy", [1, 4], F32))
    consts = p.keep(nc.sbuf_tensor("consts", [128, 8], F32))
    ones128 = p.keep(nc.sbuf_tensor("ones128", [128, 128], BF16))
    ident = p.keep(nc.sbuf_tensor("ident", [128, 128], BF16))
    identi = p.keep(nc.sbuf_tensor("identi", [128, 128], I32))
    negbig64 = p.keep(nc.sbuf_tensor("negbig64", [128, 64], F32))
    onecol = p.keep(nc.sbuf_tensor("onecol", [128, 1], BF16))

    g = dict(consts=consts, ones128=ones128, ident=ident,
             negbig64=negbig64, onecol=onecol,
             XOR_EXP=consts.bitcast(I32)[:, 0:1],
             XOR_31=consts.bitcast(I32)[:, 1:2])

    with nc.Block("init") as blk:
        @blk.vector
        def _(ve):
            ve.memset(consts.bitcast(U32)[:, 0:1], 0x7F800000)
            ve.memset(consts.bitcast(U32)[:, 1:2], 0x7FFFFFFF)
            ve.memset(consts[:, 2:3], NEGBIG)
            ve.memset(ones128[:], 1.0)
            ve.memset(negbig64[:], NEGBIG)
            ve.memset(onecol[:], 1.0)
            ve.memset(p.bar_dummy[:], 0.0)
            ve.wait_ge(p.sem["e_gpsimd"], 1)
            ve.tensor_single_scalar(ident[:], identi[:], 0.0, OP.is_equal)

        @blk.gpsimd
        def _(gp):
            gp.iota(identi[:], [[1, 128]], channel_multiplier=-1,
                    allow_small_or_imprecise_dtypes=True
                    ).then_inc(p.sem["e_gpsimd"], 1)
    p.cnt["e_gpsimd"] += 1
    p.bar_ones = ones128
    p.barrier()

    xn_cm = nc.sbuf_tensor("xn_sb", [128, NCH * T], BF16)
    xn_sb = xn_cm.__enter__()
    xn3 = xn_sb[:].rearrange("p (k t) -> p k t", k=NCH)

    if "ln" in phases:
        emit_ln(p, d, g, xn3)
        p.barrier()
        if "dbg_xn" in dbg:
            dump_fm(p, dbg["dbg_xn"], xn3)

    if "attn" in phases:
        emit_attn(p, d, g, xn3, dbg)
        p.barrier()

    xn_cm.__exit__(None, None, None)

    if "f1" in phases:
        emit_f1(p, d, g, dbg)
        p.barrier()

    if "f2" in phases:
        emit_f2(p, d, g, dbg)
        p.barrier()

    p.close()
    return nc


def dump_fm(p, dram, sb3):
    nc = p.nc
    dr = dram.rearrange("(k p) t -> p k t", p=128)
    p.cnt["dtr"] += 16
    tgt = p.cnt["dtr"]
    p.dump_n = getattr(p, "dump_n", 0) + 1
    with nc.Block(f"dump{p.dump_n}") as blk:
        @blk.sync
        def _(sy):
            sy.dma_start(out=dr, in_=sb3).then_inc(p.sem["dtr"], 16)
            sy.wait_ge(p.sem["dtr"], tgt)
    p.barrier()


def dump_flat(p, dram, sb):
    nc = p.nc
    p.cnt["dtr"] += 16
    tgt = p.cnt["dtr"]
    p.dump_n = getattr(p, "dump_n", 0) + 1
    with nc.Block(f"dumpf{p.dump_n}") as blk:
        @blk.sync
        def _(sy):
            sy.dma_start(out=dram, in_=sb).then_inc(p.sem["dtr"], 16)
            sy.wait_ge(p.sem["dtr"], tgt)
    p.barrier()



def emit_ln(p, d, g, xn3):
    nc = p.nc
    x3 = d["x"].rearrange("(k p) t -> p k t", p=128)
    xnc3 = d["xn_c"].rearrange("(k p) t -> p k t", p=128)

    with (
        nc.sbuf_tensor("ln_x", [128, 2, NCH, TG], F32) as x_t,
        nc.sbuf_tensor("ln_xbf", [128, 2, NCH, TG], BF16) as xbf,
        nc.sbuf_tensor("ln_xsq", [128, 2, NCH, TG], BF16) as xsq,
        nc.sbuf_tensor("ln_mu", [128, 2, TG], F32) as mu_t,
        nc.sbuf_tensor("ln_rs", [128, TG], F32) as rs_t,
        nc.sbuf_tensor("ln_v", [128, TG], F32) as v_t,
        nc.sbuf_tensor("ln_s1", [128, TG], F32) as s1_t,
        nc.sbuf_tensor("ln_s2", [128, TG], F32) as s2_t,
        nc.psum_tensor("ln_ps", [128, 2, 2, TG], F32) as ps,
    ):
        s = Sched(p, "ln")
        for gi in range(NTG):
            pa = gi % 2

            def loadx(sy, gi=gi, pa=pa):
                return sy.dma_start(out=x_t[:, pa],
                                    in_=x3[:, :, gi * TG:(gi + 1) * TG])
            w = [f"ap{gi - 2}"] if gi >= 2 else []
            s.add("sync", loadx, waits=w, label=f"lx{gi}", dma="dxi")

            def xbf_f(ve, gi=gi, pa=pa):
                ve.tensor_copy(xbf[:, pa], x_t[:, pa])
            s.add("vector", xbf_f, waits=[f"lx{gi}"], label=f"xb{gi}")

            def xsq_f(se, gi=gi, pa=pa):
                se.activation(out=xsq[:, pa], in_=x_t[:, pa], func=AF.Square,
                              bias=0.0, scale=1.0)
            wq_ = [f"lx{gi}"] + ([f"st{gi - 2}"] if gi >= 2 else [])
            s.add("scalar", xsq_f, waits=wq_, label=f"xq{gi}")

            def stats(te, gi=gi, pa=pa):
                for k in range(NCH):
                    te.matmul(ps[:, pa, 0], g["ones128"][:], xbf[:, pa, k],
                              start=(k == 0), stop=(k == NCH - 1))
                for k in range(NCH):
                    te.matmul(ps[:, pa, 1], g["ones128"][:], xsq[:, pa, k],
                              start=(k == 0), stop=(k == NCH - 1))
            w = [f"xb{gi}", f"xq{gi}"] + ([f"mo{gi - 2}"] if gi >= 2 else [])
            s.add("tensor", stats, waits=w, label=f"st{gi}")

            def moments(ve, gi=gi, pa=pa):
                ve.tensor_scalar_mul(mu_t[:, pa], ps[:, pa, 0], 1.0 / C)
                ve.tensor_mul(v_t[:], mu_t[:, pa], mu_t[:, pa])
                ve.scalar_tensor_tensor(v_t[:], ps[:, pa, 1], 1.0 / C,
                                        v_t[:], OP.mult, OP.subtract)
                ve.tensor_scalar_add(v_t[:], v_t[:], LN_EPS)
            s.add("vector", moments, waits=[f"st{gi}"], label=f"mo{gi}")

            def apply(ve, gi=gi, pa=pa):
                emit_rsqrt(ve, rs_t[:], v_t[:], s1_t[:], s2_t[:],
                           g["XOR_31"])
                for k in range(NCH):
                    ve.tensor_sub(s1_t[:], x_t[:, pa, k], mu_t[:, pa])
                    ve.tensor_mul(xn3[:, k, gi * TG:(gi + 1) * TG],
                                  s1_t[:], rs_t[:])
            s.add("vector", apply, label=f"ap{gi}")

            def store(sy, gi=gi):
                return sy.dma_start(out=xnc3[:, :, gi * TG:(gi + 1) * TG],
                                    in_=xn3[:, :, gi * TG:(gi + 1) * TG])
            s.add("sync", store, waits=[f"ap{gi}"], label=f"sx{gi}",
                  dma="dxn")
        def lnfin(sy):
            sy.wait_ge(p.sem["dxn"], p.cnt["dxn"])
        s.add("sync", lnfin, label="lnfin")
        s.emit()



def emit_attn(p, d, g, xn3, dbg):
    nc = p.nc
    NT2, NT4 = T // 2, T // 4

    with (
        nc.sbuf_tensor("at_filt", [128, NCH, 16], F32) as filt,
        nc.sbuf_tensor("at_gb", [128, NCH, 2], F32) as gb,
        nc.sbuf_tensor("at_g0", [128, NCH, NT2], BF16) as g0,
        nc.sbuf_tensor("at_g1", [128, NCH, NT4], BF16) as g1,
        nc.sbuf_tensor("at_att", [128, NCH, NT4], BF16) as att,
    ):
        with nc.sbuf_tensor("at_l1", [128, NCH, NT4], BF16) as l1:
            with (
                nc.sbuf_tensor("at_h0a", [128, NCH, NT2], BF16) as h0a,
                nc.sbuf_tensor("at_scr", [128, NT2], F32) as scr,
            ):
                s = Sched(p, "attnA")

                def loadf(sy):
                    return [sy.dma_start(
                        out=filt[:], in_=d["filt"].rearrange(
                            "(k p) i -> p k i", p=128)),
                        sy.dma_start(out=gb[:], in_=d["gb"].rearrange(
                            "(k p) i -> p k i", p=128))]
                s.add("sync", loadf, label="lf", dma="dxi", n_dma=2)

                def conv(ve, out3, in3, fcol, Lin, k):
                    Lo = Lin // 2
                    zi = in3[:, k]
                    f = lambda i: filt[:, k, fcol + i:fcol + i + 1]
                    ve.tensor_scalar(scr[:, 0:Lo], zi[:, 0:Lin:2], f(1),
                                     None, OP.mult)
                    ve.scalar_tensor_tensor(scr[:, 0:Lo], zi[:, 1:Lin:2],
                                            f(2), scr[:, 0:Lo], OP.mult,
                                            OP.add)
                    ve.scalar_tensor_tensor(scr[:, 1:Lo], zi[:, 1:Lin - 2:2],
                                            f(0), scr[:, 1:Lo], OP.mult,
                                            OP.add)
                    ve.scalar_tensor_tensor(scr[:, 0:Lo - 1], zi[:, 2:Lin:2],
                                            f(3), scr[:, 0:Lo - 1], OP.mult,
                                            OP.add)
                    ve.tensor_copy(out3[:, k], scr[:, 0:Lo])

                for k in range(NCH):
                    s.add("vector",
                          lambda ve, k=k: conv(ve, h0a, xn3, 0, T, k),
                          waits=["lf"], label=f"h0a{k}")
                    s.add("vector",
                          lambda ve, k=k: conv(ve, g0, xn3, 4, T, k),
                          label=f"g0{k}")
                for k in range(NCH):
                    s.add("vector",
                          lambda ve, k=k: conv(ve, l1, h0a, 0, NT2, k),
                          label=f"l1{k}")
                    s.add("vector",
                          lambda ve, k=k: conv(ve, g1, h0a, 4, NT2, k),
                          label=f"g1{k}")
                s.emit()
                p.barrier()

                if "dbg_h0a" in dbg:
                    dump_fm(p, dbg["dbg_h0a"], h0a[:])
                if "dbg_g0" in dbg:
                    dump_fm(p, dbg["dbg_g0"], g0[:])
                if "dbg_l1" in dbg:
                    dump_fm(p, dbg["dbg_l1"], l1[:])
                if "dbg_g1" in dbg:
                    dump_fm(p, dbg["dbg_g1"], g1[:])

            emit_linattn(p, d, g, l1, att, dbg)
            p.barrier()
            if "dbg_att" in dbg:
                dump_fm(p, dbg["dbg_att"], att[:])

        emit_up_sln(p, d, g, xn3, att, g0, g1, gb, filt, dbg)
        p.barrier()
        if "dbg_s" in dbg:
            dump_fm(p, dbg["dbg_s"], xn3)


def emit_linattn(p, d, g, l1, att, dbg):
    nc = p.nc
    N = T // 4
    NTC = N // 128

    with (
        nc.sbuf_tensor("la_q", [128, NCH, N], BF16) as q,
        nc.sbuf_tensor("la_qtm", [128, NTC, C], BF16) as q_tm,
        nc.sbuf_tensor("la_vtm", [128, NTC, C], BF16) as v_tm,
        nc.sbuf_tensor("la_kvz", [128, 8, 130], BF16) as kvz,
        nc.sbuf_tensor("la_dens", [128, 8, 16], F32) as dens,
        nc.sbuf_tensor("la_rcp", [128, 8, 16], F32) as rcp,
        nc.sbuf_tensor("la_r1", [128, 8, 16], F32) as r1,
        nc.sbuf_tensor("la_r2", [128, 8, 16], F32) as r2,
        nc.sbuf_tensor("la_scA", [128, N], F32) as scA,
        nc.sbuf_tensor("la_scB", [128, N], F32) as scB,
        nc.psum_tensor("la_pst", [128, 2, 1024], BF16) as pst,
        nc.psum_tensor("la_kv", [64, 2, 512], F32) as pkv,
        nc.psum_tensor("la_nz", [128, 4, 512], F32) as pnz,
    ):
        att_tm = v_tm
        s = Sched(p, "linattn")

        def zkvz(ve):
            ve.memset(kvz[:], 0.0)
        s.add("vector", zkvz, label="zkvz")

        for k in range(NCH):
            s.add("vector",
                  lambda ve, k=k: ve.tensor_scalar_min(scA[:], l1[:, k], 0.0),
                  label=f"qm{k}", waits=[f"qq{k - 1}"] if k else [])
            s.add("scalar",
                  lambda se, k=k: se.activation(out=scB[:], in_=scA[:],
                                                func=AF.Exp, bias=0.0,
                                                scale=1.0),
                  waits=[f"qm{k}"], label=f"qe{k}")
            s.add("vector",
                  lambda ve, k=k: ve.scalar_tensor_tensor(
                      q[:, k], l1[:, k], 0.0, scB[:], OP.max, OP.add),
                  waits=[f"qe{k}"], label=f"qq{k}")

        w = 0
        for src, dst, nm in ((q, q_tm, "q"), (l1, v_tm, "v")):
            for tc in range(NTC):
                for k in range(NCH):
                    w += 1
                    waits = []
                    if nm == "q":
                        waits.append(f"qq{k}")
                    if w > 2:
                        waits.append(f"tev{w - 2}")
                    s.add("tensor",
                          lambda te, src=src, tc=tc, k=k, w=w: te.transpose(
                              pst[:, w % 2, 0:128],
                              src[:, k, tc * 128:(tc + 1) * 128],
                              g["ident"][:]),
                          waits=waits, label=f"tt{w}")
                    s.add("vector",
                          lambda ve, dst=dst, tc=tc, k=k, w=w: ve.tensor_copy(
                              dst[:, tc, k * 128:(k + 1) * 128],
                              pst[:, w % 2, 0:128]),
                          waits=[f"tt{w}"], label=f"tev{w}")
        ntr = w

        for h in range(16):
            pa = h % 2
            waits = [f"tev{ntr}"] if h == 0 else []
            if h >= 2:
                waits.append(f"kve{h - 2}")
            def kv_f(te, h=h, pa=pa):
                for tc in range(NTC):
                    te.matmul(pkv[:, pa, 0:64], q_tm[:, tc, h * 64:(h + 1) * 64],
                              v_tm[:, tc, h * 64:(h + 1) * 64],
                              start=(tc == 0), stop=(tc == NTC - 1))
                for tc in range(NTC):
                    te.matmul(pkv[:, pa, 64:65], q_tm[:, tc, h * 64:(h + 1) * 64],
                              g["onecol"][:], start=(tc == 0),
                              stop=(tc == NTC - 1))
            s.add("tensor", kv_f, waits=waits, label=f"kv{h}")

            def kve_f(ve, h=h, pa=pa):
                j, half = h // 2, h % 2
                ve.tensor_copy(kvz[half * 64:(half + 1) * 64, j,
                                   half * 65:half * 65 + 64],
                               pkv[:, pa, 0:64])
                ve.tensor_copy(kvz[half * 64:(half + 1) * 64, j,
                                   half * 65 + 64:half * 65 + 65],
                               pkv[:, pa, 64:65])
            s.add("vector", kve_f, waits=[f"kv{h}"], label=f"kve{h}")

        nn = 0
        for tc in range(NTC):
            for j in range(8):
                nn += 1
                waits = [f"kve{2 * j + 1}"]
                if nn > 4:
                    waits.append(f"nze{nn - 4}")

                def nz_f(te, tc=tc, j=j, nn=nn):
                    te.matmul(pnz[:, nn % 4, 0:130],
                              q[:, j, tc * 128:(tc + 1) * 128],
                              kvz[:, j], start=True, stop=True)
                s.add("tensor", nz_f, waits=waits, label=f"nz{nn}")

                def nze_f(ve, tc=tc, j=j, nn=nn):
                    ve.tensor_scalar_add(dens[:, tc, 2 * j:2 * j + 1],
                                         pnz[:, nn % 4, 64:65], ATTN_EPS)
                    ve.tensor_scalar_add(dens[:, tc, 2 * j + 1:2 * j + 2],
                                         pnz[:, nn % 4, 129:130], ATTN_EPS)
                    ve.tensor_scalar_add(att_tm[:, tc, 2 * j * 64:
                                                 (2 * j + 1) * 64],
                                         pnz[:, nn % 4, 0:64], 0.0)
                    ve.tensor_scalar_add(att_tm[:, tc, (2 * j + 1) * 64:
                                                 (2 * j + 2) * 64],
                                         pnz[:, nn % 4, 65:129], 0.0)
                    if "dbg_pnz" in dbg and nn in (1, 5):
                        ix = 0 if nn == 1 else 1
                        ve.tensor_copy(scA[:, ix * 260:ix * 260 + 130],
                                       pnz[:, nn % 4, 0:130])
                        ve.tensor_copy(
                            scA[:, ix * 260 + 130:ix * 260 + 258],
                            att_tm[:, tc, 2 * j * 64:(2 * j + 2) * 64])
                s.add("vector", nze_f, waits=[f"nz{nn}"],
                      label=f"nze{nn}")

        s.add("vector",
              lambda ve: emit_recip(ve, rcp[:], dens[:], r1[:], r2[:],
                                    g["XOR_EXP"]),
              label="rcpa")
        for tc in range(NTC):
            def norm_f(ve, tc=tc):
                for h in range(16):
                    ve.tensor_scalar(att_tm[:, tc, h * 64:(h + 1) * 64],
                                     att_tm[:, tc, h * 64:(h + 1) * 64],
                                     rcp[:, tc, h:h + 1], None, OP.mult)
            s.add("vector", norm_f, waits=["rcpa"], label=f"nrm{tc}")

        w2 = 0
        for tc in range(NTC):
            for k in range(NCH):
                w2 += 1
                waits = [f"nrm{tc}"]
                if w2 > 2:
                    waits.append(f"tae{w2 - 2}")
                s.add("tensor",
                      lambda te, tc=tc, k=k, w2=w2: te.transpose(
                          pst[:, w2 % 2, 0:128],
                          att_tm[:, tc, k * 128:(k + 1) * 128],
                          g["ident"][:]),
                      waits=waits, label=f"ta{w2}")
                s.add("vector",
                      lambda ve, tc=tc, k=k, w2=w2: ve.tensor_copy(
                          att[:, k, tc * 128:(tc + 1) * 128],
                          pst[:, w2 % 2, 0:128]),
                      waits=[f"ta{w2}"], label=f"tae{w2}")
        if "dbg_kvz" in dbg:
            s.add("sync", lambda sy: sy.dma_start(
                out=dbg["dbg_kvz"].rearrange("p (h e) -> p h e", h=8),
                in_=kvz[:]), waits=["kve15"], label="dkvz", dma="dtr")
        if "dbg_q" in dbg:
            s.add("sync", lambda sy: sy.dma_start(
                out=dbg["dbg_q"].rearrange("(k p) t -> p k t", p=128),
                in_=q[:]), waits=["qq7"], label="dq", dma="dtr")
        if "dbg_attm" in dbg:
            s.add("sync", lambda sy: sy.dma_start(
                out=dbg["dbg_attm"].rearrange("p (a b) -> p a b", a=8),
                in_=att_tm[:]), waits=[f"nrm{NTC - 1}"], label="dattm",
                dma="dtr")
        if "dbg_rcp" in dbg:
            s.add("sync", lambda sy: sy.dma_start(
                out=dbg["dbg_rcp"].rearrange("p (a b) -> p a b", a=8),
                in_=rcp[:]),
                waits=[f"nrm{NTC - 1}"], label="drcp", dma="dtr")
        if "dbg_pnz" in dbg:
            s.add("sync", lambda sy: sy.dma_start(
                out=dbg["dbg_pnz"], in_=scA[:, 0:520]),
                waits=["nze5"], label="dpnz", dma="dtr")
        if "dbg_dens" in dbg:
            s.add("sync", lambda sy: sy.dma_start(
                out=dbg["dbg_dens"].rearrange("p (a b) -> p a b", a=8),
                in_=dens[:]),
                waits=[f"nrm{NTC - 1}"], label="ddens", dma="dtr")
        if "dbg_qtm" in dbg:
            s.add("sync", lambda sy: sy.dma_start(
                out=dbg["dbg_qtm"].rearrange("p (a b) -> p a b", a=8),
                in_=q_tm[:]), waits=["tev128"], label="dqtm", dma="dtr")
        s.emit()


def emit_up_sln(p, d, g, xn3, att, g0, g1, gb, filt, dbg):
    nc = p.nc
    NT2, NT4 = T // 2, T // 4
    sc3 = d["s_c"].rearrange("(k p) t -> p k t", p=128)

    with (
        nc.sbuf_tensor("up_comb", [128, NCH, 10], F32) as comb,
        nc.sbuf_tensor("up_diag", [128, 2, 10, 128], BF16) as diag,
        nc.sbuf_tensor("up_low1", [128, NCH, NT2], BF16) as low1,
        nc.sbuf_tensor("up_rsq", [128, 2, NCH, TG], BF16) as rsq,
        nc.sbuf_tensor("up_mu", [128, 2, TG], F32) as mu_t,
        nc.sbuf_tensor("up_rs", [128, TG], F32) as rs_t,
        nc.sbuf_tensor("up_s1", [128, TG], F32) as s1_t,
        nc.sbuf_tensor("up_s2", [128, TG], F32) as s2_t,
        nc.sbuf_tensor("up_v", [128, TG], F32) as v_t,
        nc.psum_tensor("up_ps", [128, 4, 512], F32) as ps,
        nc.psum_tensor("up_st", [128, 2, 2, TG], F32) as pstat,
    ):
        s = Sched(p, "upconv")

        def combs(ve):
            for k in range(NCH):
                for base, col in ((8, 0), (12, 5)):
                    f = lambda i: filt[:, k, base + i:base + i + 1]
                    cs = lambda j: comb[:, k, col + j:col + j + 1]
                    ve.tensor_copy(cs(0), f(0))
                    ve.tensor_add(cs(1), f(1), f(2))
                    ve.tensor_copy(cs(2), f(3))
                    ve.tensor_add(cs(3), f(0), f(1))
                    ve.tensor_add(cs(4), f(2), f(3))
            ve.drain()
        s.add("vector", combs, label="cmb")

        def add_diag(tag, k, waits):
            def f(ve, k=k):
                for j in range(10):
                    ve.tensor_scalar(diag[:, k % 2, j], g["ident"][:],
                                     comb[:, k, j:j + 1], None, OP.mult)
            s.add("vector", f, waits=waits, label=tag)

        def up_pass(passname, vin1, vin2, Lh, nslots_used, evac_eng,
                    evac_fn, prev_diag_user):
            ngr = Lh // 512
            wv = 0
            for k in range(NCH):
                dwaits = ["cmb"]
                if k >= 2:
                    dwaits.append(f"{passname}mm{k - 2}_last")
                add_diag(f"{passname}dg{k}", k, dwaits)
                for ph in range(2):
                    for tg2 in range(ngr):
                        wv += 1
                        lo = tg2 * 512
                        slot = (wv - 1) % 4
                        waits = [f"{passname}dg{k}"]
                        if wv > 4:
                            waits.append(f"{passname}ev{wv - 4}")

                        def mm_f(te, k=k, ph=ph, lo=lo, slot=slot,
                                 vin1=vin1, vin2=vin2, Lh=Lh):
                            seq = []
                            for (src, base) in ((vin1, 0), (vin2, 5)):
                                if ph == 0:
                                    seq += [(src, base + 1, 0),
                                            (src, base + 0, -1),
                                            (src, base + 2, +1)]
                                else:
                                    seq += [(src, base + 3, 0),
                                            (src, base + 4, +1)]
                            for idx, (src, tap, off) in enumerate(seq):
                                a, b = lo + off, lo + off + 512
                                pslice = ps[:, slot, 0:512]
                                if a < 0:
                                    pslice = ps[:, slot, 1:512]
                                    a, b = 0, 511
                                elif b > Lh:
                                    pslice = ps[:, slot, 0:512 - (b - Lh)]
                                    b = Lh
                                te.matmul(pslice, diag[:, k % 2, tap],
                                          src[:, k, a:b],
                                          start=(idx == 0),
                                          stop=(idx == len(seq) - 1))
                        lbl = f"{passname}mm{wv}"
                        s.add("tensor", mm_f, waits=waits, label=lbl)
                        if ph == 1 and tg2 == ngr - 1:
                            s.labels[f"{passname}mm{k}_last"] = \
                                s.labels[lbl]
                        s.add(evac_eng,
                              lambda e, k=k, ph=ph, tg2=tg2, slot=slot:
                              evac_fn(e, k, ph, tg2, slot),
                              waits=[lbl], label=f"{passname}ev{wv}")
            return wv

        def ev1(se, k, ph, tg2, slot):
            se.activation(out=low1[:, k, tg2 * 1024 + ph:
                                   (tg2 + 1) * 1024:2],
                          in_=ps[:, slot, 0:512], func=AF.Copy, bias=0.0,
                          scale=1.0)
        n1 = up_pass("u1", att, g1, NT4, 4, "scalar", ev1, None)

        def ev0(ve, k, ph, tg2, slot):
            ve.tensor_add(xn3[:, k, tg2 * 1024 + ph:(tg2 + 1) * 1024:2],
                          ps[:, slot, 0:512],
                          xn3[:, k, tg2 * 1024 + ph:(tg2 + 1) * 1024:2])
        def up0():
            ngr = NT2 // 512
            wv = 0
            for k in range(NCH):
                dwaits = [f"u1ev{n1}"] if k == 0 else []
                dwaits.append("cmb")
                if k >= 2:
                    dwaits.append(f"u0mm{k - 2}_last")
                add_diag(f"u0dg{k}", k, dwaits)
                for ph in range(2):
                    for tg2 in range(ngr):
                        wv += 1
                        lo = tg2 * 512
                        slot = (wv - 1) % 4
                        waits = [f"u0dg{k}"]
                        if wv > 4:
                            waits.append(f"u0ev{wv - 4}")

                        def mm_f(te, k=k, ph=ph, lo=lo, slot=slot):
                            seq = []
                            for (src, base) in ((low1, 0), (g0, 5)):
                                if ph == 0:
                                    seq += [(src, base + 1, 0),
                                            (src, base + 0, -1),
                                            (src, base + 2, +1)]
                                else:
                                    seq += [(src, base + 3, 0),
                                            (src, base + 4, +1)]
                            for idx, (src, tap, off) in enumerate(seq):
                                a, b = lo + off, lo + off + 512
                                pslice = ps[:, slot, 0:512]
                                if a < 0:
                                    pslice = ps[:, slot, 1:512]
                                    a, b = 0, 511
                                elif b > NT2:
                                    pslice = ps[:, slot, 0:512 - (b - NT2)]
                                    b = NT2
                                te.matmul(pslice, diag[:, k % 2, tap],
                                          src[:, k, a:b],
                                          start=(idx == 0),
                                          stop=(idx == len(seq) - 1))
                        lbl = f"u0mm{wv}"
                        s.add("tensor", mm_f, waits=waits, label=lbl)
                        if ph == 1 and tg2 == ngr - 1:
                            s.labels[f"u0mm{k}_last"] = s.labels[lbl]
                        s.add("vector",
                              lambda ve, k=k, ph=ph, tg2=tg2, slot=slot:
                              ev0(ve, k, ph, tg2, slot),
                              waits=[lbl], label=f"u0ev{wv}")
            return wv
        n0 = up0()

        for gi in range(NTG):
            pa = gi % 2

            def sq_f(se, gi=gi, pa=pa):
                for k in range(NCH):
                    se.activation(out=rsq[:, pa, k],
                                  in_=xn3[:, k, gi * TG:(gi + 1) * TG],
                                  func=AF.Square, bias=0.0, scale=1.0)
            waits = [f"u0ev{n0}"] if gi == 0 else []
            if gi >= 2:
                waits.append(f"sap{gi - 2}")
            s.add("scalar", sq_f, waits=waits, label=f"ssq{gi}")

            def sstat(te, gi=gi, pa=pa):
                for k in range(NCH):
                    te.matmul(pstat[:, pa, 0], g["ones128"][:],
                              xn3[:, k, gi * TG:(gi + 1) * TG],
                              start=(k == 0), stop=(k == NCH - 1))
                for k in range(NCH):
                    te.matmul(pstat[:, pa, 1], g["ones128"][:],
                              rsq[:, pa, k], start=(k == 0),
                              stop=(k == NCH - 1))
            w = [f"ssq{gi}"] + ([f"smo{gi - 2}"] if gi >= 2 else [])
            s.add("tensor", sstat, waits=w, label=f"sst{gi}")

            def smom(ve, gi=gi, pa=pa):
                ve.tensor_scalar_mul(mu_t[:, pa], pstat[:, pa, 0], 1.0 / C)
                ve.tensor_mul(v_t[:], mu_t[:, pa], mu_t[:, pa])
                ve.scalar_tensor_tensor(v_t[:], pstat[:, pa, 1], 1.0 / C,
                                        v_t[:], OP.mult, OP.subtract)
                ve.tensor_scalar_add(v_t[:], v_t[:], LN_EPS)
            s.add("vector", smom, waits=[f"sst{gi}"], label=f"smo{gi}")

            def sap(ve, gi=gi, pa=pa):
                emit_rsqrt(ve, rs_t[:], v_t[:], s1_t[:], s2_t[:],
                           g["XOR_31"])
                for k in range(NCH):
                    ve.tensor_sub(s1_t[:], xn3[:, k, gi * TG:(gi + 1) * TG],
                                  mu_t[:, pa])
                    ve.tensor_mul(s1_t[:], s1_t[:], rs_t[:])
                    ve.tensor_scalar(xn3[:, k, gi * TG:(gi + 1) * TG],
                                     s1_t[:], gb[:, k, 0:1], gb[:, k, 1:2],
                                     OP.mult, OP.add)
            s.add("vector", sap, label=f"sap{gi}")

            s.add("sync",
                  lambda sy, gi=gi: sy.dma_start(
                      out=sc3[:, :, gi * TG:(gi + 1) * TG],
                      in_=xn3[:, :, gi * TG:(gi + 1) * TG]),
                  waits=[f"sap{gi}"], label=f"ssx{gi}", dma="dxn")
        s.add("sync", lambda sy: sy.wait_ge(p.sem["dxn"], p.cnt["dxn"]),
              label="upfin")
        if "dbg_low1" in dbg:
            s.add("sync", lambda sy: sy.dma_start(
                out=dbg["dbg_low1"].rearrange("(k p) t -> p k t", p=128),
                in_=low1[:]), waits=[f"u1ev{n1}"], label="dlow1", dma="dtr")
        s.emit()



def emit_f1(p, d, g, dbg):
    nc = p.nc
    xnc3 = d["xn_c"].rearrange("(k p) t -> p k t", p=128)
    esc3 = d["es_c"].rearrange("(k p) t -> p k t", p=128)
    ue3 = d["ue"].rearrange("(ec p) dd -> p ec dd", p=128)

    es_ = ExitStack()
    def sb(name, shape, dt_):
        return es_.enter_context(nc.sbuf_tensor(name, shape, dt_))
    def psb(name, shape, dt_):
        return es_.enter_context(nc.psum_tensor(name, shape, dt_))
    dn = sb("f1_dn", [128, NCH, 4096], BF16)
    keys = sb("f1_keys", [128, 4, 128], BF16)
    wq = sb("f1_wq", [128, NCH, 512], BF16)
    xn_g = sb("f1_xn", [128, 2, NCH, TG], BF16)
    qret = sb("f1_qret", [128, 4, TG], BF16)
    Bm = sb("f1_Bm", [128, 4, 8, 64], F32)
    mx = sb("f1_mx", [128, 4, 8, 8], F32)
    tnew = sb("f1_tnew", [128, 4, 64], F32)
    msk64 = sb("f1_msk", [128, 4, 64], mybir.dt.uint8)
    grid_a = sb("f1_grid", [128, 4, 64], F32)
    g8a = sb("f1_g8", [128, 16, 8], F32)
    negm = sb("f1_nm", [128, 16], F32)
    theta = sb("f1_th", [128, 16], F32)
    e8a = sb("f1_e8", [128, 16, 8], F32)
    Zb = sb("f1_Z", [128, 64], F32)
    rZ = sb("f1_rZ", [128, 64], F32)
    D = sb("f1_D", [128, 2, 4096], F32)
    W = sb("f1_W", [128, 4096], BF16)
    M = sb("f1_M", [128, 4096], BF16)
    S = sb("f1_S", [128, 4096], BF16)
    sP = sb("f1_sP", [128, 4096], BF16)
    Sfm = sb("f1_Sfm", [128, 32, TG], BF16)
    ue_t = sb("f1_ue", [128, 2, 512], BF16)
    es_t = sb("f1_es", [128, 2, TG], BF16)
    psim = psb("f1_psim", [128, 2, 512], F32)
    ppd = psb("f1_pd", [128, 2, 512], F32)
    pes = psb("f1_pes", [128, 4, 512], F32)
    if True:
        D3 = [D[:, sl].rearrange("p (i j) -> p i j", i=64)
              for sl in range(2)]
        s = Sched(p, "f1")

        kr = d["keysT"].rearrange("q h n k -> n (q h) k")

        def zkeys(ve):
            ve.memset(keys[:], 0.0)
            ve.memset(Zb[:], 1.0)
        s.add("vector", zkeys, label="zk")

        def loadw(gp):
            out = [
                gp.dma_start(out=dn[:], in_=d["dnT"].rearrange(
                    "(k p) e -> p k e", p=128)),
                gp.dma_start(out=wq[:], in_=d["wq"].rearrange(
                    "(k p) r -> p k r", p=128)),
            ]
            for rc in range(4):
                out.append(gp.dma_start(out=keys[0:64, rc, 0:64],
                                        in_=kr[:, 2 * rc]))
                out.append(gp.dma_start(out=keys[64:128, rc, 64:128],
                                        in_=kr[:, 2 * rc + 1]))
            return out
        s.add("gpsimd", loadw, waits=["zk"], label="lw", dma="dwt",
              n_dma=10)

        for gi in range(NTG):
            pa = gi % 2

            waits = ["lw"] if gi == 0 else [f"pdn{gi - 1}_3_7"]
            s.add("sync",
                  lambda sy, gi=gi, pa=pa: sy.dma_start(
                      out=xn_g[:, pa], in_=xnc3[:, :, gi * TG:(gi + 1) * TG]),
                  waits=waits, label=f"lxn{gi}", dma="dxi")

            for rc in range(4):
                waits = [f"lxn{gi}"]
                if gi > 0:
                    waits.append(f"slu{gi - 1}_3_{6 + rc % 2}")
                if rc >= 2:
                    waits.append(f"qe{gi}_{rc - 2}")

                def qr_f(te, gi=gi, pa=pa, rc=rc):
                    for k in range(NCH):
                        te.matmul(ppd[:, rc % 2],
                                  wq[:, k, rc * 128:(rc + 1) * 128],
                                  xn_g[:, pa, k], start=(k == 0),
                                  stop=(k == NCH - 1))
                s.add("tensor", qr_f, waits=waits, label=f"qr{gi}_{rc}")
                s.add("vector",
                      lambda ve, rc=rc: ve.tensor_copy(qret[:, rc],
                                                       ppd[:, rc % 2]),
                      waits=[f"qr{gi}_{rc}"], label=f"qe{gi}_{rc}")

            for tc in range(4):
                waits = [f"qe{gi}_3"]
                if gi > 0 or tc >= 2:
                    waits.append(f"st1{gi}_{tc - 2}" if tc >= 2 else
                                 f"st1{gi - 1}_{2 + tc}")

                def sim_f(te, gi=gi, tc=tc):
                    for rc in range(4):
                        te.matmul(psim[:, tc % 2, rc * 128:(rc + 1) * 128],
                                  qret[:, rc, tc * 128:(tc + 1) * 128],
                                  keys[:, rc], start=True, stop=True)
                s.add("tensor", sim_f, waits=waits, label=f"sim{gi}_{tc}")

                def st1_f(ve, gi=gi, tc=tc):
                    def sl(ph):
                        rc, off = ph // 2, (ph % 2) * 64
                        return psim[:, tc % 2, rc * 128 + off:
                                    rc * 128 + off + 64]
                    for half in range(2):
                        phs = range(half * 4, half * 4 + 4)
                        for ph in phs:
                            ve.max(out=mx[:, tc, ph], in_=sl(ph))
                        for ph in phs:
                            ve.match_replace(out=tnew[:, ph % 4],
                                             in_to_replace=mx[:, tc, ph],
                                             in_values=sl(ph),
                                             imm_value=NEGBIG)
                        for ph in phs:
                            ve.tensor_scalar(msk64[:, ph % 4],
                                             tnew[:, ph % 4], NEGBIG,
                                             None, OP.is_equal)
                        for ph in phs:
                            ve.select(Bm[:, tc, ph], msk64[:, ph % 4],
                                      sl(ph), g["negbig64"][:])
                s.add("vector", st1_f, waits=[f"sim{gi}_{tc}"],
                      label=f"st1{gi}_{tc}")

            def zg_f(ve, gi=gi):
                for q in range(4):
                    for j in range(4):
                        tc, h = q, j
                        ve.tensor_add(
                            grid_a[:, j].rearrange("p (a b) -> p a b",
                                                   a=8),
                            mx[:, tc, h, :].unsqueeze(2).broadcast_to(
                                [128, 8, 8]),
                            mx[:, tc, 4 + h, :].unsqueeze(1).broadcast_to(
                                [128, 8, 8]))
                    for j in range(4):
                        ve.max(out=g8a[:, q * 4 + j], in_=grid_a[:, j])
                for i in range(16):
                    ve.tensor_scalar(negm[:, i:i + 1], g8a[:, i, 0:1],
                                     -1.0, None, OP.mult)
                for i in range(16):
                    ve.tensor_copy(theta[:, i:i + 1], g8a[:, i, 7:8])
                for i in range(16):
                    ve.tensor_scalar(e8a[:, i], g8a[:, i],
                                     negm[:, i:i + 1], None, OP.add)
            s.add("vector", zg_f, waits=[f"st1{gi}_3"], label=f"zg{gi}")

            s.add("scalar",
                  lambda se: se.activation(
                      out=e8a[:].rearrange("p a b -> p (a b)"),
                      in_=e8a[:].rearrange("p a b -> p (a b)"),
                      func=AF.Exp, bias=0.0, scale=1.0),
                  waits=[f"zg{gi}"], label=f"ze{gi}")

            def zs_f(ve, gi=gi):
                for i in range(16):
                    ve.reduce_sum(Zb[:, i:i + 1], e8a[:, i],
                                  axis=mybir.AxisListType.X)
                ve.drain()
                emit_recip(ve, rZ[:], Zb[:], grid_a[:, 0], grid_a[:, 1],
                           g["XOR_EXP"])
            s.add("vector", zs_f, waits=[f"ze{gi}"], label=f"zs{gi}")

            for tc in range(4):
                def mk_dD(tc, h, waits):
                    def dD_f(ve, tc=tc, h=h):
                        ve.tensor_add(
                            D3[h % 2],
                            Bm[:, tc, h, :].unsqueeze(2).broadcast_to(
                                [128, 64, 64]),
                            Bm[:, tc, 4 + h, :].unsqueeze(1).broadcast_to(
                                [128, 64, 64]))
                    s.add("vector", dD_f, waits=waits,
                          label=f"dD{gi}_{tc}_{h}")

                def mk_sW(tc, h, waits):
                    def sW_f(se, tc=tc, h=h):
                        se.activation(out=W[:], in_=D[:, h % 2],
                                      func=AF.Exp,
                                      bias=negm[:, tc * 4 + h:
                                                tc * 4 + h + 1], scale=1.0)
                    s.add("scalar", sW_f, waits=waits,
                          label=f"sW{gi}_{tc}_{h}")

                def mk_dM(tc, h):
                    def dM_f(ve, tc=tc, h=h):
                        i = tc * 4 + h
                        ve.tensor_scalar(M[:], D[:, h % 2],
                                         theta[:, i:i + 1], None, OP.is_ge)
                        if h == 0:
                            ve.scalar_tensor_tensor(
                                S[:], W[:], rZ[:, i:i + 1], M[:],
                                OP.mult, OP.mult)
                        else:
                            ve.scalar_tensor_tensor(
                                M[:], W[:], rZ[:, i:i + 1], M[:],
                                OP.mult, OP.mult)
                            ve.tensor_add(S[:], S[:], M[:])
                    s.add("vector", dM_f, waits=[f"sW{gi}_{tc}_{h}"],
                          label=f"dm{gi}_{tc}_{h}")

                w0 = [f"st1{gi}_{tc}"]
                pw = []
                if not (gi == 0 and tc == 0):
                    w0.append(f"tr{gi}_{tc - 1}" if tc > 0 else
                              f"tr{gi - 1}_3")
                    pt, pg_ = (tc - 1, gi) if tc > 0 else (3, gi - 1)
                    pw = [f"dm{pg_}_{pt}_3"]
                mk_dD(tc, 0, w0)
                mk_dD(tc, 1, [])
                mk_sW(tc, 0, [f"dD{gi}_{tc}_0"] + pw)
                mk_dM(tc, 0)
                mk_sW(tc, 1, [f"dD{gi}_{tc}_1", f"dm{gi}_{tc}_0"])
                mk_dD(tc, 2, [])
                mk_dM(tc, 1)
                mk_sW(tc, 2, [f"dD{gi}_{tc}_2", f"dm{gi}_{tc}_1"])
                mk_dD(tc, 3, [])
                mk_dM(tc, 2)
                mk_sW(tc, 3, [f"dD{gi}_{tc}_3", f"dm{gi}_{tc}_2"])
                mk_dM(tc, 3)

                for eg in range(8):
                    waits = []
                    if eg >= 2:
                        waits.append(f"slu{gi}_{tc}_{eg - 2}")
                    elif not (gi == 0 and tc == 0):
                        waits.append(f"slu{gi}_{tc - 1}_{6 + eg}" if tc > 0
                                     else f"qe{gi}_{2 + eg}")
                    if eg < 2 and not (gi == 0 and tc == 0) and tc == 0:
                        pass
                    if eg == 0:
                        waits.append(f"lxn{gi}")

                    def pd_f(te, gi=gi, pa=pa, tc=tc, eg=eg):
                        for k in range(NCH):
                            te.matmul(ppd[:, eg % 2],
                                      xn_g[:, pa, k, tc * 128:(tc + 1) * 128],
                                      dn[:, k, eg * 512:(eg + 1) * 512],
                                      start=(k == 0), stop=(k == NCH - 1))
                    s.add("tensor", pd_f, waits=waits,
                          label=f"pdn{gi}_{tc}_{eg}")

                    wsl = [f"pdn{gi}_{tc}_{eg}"]
                    if eg == 0 and not (gi == 0 and tc == 0):
                        wsl.append(f"smul{gi}_{tc - 1}" if tc > 0 else
                                   f"smul{gi - 1}_3")
                    s.add("scalar",
                          lambda se, eg=eg: se.activation(
                              out=sP[:, eg * 512:(eg + 1) * 512],
                              in_=ppd[:, eg % 2], func=AF.Silu, bias=0.0,
                              scale=1.0),
                          waits=wsl, label=f"slu{gi}_{tc}_{eg}")

                s.add("vector",
                      lambda ve: ve.tensor_mul(S[:], S[:], sP[:]),
                      waits=[f"dm{gi}_{tc}_3", f"slu{gi}_{tc}_7"],
                      label=f"smul{gi}_{tc}")

                if "dbg_S" in dbg and gi == 0 and tc == 0:
                    s.add("sync", lambda sy: sy.dma_start(
                        out=dbg["dbg_S"], in_=S[:]),
                        waits=[f"smul{gi}_{tc}"], label="dS", dma="dtr")
                if "dbg_siluP" in dbg and gi == 0 and tc == 0:
                    s.add("sync", lambda sy: sy.dma_start(
                        out=dbg["dbg_siluP"], in_=sP[:]),
                        waits=[f"slu{gi}_{tc}_7"], label="dsP", dma="dtr")

                waits = [f"smul{gi}_{tc}"]
                if gi > 0 and tc == 0:
                    waits.append(f"upm{gi - 1}_1_31")

                def tr_f(sy, gi=gi, tc=tc):
                    out = []
                    for ec in range(32):
                        out.append(sy.dma_start(
                            out=Sfm[:, ec, tc * 128:(tc + 1) * 128],
                            in_=S[:, ec * 128:(ec + 1) * 128],
                            transpose=True))
                    return out
                s.add("sync", tr_f, waits=waits, label=f"tr{gi}_{tc}",
                      dma="dtr", n_dma=32)

            for pp in range(2):
                for ec in range(32):
                    waits = []
                    if ec >= 2:
                        waits.append(f"upm{gi}_{pp}_{ec - 2}")
                    elif pp == 1:
                        waits.append(f"upm{gi}_0_{30 + ec}")
                    elif gi > 0:
                        waits.append(f"upm{gi - 1}_1_{30 + ec}")
                    s.add("gpsimd",
                          lambda gp, gi=gi, pp=pp, ec=ec: gp.dma_start(
                              out=ue_t[:, ec % 2],
                              in_=ue3[:, ec, pp * 512:(pp + 1) * 512]),
                          waits=waits, label=f"lue{gi}_{pp}_{ec}", dma="dwt")

                    waits = [f"lue{gi}_{pp}_{ec}", f"tr{gi}_3"]
                    if ec == 0:
                        waits.append(f"ese{gi - 1}_7" if (pp == 0 and gi > 0)
                                     else (f"ese{gi}_3" if pp == 1 else
                                           "lw"))

                    def up_f(te, gi=gi, pp=pp, ec=ec):
                        for d4 in range(4):
                            te.matmul(pes[:, d4],
                                      ue_t[:, ec % 2,
                                           d4 * 128:(d4 + 1) * 128],
                                      Sfm[:, ec], start=(ec == 0),
                                      stop=(ec == 31))
                    s.add("tensor", up_f, waits=waits,
                          label=f"upm{gi}_{pp}_{ec}")

                for d4 in range(4):
                    dc = pp * 4 + d4
                    waits = [f"upm{gi}_{pp}_31"]
                    if dc >= 2:
                        waits.append(f"esd{gi}_{dc - 2}")
                    elif gi > 0:
                        waits.append(f"esd{gi - 1}_{6 + dc}")
                    s.add("vector",
                          lambda ve, dc=dc, d4=d4: ve.tensor_copy(
                              es_t[:, dc % 2], pes[:, d4]),
                          waits=waits, label=f"ese{gi}_{dc}")
                    s.add("sync",
                          lambda sy, gi=gi, dc=dc: sy.dma_start(
                              out=esc3[:, dc, gi * TG:(gi + 1) * TG],
                              in_=es_t[:, dc % 2]),
                          waits=[f"ese{gi}_{dc}"], label=f"esd{gi}_{dc}",
                          dma="dxn")

        def f1fin(sy):
            sy.wait_ge(p.sem["dxn"], p.cnt["dxn"])
        s.add("sync", f1fin, label="f1fin")
        s.emit()
    es_.close()



def emit_f2(p, d, g, dbg):
    nc = p.nc
    xnc3 = d["xn_c"].rearrange("(k p) t -> p k t", p=128)
    esc3 = d["es_c"].rearrange("(k p) t -> p k t", p=128)
    sc3 = d["s_c"].rearrange("(k p) t -> p k t", p=128)
    x3 = d["x"].rearrange("(k p) t -> p k t", p=128)
    o3 = d["out"].rearrange("(k p) t -> p k t", p=128)
    wg4 = d["wg"].rearrange("(k p) cq -> p k cq", p=128)
    wu4 = d["wu"].rearrange("(k p) cq -> p k cq", p=128)
    wd3 = d["wd"].rearrange("(c p) n -> p c n", p=128)
    bgu3 = d["bgu"].rearrange("(c p) i -> p c i", p=128)
    bd2 = d["bd"].rearrange("(k p) i -> p k i", p=128)

    NCG, CPG = 4, 8

    with (
        nc.sbuf_tensor("f2_acc", [128, NCH, T], BF16) as acc,
        nc.sbuf_tensor("f2_bd", [128, NCH, 1], F32) as bd_t,
    ):
        with (
            nc.sbuf_tensor("f2_wg", [128, 2, CPG, NCH, 128], BF16) as wgt_g,
            nc.sbuf_tensor("f2_wu", [128, 2, CPG, NCH, 128], BF16) as wgt_u,
            nc.sbuf_tensor("f2_wd", [128, 2, CPG, 1024], BF16) as wgt_d,
            nc.sbuf_tensor("f2_bgu", [128, 2, CPG, 2], F32) as bgu_t,
            nc.sbuf_tensor("f2_xn", [128, 2, NCH, TG], BF16) as xn_g,
            nc.sbuf_tensor("f2_tg", [128, 512], BF16) as t_g,
            nc.sbuf_tensor("f2_mid", [128, CPG, TG], BF16) as mid,
            nc.psum_tensor("f2_pg", [128, 2, 512], F32) as pg,
            nc.psum_tensor("f2_pu", [128, 2, 512], F32) as pu,
            nc.psum_tensor("f2_po", [128, 2, 512], F32) as po,
        ):
            s = Sched(p, "f2a")
            s.add("sync", lambda sy: sy.dma_start(out=acc[:], in_=esc3),
                  label="lacc", dma="dxi")
            s.add("gpsimd", lambda gp: gp.dma_start(out=bd_t[:], in_=bd2, allow_slow_non_contiguous=True),
                  label="lbd", dma="dwt")

            for cg in range(NCG):
                waits = []
                if cg >= 2:
                    waits.append(f"wdone{cg - 2}")

                def lw_f(gp, cg=cg):
                    pa = cg % 2
                    out = []
                    for k in range(NCH):
                        out.append(gp.dma_start(
                            out=wgt_g[:, pa, :, k, :],
                            in_=wg4[:, k, cg * 1024:(cg + 1) * 1024]))
                        out.append(gp.dma_start(
                            out=wgt_u[:, pa, :, k, :],
                            in_=wu4[:, k, cg * 1024:(cg + 1) * 1024]))
                    out.append(gp.dma_start(
                        out=wgt_d[:, pa],
                        in_=wd3[:, cg * CPG:(cg + 1) * CPG]))
                    out.append(gp.dma_start(
                        out=bgu_t[:, pa],
                        in_=bgu3[:, cg * CPG:(cg + 1) * CPG]))
                    return out
                s.add("gpsimd", lw_f, waits=waits, label=f"lwg{cg}",
                      dma="dwt", n_dma=18)

            for cg in range(NCG):
                pa = cg % 2
                for gi in range(NTG):
                    gpa = gi % 2
                    waits = []
                    if cg > 0 or gi >= 2:
                        waits.append(f"mid{cg}_{gi - 2}_7" if gi >= 2 else
                                     f"mid{cg - 1}_{6 + gi}_7")
                    s.add("sync",
                          lambda sy, cg=cg, gi=gi, gpa=gpa: sy.dma_start(
                              out=xn_g[:, gpa],
                              in_=xnc3[:, :, gi * TG:(gi + 1) * TG]),
                          waits=waits, label=f"f2x{cg}_{gi}", dma="dxi")

                    for ci in range(CPG):
                        waits = [f"f2x{cg}_{gi}", f"lwg{cg}"]
                        if not (cg == 0 and gi == 0 and ci < 2):
                            waits.append(
                                f"mid{cg}_{gi}_{ci - 2}" if ci >= 2 else
                                (f"mid{cg}_{gi - 1}_{6 + ci}" if gi > 0 else
                                 f"mid{cg - 1}_7_{6 + ci}"))

                        def ag_f(te, cg=cg, pa=pa, gi=gi, gpa=gpa, ci=ci):
                            for k in range(NCH):
                                te.matmul(pg[:, ci % 2],
                                          wgt_g[:, pa, ci, k],
                                          xn_g[:, gpa, k], start=(k == 0),
                                          stop=(k == NCH - 1))
                            for k in range(NCH):
                                te.matmul(pu[:, ci % 2],
                                          wgt_u[:, pa, ci, k],
                                          xn_g[:, gpa, k], start=(k == 0),
                                          stop=(k == NCH - 1))
                        s.add("tensor", ag_f, waits=waits,
                              label=f"agu{cg}_{gi}_{ci}")

                        wtg = [f"agu{cg}_{gi}_{ci}"]
                        if not (cg == 0 and gi == 0 and ci == 0):
                            wtg.append(
                                f"mid{cg}_{gi}_{ci - 1}" if ci > 0 else
                                (f"mid{cg}_{gi - 1}_7" if gi > 0 else
                                 f"mid{cg - 1}_7_7"))
                        s.add("scalar",
                              lambda se, pa=pa, ci=ci: se.activation(
                                  out=t_g[:], in_=pg[:, ci % 2],
                                  func=AF.Silu,
                                  bias=bgu_t[:, pa, ci, 0:1], scale=1.0),
                              waits=wtg, label=f"tg{cg}_{gi}_{ci}")

                        s.add("vector",
                              lambda ve, pa=pa, ci=ci:
                              ve.scalar_tensor_tensor(
                                  mid[:, ci], pu[:, ci % 2],
                                  bgu_t[:, pa, ci, 1:2], t_g[:],
                                  OP.add, OP.mult),
                              waits=[f"tg{cg}_{gi}_{ci}"],
                              label=f"mid{cg}_{gi}_{ci}")

                    for oc in range(8):
                        waits = [f"mid{cg}_{gi}_7"]
                        if not (cg == 0 and gi == 0 and oc < 2):
                            waits.append(
                                f"oacc{cg}_{gi}_{oc - 2}" if oc >= 2 else
                                (f"oacc{cg}_{gi - 1}_{6 + oc}" if gi > 0
                                 else f"oacc{cg - 1}_7_{6 + oc}"))

                        def wd_f(te, cg=cg, pa=pa, gi=gi, oc=oc):
                            for ci in range(CPG):
                                te.matmul(po[:, oc % 2],
                                          wgt_d[:, pa, ci,
                                                oc * 128:(oc + 1) * 128],
                                          mid[:, ci], start=(ci == 0),
                                          stop=(ci == CPG - 1))
                        s.add("tensor", wd_f, waits=waits,
                              label=f"wd{cg}_{gi}_{oc}")

                        wacc = [f"wd{cg}_{gi}_{oc}"]
                        if cg == 0 and gi == 0 and oc == 0:
                            wacc.append("lacc")

                        def acc_f(ve, gi=gi, oc=oc):
                            ve.tensor_add(acc[:, oc, gi * TG:(gi + 1) * TG],
                                          acc[:, oc, gi * TG:(gi + 1) * TG],
                                          po[:, oc % 2])
                        s.add("vector", acc_f, waits=wacc,
                              label=f"oacc{cg}_{gi}_{oc}")
                s.labels[f"wdone{cg}"] = s.labels[f"oacc{cg}_7_7"]
            s.emit()
        p.barrier()

        if "dbg_acc" in dbg:
            dump_fm(p, dbg["dbg_acc"], acc[:])

        with (
            nc.sbuf_tensor("f2_x", [128, 2, NCH, TG], F32) as x_t,
            nc.sbuf_tensor("f2_s", [128, 2, NCH, TG], BF16) as s_t,
            nc.sbuf_tensor("f2_gk", [128, TG], BF16) as gk,
            nc.sbuf_tensor("f2_o", [128, 2, NCH, TG], F32) as o_t,
        ):
            s = Sched(p, "f2b")
            for gi in range(NTG):
                gpa = gi % 2
                waits = []
                if gi >= 2:
                    waits.append(f"cmb{gi - 2}")
                    waits.append(f"co{gi - 2}")

                def lx_f(sy, gi=gi, gpa=gpa):
                    return [
                        sy.dma_start(out=x_t[:, gpa],
                                     in_=x3[:, :, gi * TG:(gi + 1) * TG]),
                        sy.dma_start(out=s_t[:, gpa],
                                     in_=sc3[:, :, gi * TG:(gi + 1) * TG]),
                    ]
                s.add("sync", lx_f, waits=waits, label=f"cx{gi}",
                      dma="dxi", n_dma=2)

                for k in range(NCH):
                    wgk = [f"cx{gi}"]
                    if k > 0:
                        wgk.append(f"cmk{gi}_{k - 1}")
                    elif gi > 0:
                        wgk.append(f"cmk{gi - 1}_7")
                    s.add("scalar",
                          lambda se, gi=gi, k=k: se.activation(
                              out=gk[:],
                              in_=acc[:, k, gi * TG:(gi + 1) * TG],
                              func=AF.Silu, bias=bd_t[:, k, 0:1],
                              scale=1.0),
                          waits=wgk, label=f"cgk{gi}_{k}")

                    def cm_f(ve, gi=gi, gpa=gpa, k=k):
                        ve.tensor_mul(gk[:], s_t[:, gpa, k], gk[:])
                        ve.tensor_add(o_t[:, gpa, k], x_t[:, gpa, k], gk[:])
                    s.add("vector", cm_f, waits=[f"cgk{gi}_{k}"],
                          label=f"cmk{gi}_{k}")
                s.labels[f"cmb{gi}"] = s.labels[f"cmk{gi}_7"]

                s.add("sync",
                      lambda sy, gi=gi, gpa=gpa: sy.dma_start(
                          out=o3[:, :, gi * TG:(gi + 1) * TG],
                          in_=o_t[:, gpa]),
                      waits=[f"cmb{gi}"], label=f"co{gi}", dma="dout")

            def f2fin(sy):
                sy.wait_ge(p.sem["dout"], p.cnt["dout"])
            s.add("sync", f2fin, label="f2fin")
            s.emit()



def stage_inputs(inputs):
    x = np.asarray(inputs["L0_attn_x"])
    filt = np.ascontiguousarray(np.concatenate(
        [np.asarray(inputs["L0_attn_h_filter"]).T,
         np.asarray(inputs["L0_attn_g_filter"]).T,
         np.asarray(inputs["L0_attn_h_synth"]).T,
         np.asarray(inputs["L0_attn_g_synth"]).T], axis=1)).astype(np.float32)
    gb = np.ascontiguousarray(np.stack(
        [np.asarray(inputs["L0_attn_norm_gamma"]),
         np.asarray(inputs["L0_attn_norm_beta"])], axis=1)).astype(np.float32)
    keysT = np.ascontiguousarray(
        np.transpose(np.asarray(inputs["L0_ffn_keys"]), (2, 0, 3, 1))
    ).astype(np.float32)
    dnT = np.ascontiguousarray(
        np.asarray(inputs["L0_ffn_down_embed"]).T).astype(np.float32)
    bgu = np.ascontiguousarray(np.stack(
        [np.asarray(inputs["L0_ffn_bg"]),
         np.asarray(inputs["L0_ffn_bu"])], axis=1)).astype(np.float32)
    shared = {
        "filt": filt, "gb": gb,
        "wq": np.ascontiguousarray(np.asarray(inputs["L0_ffn_Wq"],
                                              np.float32)),
        "keysT": keysT, "dnT": dnT,
        "ue": np.ascontiguousarray(np.asarray(inputs["L0_ffn_up_embed"],
                                              np.float32)),
        "wg": np.ascontiguousarray(np.asarray(inputs["L0_ffn_Wg"],
                                              np.float32)),
        "wu": np.ascontiguousarray(np.asarray(inputs["L0_ffn_Wu"],
                                              np.float32)),
        "wd": np.ascontiguousarray(np.asarray(inputs["L0_ffn_Wd"],
                                              np.float32)),
        "bgu": bgu,
        "bd": np.ascontiguousarray(
            np.asarray(inputs["L0_ffn_bd"], np.float32)[:, None]),
    }
    per_core = []
    for b in range(8):
        m = dict(shared)
        m["x"] = np.ascontiguousarray(x[b].T).astype(np.float32)
        per_core.append(m)
    return per_core


def kernel(**inputs):
    nc = build_program()
    in_maps = stage_inputs(inputs)
    res = run_bass_kernel_spmd(nc, in_maps, core_ids=list(range(8)))
    out = np.stack([np.asarray(r["out"]).T for r in res.results])
    return out.astype(np.float32)


# version 07
# speedup vs baseline: 1.0565x; 1.0116x over previous
import numpy as np
from contextlib import ExitStack
import concourse.bass as bass
import concourse.mybir as mybir
from concourse.bass_utils import run_bass_kernel_spmd

F32 = mybir.dt.float32
BF16 = mybir.dt.bfloat16
I32 = mybir.dt.int32
U32 = mybir.dt.uint32
AF = mybir.ActivationFunctionType
OP = mybir.AluOpType

C = 1024
T = 4096
NCH = 8
TG = 512
NTG = T // TG
LN_EPS = 1e-5
ATTN_EPS = 1e-6
NEGBIG = -1e30

RQ0, RQ1, RQ2 = 0.7765257306659091, -0.17066219872211436, 0.011377491746806564
RSQRT_C = 1.8369135901471134e-20

ENGINES = ("tensor", "vector", "scalar", "gpsimd", "sync")


class Prog:
    def __init__(self, nc):
        self.nc = nc
        self._ctx = []
        self.sem = {}
        self.cnt = {}
        self.bar_round = 0

    def keep(self, cm):
        v = cm.__enter__()
        self._ctx.append(cm)
        return v

    def add_sems(self, *names):
        for n in names:
            self.sem[n] = self.keep(self.nc.semaphore(n))
            self.cnt[n] = 0

    def close(self):
        for cm in reversed(self._ctx):
            cm.__exit__(None, None, None)

    def barrier(self):
        self.bar_round += 1
        target = self.bar_round * len(ENGINES)
        bar = self.sem["bar"]
        dummy = self.bar_dummy
        ones = self.bar_ones
        with self.nc.Block(f"bar{self.bar_round}") as blk:
            @blk.tensor
            def _(te):
                te.ldweights(ones[:, 0:1])
                te.drain().then_inc(bar, 1)
                te.wait_ge(bar, target)

            @blk.vector
            def _(ve):
                ve.memset(dummy[:, 0:1], 0.0)
                ve.drain().then_inc(bar, 1)
                ve.wait_ge(bar, target)

            @blk.scalar
            def _(se):
                se.activation(out=dummy[:, 1:2], in_=dummy[:, 1:2],
                              func=AF.Copy, bias=0.0, scale=1.0)
                se.drain().then_inc(bar, 1)
                se.wait_ge(bar, target)

            @blk.gpsimd
            def _(gp):
                gp.memset(dummy[:, 2:3], 0.0)
                gp.drain().then_inc(bar, 1)
                gp.wait_ge(bar, target)

            @blk.sync
            def _(sy):
                sy.sem_inc(bar, 1)
                sy.wait_ge(bar, target)
        self.cnt["bar"] = target


class _Rec:

    def __init__(self, eng):
        object.__setattr__(self, "_eng", eng)
        object.__setattr__(self, "last", None)

    def __getattr__(self, n):
        a = getattr(object.__getattribute__(self, "_eng"), n)
        if callable(a):
            def call(*args, **kw):
                r = a(*args, **kw)
                if hasattr(r, "then_inc"):
                    object.__setattr__(self, "last", r)
                return r
            return call
        return a


class Sched:

    def __init__(self, p, name):
        self.p = p
        self.name = name
        self.steps = {e: [] for e in ENGINES}
        self.labels = {}

    def add(self, eng, fn, waits=(), label=None, dma=None, n_dma=1):
        p = self.p
        sem_name = dma if dma else "e_" + eng
        inc = 16 * n_dma if dma else 1
        p.cnt[sem_name] += inc
        tgt = p.cnt[sem_name]
        if label is not None:
            assert label not in self.labels, label
            self.labels[label] = (sem_name, tgt)
        self.steps[eng].append((fn, tuple(waits), sem_name, tgt, dma, n_dma))

    def emit(self):
        p = self.p
        with p.nc.Block(self.name) as blk:
            for en in ENGINES:
                if not self.steps[en]:
                    continue

                def _mk(en):
                    def _(eng):
                        for fn, waits, sem_name, tgt, dma, n_dma in \
                                self.steps[en]:
                            for w in waits:
                                ws, wv = self.labels[w]
                                eng.wait_ge(p.sem[ws], wv)
                            if dma is not None:
                                instrs = fn(eng)
                                if not isinstance(instrs, (list, tuple)):
                                    instrs = [instrs]
                                assert len(instrs) == n_dma, \
                                    (self.name, n_dma, len(instrs))
                                for ins in instrs:
                                    ins.then_inc(p.sem[dma], 16)
                            elif en == "sync":
                                rec = _Rec(eng)
                                fn(rec)
                                last = object.__getattribute__(rec, "last")
                                if last is not None:
                                    last.then_inc(p.sem[sem_name], 1)
                                else:
                                    eng.sem_inc(p.sem[sem_name], 1)
                            else:
                                fn(eng)
                                eng.drain().then_inc(p.sem[sem_name], 1)
                    return _
                getattr(blk, en)(_mk(en))


def _width(ap):
    n = 1
    for d in ap.shape[1:]:
        n *= d
    return n


def emit_rsqrt(ve, out, v, z, a, xor_31, iters=3):
    gap = (lambda: ve.drain()) if _width(v) < 64 else (lambda: 0)
    ve.tensor_single_scalar(z.bitcast(I32), v.bitcast(I32), 1,
                            OP.logical_shift_right)
    gap()
    ve.tensor_scalar(z.bitcast(I32), z.bitcast(I32), xor_31, None,
                     OP.bitwise_xor)
    gap()
    ve.tensor_scalar_mul(z, z, RSQRT_C)
    for i in range(iters):
        last = i == iters - 1
        gap()
        ve.tensor_mul(a, z, z)
        gap()
        ve.tensor_mul(a, a, v)
        gap()
        ve.tensor_scalar(a, a, 3.0, -0.5, OP.subtract, OP.mult)
        gap()
        ve.tensor_mul(out if last else z, z, a)


def emit_recip(ve, out, x, r, u, xor_exp, iters=3):
    gap = (lambda: ve.drain()) if _width(x) < 64 else (lambda: 0)
    ve.tensor_scalar(r.bitcast(I32), x.bitcast(I32), xor_exp, None,
                     OP.bitwise_xor)
    gap()
    ve.tensor_mul(u, r, x)
    gap()
    ve.tensor_scalar(out, u, RQ2, RQ1, OP.mult, OP.add)
    gap()
    ve.tensor_mul(out, out, u)
    gap()
    ve.tensor_scalar_add(out, out, RQ0)
    gap()
    ve.tensor_mul(r, out, r)
    for i in range(iters):
        last = i == iters - 1
        gap()
        ve.tensor_mul(u, r, x)
        gap()
        ve.tensor_scalar(u, u, 2.0, -1.0, OP.subtract, OP.mult)
        gap()
        ve.tensor_mul(out if last else r, r, u)



def build_program(debug=(), phases=("ln", "attn", "f1", "f2")):
    nc = bass.Bass("TRN2", target_bir_lowering=False,
                   detect_race_conditions=False)
    p = Prog(nc)
    p.add_sems("bar", "e_tensor", "e_vector", "e_scalar", "e_gpsimd",
               "e_sync", "dxi", "dxn", "dwt", "dout", "dtr")

    d = {}

    def din(name, shape, dtype=F32):
        d[name] = nc.dram_tensor(name, shape, dtype, kind="ExternalInput").ap()

    din("x", [C, T])
    din("filt", [C, 16])
    din("gb", [C, 2])
    din("wq", [C, 512])
    din("keysT", [2, 4, 64, 64])
    din("dnT", [C, 4096])
    din("ue", [4096, C])
    din("wg", [C, 4096])
    din("wu", [C, 4096])
    din("wd", [4096, C])
    din("bgu", [4096, 2])
    din("bd", [C, 1])

    d["xn_c"] = nc.dram_tensor("xn_c", [C, T], BF16, kind="Internal").ap()
    d["s_c"] = nc.dram_tensor("s_c", [C, T], BF16, kind="Internal").ap()
    d["es_c"] = nc.dram_tensor("es_c", [C, T], BF16, kind="Internal").ap()
    d["out"] = nc.dram_tensor("out", [C, T], F32, kind="ExternalOutput").ap()

    dbg = {}
    for name, shape, dt_ in [
        ("dbg_xn", [C, T], BF16), ("dbg_h0a", [C, T // 2], BF16),
        ("dbg_g0", [C, T // 2], BF16), ("dbg_l1", [C, T // 4], BF16),
        ("dbg_g1", [C, T // 4], BF16), ("dbg_att", [C, T // 4], BF16),
        ("dbg_low1", [C, T // 2], BF16), ("dbg_s", [C, T], BF16),
        ("dbg_q", [C, T // 4], BF16), ("dbg_es", [C, T], BF16),
        ("dbg_kvz", [128, 16 * 65], BF16), ("dbg_attm", [128, 8 * 1024], BF16),
        ("dbg_dens", [128, 8 * 16], F32), ("dbg_qtm", [128, 8 * 1024], BF16),
        ("dbg_S", [128, 4096], BF16), ("dbg_siluP", [128, 4096], BF16),
        ("dbg_qret", [512, T], BF16), ("dbg_acc", [C, T], BF16),
        ("dbg_pnz", [128, 520], F32), ("dbg_rcp", [128, 128], F32),
    ]:
        if name in debug:
            dbg[name] = nc.dram_tensor(name, shape, dt_,
                                       kind="ExternalOutput").ap()

    p.bar_dummy = p.keep(nc.sbuf_tensor("bar_dummy", [1, 4], F32))
    consts = p.keep(nc.sbuf_tensor("consts", [128, 8], F32))
    ones128 = p.keep(nc.sbuf_tensor("ones128", [128, 128], BF16))
    ident = p.keep(nc.sbuf_tensor("ident", [128, 128], BF16))
    identi = p.keep(nc.sbuf_tensor("identi", [128, 128], I32))
    negbig64 = p.keep(nc.sbuf_tensor("negbig64", [128, 64], F32))
    onecol = p.keep(nc.sbuf_tensor("onecol", [128, 1], BF16))

    g = dict(consts=consts, ones128=ones128, ident=ident,
             negbig64=negbig64, onecol=onecol,
             XOR_EXP=consts.bitcast(I32)[:, 0:1],
             XOR_31=consts.bitcast(I32)[:, 1:2])

    with nc.Block("init") as blk:
        @blk.vector
        def _(ve):
            ve.memset(consts.bitcast(U32)[:, 0:1], 0x7F800000)
            ve.memset(consts.bitcast(U32)[:, 1:2], 0x7FFFFFFF)
            ve.memset(consts[:, 2:3], NEGBIG)
            ve.memset(ones128[:], 1.0)
            ve.memset(negbig64[:], NEGBIG)
            ve.memset(onecol[:], 1.0)
            ve.memset(p.bar_dummy[:], 0.0)
            ve.wait_ge(p.sem["e_gpsimd"], 1)
            ve.tensor_single_scalar(ident[:], identi[:], 0.0, OP.is_equal)

        @blk.gpsimd
        def _(gp):
            gp.iota(identi[:], [[1, 128]], channel_multiplier=-1,
                    allow_small_or_imprecise_dtypes=True
                    ).then_inc(p.sem["e_gpsimd"], 1)
    p.cnt["e_gpsimd"] += 1
    p.bar_ones = ones128
    p.barrier()

    xn_cm = nc.sbuf_tensor("xn_sb", [128, NCH * T], BF16)
    xn_sb = xn_cm.__enter__()
    xn3 = xn_sb[:].rearrange("p (k t) -> p k t", k=NCH)

    if "ln" in phases:
        emit_ln(p, d, g, xn3)
        p.barrier()
        if "dbg_xn" in dbg:
            dump_fm(p, dbg["dbg_xn"], xn3)

    if "attn" in phases:
        emit_attn(p, d, g, xn3, dbg)
        p.barrier()

    xn_cm.__exit__(None, None, None)

    if "f1" in phases:
        emit_f1(p, d, g, dbg)
        p.barrier()

    if "f2" in phases:
        emit_f2(p, d, g, dbg)
        p.barrier()

    p.close()
    return nc


def dump_fm(p, dram, sb3):
    nc = p.nc
    dr = dram.rearrange("(k p) t -> p k t", p=128)
    p.cnt["dtr"] += 16
    tgt = p.cnt["dtr"]
    p.dump_n = getattr(p, "dump_n", 0) + 1
    with nc.Block(f"dump{p.dump_n}") as blk:
        @blk.sync
        def _(sy):
            sy.dma_start(out=dr, in_=sb3).then_inc(p.sem["dtr"], 16)
            sy.wait_ge(p.sem["dtr"], tgt)
    p.barrier()


def dump_flat(p, dram, sb):
    nc = p.nc
    p.cnt["dtr"] += 16
    tgt = p.cnt["dtr"]
    p.dump_n = getattr(p, "dump_n", 0) + 1
    with nc.Block(f"dumpf{p.dump_n}") as blk:
        @blk.sync
        def _(sy):
            sy.dma_start(out=dram, in_=sb).then_inc(p.sem["dtr"], 16)
            sy.wait_ge(p.sem["dtr"], tgt)
    p.barrier()



def emit_ln(p, d, g, xn3):
    nc = p.nc
    x3 = d["x"].rearrange("(k p) t -> p k t", p=128)
    xnc3 = d["xn_c"].rearrange("(k p) t -> p k t", p=128)

    with (
        nc.sbuf_tensor("ln_x", [128, 2, NCH, TG], F32) as x_t,
        nc.sbuf_tensor("ln_xbf", [128, 2, NCH, TG], BF16) as xbf,
        nc.sbuf_tensor("ln_xsq", [128, 2, NCH, TG], BF16) as xsq,
        nc.sbuf_tensor("ln_mu", [128, 2, TG], F32) as mu_t,
        nc.sbuf_tensor("ln_rs", [128, TG], F32) as rs_t,
        nc.sbuf_tensor("ln_v", [128, TG], F32) as v_t,
        nc.sbuf_tensor("ln_s1", [128, TG], F32) as s1_t,
        nc.sbuf_tensor("ln_s2", [128, TG], F32) as s2_t,
        nc.psum_tensor("ln_ps", [128, 2, 2, TG], F32) as ps,
    ):
        s = Sched(p, "ln")
        for gi in range(NTG):
            pa = gi % 2

            def loadx(sy, gi=gi, pa=pa):
                return sy.dma_start(out=x_t[:, pa],
                                    in_=x3[:, :, gi * TG:(gi + 1) * TG])
            w = [f"ap{gi - 2}"] if gi >= 2 else []
            s.add("sync", loadx, waits=w, label=f"lx{gi}", dma="dxi")

            def xbf_f(ve, gi=gi, pa=pa):
                ve.tensor_copy(xbf[:, pa], x_t[:, pa])
            s.add("vector", xbf_f, waits=[f"lx{gi}"], label=f"xb{gi}")

            def xsq_f(se, gi=gi, pa=pa):
                se.activation(out=xsq[:, pa], in_=x_t[:, pa], func=AF.Square,
                              bias=0.0, scale=1.0)
            wq_ = [f"lx{gi}"] + ([f"st{gi - 2}"] if gi >= 2 else [])
            s.add("scalar", xsq_f, waits=wq_, label=f"xq{gi}")

            def stats(te, gi=gi, pa=pa):
                for k in range(NCH):
                    te.matmul(ps[:, pa, 0], g["ones128"][:], xbf[:, pa, k],
                              start=(k == 0), stop=(k == NCH - 1))
                for k in range(NCH):
                    te.matmul(ps[:, pa, 1], g["ones128"][:], xsq[:, pa, k],
                              start=(k == 0), stop=(k == NCH - 1))
            w = [f"xb{gi}", f"xq{gi}"] + ([f"mo{gi - 2}"] if gi >= 2 else [])
            s.add("tensor", stats, waits=w, label=f"st{gi}")

            def moments(ve, gi=gi, pa=pa):
                ve.tensor_scalar_mul(mu_t[:, pa], ps[:, pa, 0], 1.0 / C)
                ve.tensor_mul(v_t[:], mu_t[:, pa], mu_t[:, pa])
                ve.scalar_tensor_tensor(v_t[:], ps[:, pa, 1], 1.0 / C,
                                        v_t[:], OP.mult, OP.subtract)
                ve.tensor_scalar_add(v_t[:], v_t[:], LN_EPS)
            s.add("vector", moments, waits=[f"st{gi}"], label=f"mo{gi}")

            def apply(ve, gi=gi, pa=pa):
                emit_rsqrt(ve, rs_t[:], v_t[:], s1_t[:], s2_t[:],
                           g["XOR_31"])
                for k in range(NCH):
                    ve.tensor_sub(s1_t[:], x_t[:, pa, k], mu_t[:, pa])
                    ve.tensor_mul(xn3[:, k, gi * TG:(gi + 1) * TG],
                                  s1_t[:], rs_t[:])
            s.add("vector", apply, label=f"ap{gi}")

            def store(sy, gi=gi):
                return sy.dma_start(out=xnc3[:, :, gi * TG:(gi + 1) * TG],
                                    in_=xn3[:, :, gi * TG:(gi + 1) * TG])
            s.add("sync", store, waits=[f"ap{gi}"], label=f"sx{gi}",
                  dma="dxn")
        def lnfin(sy):
            sy.wait_ge(p.sem["dxn"], p.cnt["dxn"])
        s.add("sync", lnfin, label="lnfin")
        s.emit()



def emit_attn(p, d, g, xn3, dbg):
    nc = p.nc
    NT2, NT4 = T // 2, T // 4

    with (
        nc.sbuf_tensor("at_filt", [128, NCH, 16], F32) as filt,
        nc.sbuf_tensor("at_gb", [128, NCH, 2], F32) as gb,
        nc.sbuf_tensor("at_g0", [128, NCH, NT2], BF16) as g0,
        nc.sbuf_tensor("at_g1", [128, NCH, NT4], BF16) as g1,
        nc.sbuf_tensor("at_att", [128, NCH, NT4], BF16) as att,
    ):
        with nc.sbuf_tensor("at_l1", [128, NCH, NT4], BF16) as l1:
            with (
                nc.sbuf_tensor("at_h0a", [128, NCH, NT2], BF16) as h0a,
                nc.sbuf_tensor("at_scr", [128, NT2], F32) as scr,
            ):
                s = Sched(p, "attnA")

                def loadf(sy):
                    return [sy.dma_start(
                        out=filt[:], in_=d["filt"].rearrange(
                            "(k p) i -> p k i", p=128)),
                        sy.dma_start(out=gb[:], in_=d["gb"].rearrange(
                            "(k p) i -> p k i", p=128))]
                s.add("sync", loadf, label="lf", dma="dxi", n_dma=2)

                def conv(ve, out3, in3, fcol, Lin, k):
                    Lo = Lin // 2
                    zi = in3[:, k]
                    f = lambda i: filt[:, k, fcol + i:fcol + i + 1]
                    ve.tensor_scalar(scr[:, 0:Lo], zi[:, 0:Lin:2], f(1),
                                     None, OP.mult)
                    ve.scalar_tensor_tensor(scr[:, 0:Lo], zi[:, 1:Lin:2],
                                            f(2), scr[:, 0:Lo], OP.mult,
                                            OP.add)
                    ve.scalar_tensor_tensor(scr[:, 1:Lo], zi[:, 1:Lin - 2:2],
                                            f(0), scr[:, 1:Lo], OP.mult,
                                            OP.add)
                    ve.scalar_tensor_tensor(scr[:, 0:Lo - 1], zi[:, 2:Lin:2],
                                            f(3), scr[:, 0:Lo - 1], OP.mult,
                                            OP.add)
                    ve.tensor_copy(out3[:, k], scr[:, 0:Lo])

                for k in range(NCH):
                    s.add("vector",
                          lambda ve, k=k: conv(ve, h0a, xn3, 0, T, k),
                          waits=["lf"], label=f"h0a{k}")
                    s.add("vector",
                          lambda ve, k=k: conv(ve, g0, xn3, 4, T, k),
                          label=f"g0{k}")
                for k in range(NCH):
                    s.add("vector",
                          lambda ve, k=k: conv(ve, l1, h0a, 0, NT2, k),
                          label=f"l1{k}")
                    s.add("vector",
                          lambda ve, k=k: conv(ve, g1, h0a, 4, NT2, k),
                          label=f"g1{k}")
                s.emit()
                p.barrier()

                if "dbg_h0a" in dbg:
                    dump_fm(p, dbg["dbg_h0a"], h0a[:])
                if "dbg_g0" in dbg:
                    dump_fm(p, dbg["dbg_g0"], g0[:])
                if "dbg_l1" in dbg:
                    dump_fm(p, dbg["dbg_l1"], l1[:])
                if "dbg_g1" in dbg:
                    dump_fm(p, dbg["dbg_g1"], g1[:])

            emit_linattn(p, d, g, l1, att, dbg)
            p.barrier()
            if "dbg_att" in dbg:
                dump_fm(p, dbg["dbg_att"], att[:])

        emit_up_sln(p, d, g, xn3, att, g0, g1, gb, filt, dbg)
        p.barrier()
        if "dbg_s" in dbg:
            dump_fm(p, dbg["dbg_s"], xn3)


def emit_linattn(p, d, g, l1, att, dbg):
    nc = p.nc
    N = T // 4
    NTC = N // 128

    with (
        nc.sbuf_tensor("la_q", [128, NCH, N], BF16) as q,
        nc.sbuf_tensor("la_qtm", [128, NTC, C], BF16) as q_tm,
        nc.sbuf_tensor("la_vtm", [128, NTC, C], BF16) as v_tm,
        nc.sbuf_tensor("la_kvz", [128, 8, 130], BF16) as kvz,
        nc.sbuf_tensor("la_dens", [128, 8, 16], F32) as dens,
        nc.sbuf_tensor("la_rcp", [128, 8, 16], F32) as rcp,
        nc.sbuf_tensor("la_r1", [128, 8, 16], F32) as r1,
        nc.sbuf_tensor("la_r2", [128, 8, 16], F32) as r2,
        nc.sbuf_tensor("la_scA", [128, N], F32) as scA,
        nc.sbuf_tensor("la_scB", [128, N], F32) as scB,
        nc.psum_tensor("la_pst", [128, 2, 1024], BF16) as pst,
        nc.psum_tensor("la_kv", [64, 2, 512], F32) as pkv,
        nc.psum_tensor("la_nz", [128, 4, 512], F32) as pnz,
    ):
        att_tm = v_tm
        s = Sched(p, "linattn")

        def zkvz(ve):
            ve.memset(kvz[:], 0.0)
        s.add("vector", zkvz, label="zkvz")

        for k in range(NCH):
            s.add("vector",
                  lambda ve, k=k: ve.tensor_scalar_min(scA[:], l1[:, k], 0.0),
                  label=f"qm{k}", waits=[f"qq{k - 1}"] if k else [])
            s.add("scalar",
                  lambda se, k=k: se.activation(out=scB[:], in_=scA[:],
                                                func=AF.Exp, bias=0.0,
                                                scale=1.0),
                  waits=[f"qm{k}"], label=f"qe{k}")
            s.add("vector",
                  lambda ve, k=k: ve.scalar_tensor_tensor(
                      q[:, k], l1[:, k], 0.0, scB[:], OP.max, OP.add),
                  waits=[f"qe{k}"], label=f"qq{k}")

        w = 0
        for src, dst, nm in ((q, q_tm, "q"), (l1, v_tm, "v")):
            for tc in range(NTC):
                for k in range(NCH):
                    w += 1
                    waits = []
                    if nm == "q":
                        waits.append(f"qq{k}")
                    if w > 2:
                        waits.append(f"tev{w - 2}")
                    s.add("tensor",
                          lambda te, src=src, tc=tc, k=k, w=w: te.transpose(
                              pst[:, w % 2, 0:128],
                              src[:, k, tc * 128:(tc + 1) * 128],
                              g["ident"][:]),
                          waits=waits, label=f"tt{w}")
                    s.add("vector",
                          lambda ve, dst=dst, tc=tc, k=k, w=w: ve.tensor_copy(
                              dst[:, tc, k * 128:(k + 1) * 128],
                              pst[:, w % 2, 0:128]),
                          waits=[f"tt{w}"], label=f"tev{w}")
        ntr = w

        for h in range(16):
            pa = h % 2
            waits = [f"tev{ntr}"] if h == 0 else []
            if h >= 2:
                waits.append(f"kve{h - 2}")
            def kv_f(te, h=h, pa=pa):
                for tc in range(NTC):
                    te.matmul(pkv[:, pa, 0:64], q_tm[:, tc, h * 64:(h + 1) * 64],
                              v_tm[:, tc, h * 64:(h + 1) * 64],
                              start=(tc == 0), stop=(tc == NTC - 1))
                for tc in range(NTC):
                    te.matmul(pkv[:, pa, 64:65], q_tm[:, tc, h * 64:(h + 1) * 64],
                              g["onecol"][:], start=(tc == 0),
                              stop=(tc == NTC - 1))
            s.add("tensor", kv_f, waits=waits, label=f"kv{h}")

            def kve_f(ve, h=h, pa=pa):
                j, half = h // 2, h % 2
                ve.tensor_copy(kvz[half * 64:(half + 1) * 64, j,
                                   half * 65:half * 65 + 64],
                               pkv[:, pa, 0:64])
                ve.tensor_copy(kvz[half * 64:(half + 1) * 64, j,
                                   half * 65 + 64:half * 65 + 65],
                               pkv[:, pa, 64:65])
            s.add("vector", kve_f, waits=[f"kv{h}"], label=f"kve{h}")

        nn = 0
        for tc in range(NTC):
            for j in range(8):
                nn += 1
                waits = [f"kve{2 * j + 1}"]
                if nn > 4:
                    waits.append(f"nze{nn - 4}")

                def nz_f(te, tc=tc, j=j, nn=nn):
                    te.matmul(pnz[:, nn % 4, 0:130],
                              q[:, j, tc * 128:(tc + 1) * 128],
                              kvz[:, j], start=True, stop=True)
                s.add("tensor", nz_f, waits=waits, label=f"nz{nn}")

                def nze_f(ve, tc=tc, j=j, nn=nn):
                    ve.tensor_scalar_add(dens[:, tc, 2 * j:2 * j + 1],
                                         pnz[:, nn % 4, 64:65], ATTN_EPS)
                    ve.tensor_scalar_add(dens[:, tc, 2 * j + 1:2 * j + 2],
                                         pnz[:, nn % 4, 129:130], ATTN_EPS)
                    ve.tensor_scalar_add(att_tm[:, tc, 2 * j * 64:
                                                 (2 * j + 1) * 64],
                                         pnz[:, nn % 4, 0:64], 0.0)
                    ve.tensor_scalar_add(att_tm[:, tc, (2 * j + 1) * 64:
                                                 (2 * j + 2) * 64],
                                         pnz[:, nn % 4, 65:129], 0.0)
                    if "dbg_pnz" in dbg and nn in (1, 5):
                        ix = 0 if nn == 1 else 1
                        ve.tensor_copy(scA[:, ix * 260:ix * 260 + 130],
                                       pnz[:, nn % 4, 0:130])
                        ve.tensor_copy(
                            scA[:, ix * 260 + 130:ix * 260 + 258],
                            att_tm[:, tc, 2 * j * 64:(2 * j + 2) * 64])
                s.add("vector", nze_f, waits=[f"nz{nn}"],
                      label=f"nze{nn}")

        s.add("vector",
              lambda ve: emit_recip(ve, rcp[:], dens[:], r1[:], r2[:],
                                    g["XOR_EXP"]),
              label="rcpa")
        for tc in range(NTC):
            def norm_f(ve, tc=tc):
                for h in range(16):
                    ve.tensor_scalar(att_tm[:, tc, h * 64:(h + 1) * 64],
                                     att_tm[:, tc, h * 64:(h + 1) * 64],
                                     rcp[:, tc, h:h + 1], None, OP.mult)
            s.add("vector", norm_f, waits=["rcpa"], label=f"nrm{tc}")

        w2 = 0
        for tc in range(NTC):
            for k in range(NCH):
                w2 += 1
                waits = [f"nrm{tc}"]
                if w2 > 2:
                    waits.append(f"tae{w2 - 2}")
                s.add("tensor",
                      lambda te, tc=tc, k=k, w2=w2: te.transpose(
                          pst[:, w2 % 2, 0:128],
                          att_tm[:, tc, k * 128:(k + 1) * 128],
                          g["ident"][:]),
                      waits=waits, label=f"ta{w2}")
                s.add("vector",
                      lambda ve, tc=tc, k=k, w2=w2: ve.tensor_copy(
                          att[:, k, tc * 128:(tc + 1) * 128],
                          pst[:, w2 % 2, 0:128]),
                      waits=[f"ta{w2}"], label=f"tae{w2}")
        if "dbg_kvz" in dbg:
            s.add("sync", lambda sy: sy.dma_start(
                out=dbg["dbg_kvz"].rearrange("p (h e) -> p h e", h=8),
                in_=kvz[:]), waits=["kve15"], label="dkvz", dma="dtr")
        if "dbg_q" in dbg:
            s.add("sync", lambda sy: sy.dma_start(
                out=dbg["dbg_q"].rearrange("(k p) t -> p k t", p=128),
                in_=q[:]), waits=["qq7"], label="dq", dma="dtr")
        if "dbg_attm" in dbg:
            s.add("sync", lambda sy: sy.dma_start(
                out=dbg["dbg_attm"].rearrange("p (a b) -> p a b", a=8),
                in_=att_tm[:]), waits=[f"nrm{NTC - 1}"], label="dattm",
                dma="dtr")
        if "dbg_rcp" in dbg:
            s.add("sync", lambda sy: sy.dma_start(
                out=dbg["dbg_rcp"].rearrange("p (a b) -> p a b", a=8),
                in_=rcp[:]),
                waits=[f"nrm{NTC - 1}"], label="drcp", dma="dtr")
        if "dbg_pnz" in dbg:
            s.add("sync", lambda sy: sy.dma_start(
                out=dbg["dbg_pnz"], in_=scA[:, 0:520]),
                waits=["nze5"], label="dpnz", dma="dtr")
        if "dbg_dens" in dbg:
            s.add("sync", lambda sy: sy.dma_start(
                out=dbg["dbg_dens"].rearrange("p (a b) -> p a b", a=8),
                in_=dens[:]),
                waits=[f"nrm{NTC - 1}"], label="ddens", dma="dtr")
        if "dbg_qtm" in dbg:
            s.add("sync", lambda sy: sy.dma_start(
                out=dbg["dbg_qtm"].rearrange("p (a b) -> p a b", a=8),
                in_=q_tm[:]), waits=["tev128"], label="dqtm", dma="dtr")
        s.emit()


def emit_up_sln(p, d, g, xn3, att, g0, g1, gb, filt, dbg):
    nc = p.nc
    NT2, NT4 = T // 2, T // 4
    sc3 = d["s_c"].rearrange("(k p) t -> p k t", p=128)

    with (
        nc.sbuf_tensor("up_comb", [128, NCH, 10], F32) as comb,
        nc.sbuf_tensor("up_diag", [128, 2, 10, 128], BF16) as diag,
        nc.sbuf_tensor("up_low1", [128, NCH, NT2], BF16) as low1,
        nc.sbuf_tensor("up_rsq", [128, 2, NCH, TG], BF16) as rsq,
        nc.sbuf_tensor("up_mu", [128, 2, TG], F32) as mu_t,
        nc.sbuf_tensor("up_rs", [128, TG], F32) as rs_t,
        nc.sbuf_tensor("up_s1", [128, TG], F32) as s1_t,
        nc.sbuf_tensor("up_s2", [128, TG], F32) as s2_t,
        nc.sbuf_tensor("up_v", [128, TG], F32) as v_t,
        nc.psum_tensor("up_ps", [128, 4, 512], F32) as ps,
        nc.psum_tensor("up_st", [128, 2, 2, TG], F32) as pstat,
    ):
        s = Sched(p, "upconv")

        def combs(ve):
            for k in range(NCH):
                for base, col in ((8, 0), (12, 5)):
                    f = lambda i: filt[:, k, base + i:base + i + 1]
                    cs = lambda j: comb[:, k, col + j:col + j + 1]
                    ve.tensor_copy(cs(0), f(0))
                    ve.tensor_add(cs(1), f(1), f(2))
                    ve.tensor_copy(cs(2), f(3))
                    ve.tensor_add(cs(3), f(0), f(1))
                    ve.tensor_add(cs(4), f(2), f(3))
            ve.drain()
        s.add("vector", combs, label="cmb")

        def add_diag(tag, k, waits):
            def f(ve, k=k):
                for j in range(10):
                    ve.tensor_scalar(diag[:, k % 2, j], g["ident"][:],
                                     comb[:, k, j:j + 1], None, OP.mult)
            s.add("vector", f, waits=waits, label=tag)

        def up_pass(passname, vin1, vin2, Lh, nslots_used, evac_eng,
                    evac_fn, prev_diag_user):
            ngr = Lh // 512
            wv = 0
            for k in range(NCH):
                dwaits = ["cmb"]
                if k >= 2:
                    dwaits.append(f"{passname}mm{k - 2}_last")
                add_diag(f"{passname}dg{k}", k, dwaits)
                for ph in range(2):
                    for tg2 in range(ngr):
                        wv += 1
                        lo = tg2 * 512
                        slot = (wv - 1) % 4
                        waits = [f"{passname}dg{k}"]
                        if wv > 4:
                            waits.append(f"{passname}ev{wv - 4}")

                        def mm_f(te, k=k, ph=ph, lo=lo, slot=slot,
                                 vin1=vin1, vin2=vin2, Lh=Lh):
                            seq = []
                            for (src, base) in ((vin1, 0), (vin2, 5)):
                                if ph == 0:
                                    seq += [(src, base + 1, 0),
                                            (src, base + 0, -1),
                                            (src, base + 2, +1)]
                                else:
                                    seq += [(src, base + 3, 0),
                                            (src, base + 4, +1)]
                            for idx, (src, tap, off) in enumerate(seq):
                                a, b = lo + off, lo + off + 512
                                pslice = ps[:, slot, 0:512]
                                if a < 0:
                                    pslice = ps[:, slot, 1:512]
                                    a, b = 0, 511
                                elif b > Lh:
                                    pslice = ps[:, slot, 0:512 - (b - Lh)]
                                    b = Lh
                                te.matmul(pslice, diag[:, k % 2, tap],
                                          src[:, k, a:b],
                                          start=(idx == 0),
                                          stop=(idx == len(seq) - 1))
                        lbl = f"{passname}mm{wv}"
                        s.add("tensor", mm_f, waits=waits, label=lbl)
                        if ph == 1 and tg2 == ngr - 1:
                            s.labels[f"{passname}mm{k}_last"] = \
                                s.labels[lbl]
                        s.add(evac_eng,
                              lambda e, k=k, ph=ph, tg2=tg2, slot=slot:
                              evac_fn(e, k, ph, tg2, slot),
                              waits=[lbl], label=f"{passname}ev{wv}")
            return wv

        def ev1(se, k, ph, tg2, slot):
            se.activation(out=low1[:, k, tg2 * 1024 + ph:
                                   (tg2 + 1) * 1024:2],
                          in_=ps[:, slot, 0:512], func=AF.Copy, bias=0.0,
                          scale=1.0)
        n1 = up_pass("u1", att, g1, NT4, 4, "scalar", ev1, None)

        def ev0(ve, k, ph, tg2, slot):
            ve.tensor_add(xn3[:, k, tg2 * 1024 + ph:(tg2 + 1) * 1024:2],
                          ps[:, slot, 0:512],
                          xn3[:, k, tg2 * 1024 + ph:(tg2 + 1) * 1024:2])
        def up0():
            ngr = NT2 // 512
            wv = 0
            for k in range(NCH):
                dwaits = [f"u1ev{n1}"] if k == 0 else []
                dwaits.append("cmb")
                if k >= 2:
                    dwaits.append(f"u0mm{k - 2}_last")
                add_diag(f"u0dg{k}", k, dwaits)
                for ph in range(2):
                    for tg2 in range(ngr):
                        wv += 1
                        lo = tg2 * 512
                        slot = (wv - 1) % 4
                        waits = [f"u0dg{k}"]
                        if wv > 4:
                            waits.append(f"u0ev{wv - 4}")

                        def mm_f(te, k=k, ph=ph, lo=lo, slot=slot):
                            seq = []
                            for (src, base) in ((low1, 0), (g0, 5)):
                                if ph == 0:
                                    seq += [(src, base + 1, 0),
                                            (src, base + 0, -1),
                                            (src, base + 2, +1)]
                                else:
                                    seq += [(src, base + 3, 0),
                                            (src, base + 4, +1)]
                            for idx, (src, tap, off) in enumerate(seq):
                                a, b = lo + off, lo + off + 512
                                pslice = ps[:, slot, 0:512]
                                if a < 0:
                                    pslice = ps[:, slot, 1:512]
                                    a, b = 0, 511
                                elif b > NT2:
                                    pslice = ps[:, slot, 0:512 - (b - NT2)]
                                    b = NT2
                                te.matmul(pslice, diag[:, k % 2, tap],
                                          src[:, k, a:b],
                                          start=(idx == 0),
                                          stop=(idx == len(seq) - 1))
                        lbl = f"u0mm{wv}"
                        s.add("tensor", mm_f, waits=waits, label=lbl)
                        if ph == 1 and tg2 == ngr - 1:
                            s.labels[f"u0mm{k}_last"] = s.labels[lbl]
                        s.add("vector",
                              lambda ve, k=k, ph=ph, tg2=tg2, slot=slot:
                              ev0(ve, k, ph, tg2, slot),
                              waits=[lbl], label=f"u0ev{wv}")
            return wv
        n0 = up0()

        for gi in range(NTG):
            pa = gi % 2

            def sq_f(se, gi=gi, pa=pa):
                for k in range(NCH):
                    se.activation(out=rsq[:, pa, k],
                                  in_=xn3[:, k, gi * TG:(gi + 1) * TG],
                                  func=AF.Square, bias=0.0, scale=1.0)
            waits = [f"u0ev{n0}"] if gi == 0 else []
            if gi >= 2:
                waits.append(f"sap{gi - 2}")
            s.add("scalar", sq_f, waits=waits, label=f"ssq{gi}")

            def sstat(te, gi=gi, pa=pa):
                for k in range(NCH):
                    te.matmul(pstat[:, pa, 0], g["ones128"][:],
                              xn3[:, k, gi * TG:(gi + 1) * TG],
                              start=(k == 0), stop=(k == NCH - 1))
                for k in range(NCH):
                    te.matmul(pstat[:, pa, 1], g["ones128"][:],
                              rsq[:, pa, k], start=(k == 0),
                              stop=(k == NCH - 1))
            w = [f"ssq{gi}"] + ([f"smo{gi - 2}"] if gi >= 2 else [])
            s.add("tensor", sstat, waits=w, label=f"sst{gi}")

            def smom(ve, gi=gi, pa=pa):
                ve.tensor_scalar_mul(mu_t[:, pa], pstat[:, pa, 0], 1.0 / C)
                ve.tensor_mul(v_t[:], mu_t[:, pa], mu_t[:, pa])
                ve.scalar_tensor_tensor(v_t[:], pstat[:, pa, 1], 1.0 / C,
                                        v_t[:], OP.mult, OP.subtract)
                ve.tensor_scalar_add(v_t[:], v_t[:], LN_EPS)
            s.add("vector", smom, waits=[f"sst{gi}"], label=f"smo{gi}")

            def sap(ve, gi=gi, pa=pa):
                emit_rsqrt(ve, rs_t[:], v_t[:], s1_t[:], s2_t[:],
                           g["XOR_31"])
                for k in range(NCH):
                    ve.tensor_sub(s1_t[:], xn3[:, k, gi * TG:(gi + 1) * TG],
                                  mu_t[:, pa])
                    ve.tensor_mul(s1_t[:], s1_t[:], rs_t[:])
                    ve.tensor_scalar(xn3[:, k, gi * TG:(gi + 1) * TG],
                                     s1_t[:], gb[:, k, 0:1], gb[:, k, 1:2],
                                     OP.mult, OP.add)
            s.add("vector", sap, label=f"sap{gi}")

            s.add("sync",
                  lambda sy, gi=gi: sy.dma_start(
                      out=sc3[:, :, gi * TG:(gi + 1) * TG],
                      in_=xn3[:, :, gi * TG:(gi + 1) * TG]),
                  waits=[f"sap{gi}"], label=f"ssx{gi}", dma="dxn")
        s.add("sync", lambda sy: sy.wait_ge(p.sem["dxn"], p.cnt["dxn"]),
              label="upfin")
        if "dbg_low1" in dbg:
            s.add("sync", lambda sy: sy.dma_start(
                out=dbg["dbg_low1"].rearrange("(k p) t -> p k t", p=128),
                in_=low1[:]), waits=[f"u1ev{n1}"], label="dlow1", dma="dtr")
        s.emit()



def emit_f1(p, d, g, dbg):
    nc = p.nc
    xnc3 = d["xn_c"].rearrange("(k p) t -> p k t", p=128)
    esc3 = d["es_c"].rearrange("(k p) t -> p k t", p=128)
    ue3 = d["ue"].rearrange("(ec p) dd -> p ec dd", p=128)

    es_ = ExitStack()
    def sb(name, shape, dt_):
        return es_.enter_context(nc.sbuf_tensor(name, shape, dt_))
    def psb(name, shape, dt_):
        return es_.enter_context(nc.psum_tensor(name, shape, dt_))
    dn = sb("f1_dn", [128, NCH, 4096], BF16)
    keys = sb("f1_keys", [128, 4, 128], BF16)
    wq = sb("f1_wq", [128, NCH, 512], BF16)
    xn_g = sb("f1_xn", [128, 2, NCH, TG], BF16)
    qret = sb("f1_qret", [128, 4, TG], BF16)
    Bm = sb("f1_Bm", [128, 4, 8, 64], F32)
    mx = sb("f1_mx", [128, 4, 8, 8], F32)
    tnew = sb("f1_tnew", [128, 4, 64], F32)
    msk64 = sb("f1_msk", [128, 4, 64], mybir.dt.uint8)
    grid_a = sb("f1_grid", [128, 4, 64], F32)
    g8a = sb("f1_g8", [128, 16, 8], F32)
    negm = sb("f1_nm", [128, 16], F32)
    theta = sb("f1_th", [128, 16], F32)
    e8a = sb("f1_e8", [128, 16, 8], F32)
    Zb = sb("f1_Z", [128, 64], F32)
    rZ = sb("f1_rZ", [128, 64], F32)
    D = sb("f1_D", [128, 2, 4096], F32)
    W = sb("f1_W", [128, 4096], BF16)
    M = sb("f1_M", [128, 4096], BF16)
    S = sb("f1_S", [128, 4096], BF16)
    sP = sb("f1_sP", [128, 4096], BF16)
    Sfm = sb("f1_Sfm", [128, 32, TG], BF16)
    ue_t = sb("f1_ue", [128, 2, 512], BF16)
    es_t = sb("f1_es", [128, 2, TG], BF16)
    psim = psb("f1_psim", [128, 2, 512], F32)
    ppd = psb("f1_pd", [128, 2, 512], F32)
    pes = psb("f1_pes", [128, 4, 512], F32)
    if True:
        D3 = [D[:, sl].rearrange("p (i j) -> p i j", i=64)
              for sl in range(2)]
        s = Sched(p, "f1")

        kr = d["keysT"].rearrange("q h n k -> n (q h) k")

        def zkeys(ve):
            ve.memset(keys[:], 0.0)
            ve.memset(Zb[:], 1.0)
        s.add("vector", zkeys, label="zk")

        def loadw(gp):
            out = [
                gp.dma_start(out=dn[:], in_=d["dnT"].rearrange(
                    "(k p) e -> p k e", p=128)),
                gp.dma_start(out=wq[:], in_=d["wq"].rearrange(
                    "(k p) r -> p k r", p=128)),
            ]
            for rc in range(4):
                out.append(gp.dma_start(out=keys[0:64, rc, 0:64],
                                        in_=kr[:, 2 * rc]))
                out.append(gp.dma_start(out=keys[64:128, rc, 64:128],
                                        in_=kr[:, 2 * rc + 1]))
            return out
        s.add("gpsimd", loadw, waits=["zk"], label="lw", dma="dwt",
              n_dma=10)

        for gi in range(NTG):
            pa = gi % 2

            waits = ["lw"] if gi == 0 else [f"pdn{gi - 1}_3_7"]
            s.add("sync",
                  lambda sy, gi=gi, pa=pa: sy.dma_start(
                      out=xn_g[:, pa], in_=xnc3[:, :, gi * TG:(gi + 1) * TG]),
                  waits=waits, label=f"lxn{gi}", dma="dxi")

            for rc in range(4):
                waits = [f"lxn{gi}"]
                if gi > 0:
                    waits.append(f"slu{gi - 1}_3_{6 + rc % 2}")
                if rc >= 2:
                    waits.append(f"qe{gi}_{rc - 2}")

                def qr_f(te, gi=gi, pa=pa, rc=rc):
                    for k in range(NCH):
                        te.matmul(ppd[:, rc % 2],
                                  wq[:, k, rc * 128:(rc + 1) * 128],
                                  xn_g[:, pa, k], start=(k == 0),
                                  stop=(k == NCH - 1))
                s.add("tensor", qr_f, waits=waits, label=f"qr{gi}_{rc}")
                s.add("vector",
                      lambda ve, rc=rc: ve.tensor_copy(qret[:, rc],
                                                       ppd[:, rc % 2]),
                      waits=[f"qr{gi}_{rc}"], label=f"qe{gi}_{rc}")

            for tc in range(4):
                waits = [f"qe{gi}_3"]
                if gi > 0 or tc >= 2:
                    waits.append(f"st1{gi}_{tc - 2}" if tc >= 2 else
                                 f"st1{gi - 1}_{2 + tc}")

                def sim_f(te, gi=gi, tc=tc):
                    for rc in range(4):
                        te.matmul(psim[:, tc % 2, rc * 128:(rc + 1) * 128],
                                  qret[:, rc, tc * 128:(tc + 1) * 128],
                                  keys[:, rc], start=True, stop=True)
                s.add("tensor", sim_f, waits=waits, label=f"sim{gi}_{tc}")

                def st1_f(ve, gi=gi, tc=tc):
                    def sl(ph):
                        rc, off = ph // 2, (ph % 2) * 64
                        return psim[:, tc % 2, rc * 128 + off:
                                    rc * 128 + off + 64]
                    for half in range(2):
                        phs = range(half * 4, half * 4 + 4)
                        for ph in phs:
                            ve.max(out=mx[:, tc, ph], in_=sl(ph))
                        for ph in phs:
                            ve.match_replace(out=tnew[:, ph % 4],
                                             in_to_replace=mx[:, tc, ph],
                                             in_values=sl(ph),
                                             imm_value=NEGBIG)
                        for ph in phs:
                            ve.tensor_scalar(msk64[:, ph % 4],
                                             tnew[:, ph % 4], NEGBIG,
                                             None, OP.is_equal)
                        for ph in phs:
                            ve.select(Bm[:, tc, ph], msk64[:, ph % 4],
                                      sl(ph), g["negbig64"][:])
                s.add("vector", st1_f, waits=[f"sim{gi}_{tc}"],
                      label=f"st1{gi}_{tc}")

            if gi > 0:
                emit_up(gi - 1)

            def zg_f(ve, gi=gi):
                for q in range(4):
                    for j in range(4):
                        tc, h = q, j
                        ve.tensor_add(
                            grid_a[:, j].rearrange("p (a b) -> p a b",
                                                   a=8),
                            mx[:, tc, h, :].unsqueeze(2).broadcast_to(
                                [128, 8, 8]),
                            mx[:, tc, 4 + h, :].unsqueeze(1).broadcast_to(
                                [128, 8, 8]))
                    for j in range(4):
                        ve.max(out=g8a[:, q * 4 + j], in_=grid_a[:, j])
                for i in range(16):
                    ve.tensor_scalar(negm[:, i:i + 1], g8a[:, i, 0:1],
                                     -1.0, None, OP.mult)
                for i in range(16):
                    ve.tensor_copy(theta[:, i:i + 1], g8a[:, i, 7:8])
                for i in range(16):
                    ve.tensor_scalar(e8a[:, i], g8a[:, i],
                                     negm[:, i:i + 1], None, OP.add)
            s.add("vector", zg_f, waits=[f"st1{gi}_3"], label=f"zg{gi}")

            s.add("scalar",
                  lambda se: se.activation(
                      out=e8a[:].rearrange("p a b -> p (a b)"),
                      in_=e8a[:].rearrange("p a b -> p (a b)"),
                      func=AF.Exp, bias=0.0, scale=1.0),
                  waits=[f"zg{gi}"], label=f"ze{gi}")

            def zs_f(ve, gi=gi):
                for i in range(16):
                    ve.reduce_sum(Zb[:, i:i + 1], e8a[:, i],
                                  axis=mybir.AxisListType.X)
                ve.drain()
                emit_recip(ve, rZ[:], Zb[:], grid_a[:, 0], grid_a[:, 1],
                           g["XOR_EXP"])
            s.add("vector", zs_f, waits=[f"ze{gi}"], label=f"zs{gi}")

            for tc in range(4):
                def mk_dD(tc, h, waits):
                    def dD_f(ve, tc=tc, h=h):
                        ve.tensor_add(
                            D3[h % 2],
                            Bm[:, tc, h, :].unsqueeze(2).broadcast_to(
                                [128, 64, 64]),
                            Bm[:, tc, 4 + h, :].unsqueeze(1).broadcast_to(
                                [128, 64, 64]))
                    s.add("vector", dD_f, waits=waits,
                          label=f"dD{gi}_{tc}_{h}")

                def mk_sW(tc, h, waits):
                    def sW_f(se, tc=tc, h=h):
                        se.activation(out=W[:], in_=D[:, h % 2],
                                      func=AF.Exp,
                                      bias=negm[:, tc * 4 + h:
                                                tc * 4 + h + 1], scale=1.0)
                    s.add("scalar", sW_f, waits=waits,
                          label=f"sW{gi}_{tc}_{h}")

                def mk_dM(tc, h):
                    def dM_f(ve, tc=tc, h=h):
                        i = tc * 4 + h
                        ve.tensor_scalar(M[:], D[:, h % 2],
                                         theta[:, i:i + 1], None, OP.is_ge)
                        if h == 0:
                            ve.scalar_tensor_tensor(
                                S[:], W[:], rZ[:, i:i + 1], M[:],
                                OP.mult, OP.mult)
                        else:
                            ve.scalar_tensor_tensor(
                                M[:], W[:], rZ[:, i:i + 1], M[:],
                                OP.mult, OP.mult)
                            ve.tensor_add(S[:], S[:], M[:])
                    s.add("vector", dM_f, waits=[f"sW{gi}_{tc}_{h}"],
                          label=f"dm{gi}_{tc}_{h}")

                w0 = [f"st1{gi}_{tc}"]
                pw = []
                if not (gi == 0 and tc == 0):
                    w0.append(f"tr{gi}_{tc - 1}" if tc > 0 else
                              f"tr{gi - 1}_3")
                    pt, pg_ = (tc - 1, gi) if tc > 0 else (3, gi - 1)
                    pw = [f"dm{pg_}_{pt}_3"]
                mk_dD(tc, 0, w0)
                mk_dD(tc, 1, [])
                mk_sW(tc, 0, [f"dD{gi}_{tc}_0"] + pw)
                mk_dM(tc, 0)
                mk_sW(tc, 1, [f"dD{gi}_{tc}_1", f"dm{gi}_{tc}_0"])
                mk_dD(tc, 2, [])
                mk_dM(tc, 1)
                mk_sW(tc, 2, [f"dD{gi}_{tc}_2", f"dm{gi}_{tc}_1"])
                mk_dD(tc, 3, [])
                mk_dM(tc, 2)
                mk_sW(tc, 3, [f"dD{gi}_{tc}_3", f"dm{gi}_{tc}_2"])
                mk_dM(tc, 3)

                for eg in range(8):
                    waits = []
                    if eg >= 2:
                        waits.append(f"slu{gi}_{tc}_{eg - 2}")
                    elif not (gi == 0 and tc == 0):
                        waits.append(f"slu{gi}_{tc - 1}_{6 + eg}" if tc > 0
                                     else f"qe{gi}_{2 + eg}")
                    if eg < 2 and not (gi == 0 and tc == 0) and tc == 0:
                        pass
                    if eg == 0:
                        waits.append(f"lxn{gi}")

                    def pd_f(te, gi=gi, pa=pa, tc=tc, eg=eg):
                        for k in range(NCH):
                            te.matmul(ppd[:, eg % 2],
                                      xn_g[:, pa, k, tc * 128:(tc + 1) * 128],
                                      dn[:, k, eg * 512:(eg + 1) * 512],
                                      start=(k == 0), stop=(k == NCH - 1))
                    s.add("tensor", pd_f, waits=waits,
                          label=f"pdn{gi}_{tc}_{eg}")

                    wsl = [f"pdn{gi}_{tc}_{eg}"]
                    if eg == 0 and not (gi == 0 and tc == 0):
                        wsl.append(f"smul{gi}_{tc - 1}" if tc > 0 else
                                   f"smul{gi - 1}_3")
                    s.add("scalar",
                          lambda se, eg=eg: se.activation(
                              out=sP[:, eg * 512:(eg + 1) * 512],
                              in_=ppd[:, eg % 2], func=AF.Silu, bias=0.0,
                              scale=1.0),
                          waits=wsl, label=f"slu{gi}_{tc}_{eg}")

                s.add("vector",
                      lambda ve: ve.tensor_mul(S[:], S[:], sP[:]),
                      waits=[f"dm{gi}_{tc}_3", f"slu{gi}_{tc}_7"],
                      label=f"smul{gi}_{tc}")

                if "dbg_S" in dbg and gi == 0 and tc == 0:
                    s.add("sync", lambda sy: sy.dma_start(
                        out=dbg["dbg_S"], in_=S[:]),
                        waits=[f"smul{gi}_{tc}"], label="dS", dma="dtr")
                if "dbg_siluP" in dbg and gi == 0 and tc == 0:
                    s.add("sync", lambda sy: sy.dma_start(
                        out=dbg["dbg_siluP"], in_=sP[:]),
                        waits=[f"slu{gi}_{tc}_7"], label="dsP", dma="dtr")

                waits = [f"smul{gi}_{tc}"]
                if gi > 0 and tc == 0:
                    waits.append(f"upm{gi - 1}_1_31")

                def tr_f(sy, gi=gi, tc=tc):
                    out = []
                    for ec in range(32):
                        out.append(sy.dma_start(
                            out=Sfm[:, ec, tc * 128:(tc + 1) * 128],
                            in_=S[:, ec * 128:(ec + 1) * 128],
                            transpose=True))
                    return out
                s.add("sync", tr_f, waits=waits, label=f"tr{gi}_{tc}",
                      dma="dtr", n_dma=32)

            def emit_up(gi):
                for pp in range(2):
                    for ec in range(32):
                        waits = []
                        if ec >= 2:
                            waits.append(f"upm{gi}_{pp}_{ec - 2}")
                        elif pp == 1:
                            waits.append(f"upm{gi}_0_{30 + ec}")
                        elif gi > 0:
                            waits.append(f"upm{gi - 1}_1_{30 + ec}")
                        s.add("gpsimd",
                              lambda gp, gi=gi, pp=pp, ec=ec: gp.dma_start(
                                  out=ue_t[:, ec % 2],
                                  in_=ue3[:, ec, pp * 512:(pp + 1) * 512]),
                              waits=waits, label=f"lue{gi}_{pp}_{ec}", dma="dwt")

                        waits = [f"lue{gi}_{pp}_{ec}", f"tr{gi}_3"]
                        if ec == 0:
                            waits.append(f"ese{gi - 1}_7" if (pp == 0 and gi > 0)
                                         else (f"ese{gi}_3" if pp == 1 else
                                               "lw"))

                        def up_f(te, gi=gi, pp=pp, ec=ec):
                            for d4 in range(4):
                                te.matmul(pes[:, d4],
                                          ue_t[:, ec % 2,
                                               d4 * 128:(d4 + 1) * 128],
                                          Sfm[:, ec], start=(ec == 0),
                                          stop=(ec == 31))
                        s.add("tensor", up_f, waits=waits,
                              label=f"upm{gi}_{pp}_{ec}")

                    for d4 in range(4):
                        dc = pp * 4 + d4
                        waits = [f"upm{gi}_{pp}_31"]
                        if dc >= 2:
                            waits.append(f"esd{gi}_{dc - 2}")
                        elif gi > 0:
                            waits.append(f"esd{gi - 1}_{6 + dc}")
                        s.add("vector",
                              lambda ve, dc=dc, d4=d4: ve.tensor_copy(
                                  es_t[:, dc % 2], pes[:, d4]),
                              waits=waits, label=f"ese{gi}_{dc}")
                        s.add("sync",
                              lambda sy, gi=gi, dc=dc: sy.dma_start(
                                  out=esc3[:, dc, gi * TG:(gi + 1) * TG],
                                  in_=es_t[:, dc % 2]),
                              waits=[f"ese{gi}_{dc}"], label=f"esd{gi}_{dc}",
                              dma="dxn")

        emit_up(NTG - 1)

        def f1fin(sy):
            sy.wait_ge(p.sem["dxn"], p.cnt["dxn"])
        s.add("sync", f1fin, label="f1fin")
        s.emit()
    es_.close()



def emit_f2(p, d, g, dbg):
    nc = p.nc
    xnc3 = d["xn_c"].rearrange("(k p) t -> p k t", p=128)
    esc3 = d["es_c"].rearrange("(k p) t -> p k t", p=128)
    sc3 = d["s_c"].rearrange("(k p) t -> p k t", p=128)
    x3 = d["x"].rearrange("(k p) t -> p k t", p=128)
    o3 = d["out"].rearrange("(k p) t -> p k t", p=128)
    wg4 = d["wg"].rearrange("(k p) cq -> p k cq", p=128)
    wu4 = d["wu"].rearrange("(k p) cq -> p k cq", p=128)
    wd3 = d["wd"].rearrange("(c p) n -> p c n", p=128)
    bgu3 = d["bgu"].rearrange("(c p) i -> p c i", p=128)
    bd2 = d["bd"].rearrange("(k p) i -> p k i", p=128)

    NCG, CPG = 4, 8

    with (
        nc.sbuf_tensor("f2_acc", [128, NCH, T], BF16) as acc,
        nc.sbuf_tensor("f2_bd", [128, NCH, 1], F32) as bd_t,
    ):
        with (
            nc.sbuf_tensor("f2_wg", [128, 2, CPG, NCH, 128], BF16) as wgt_g,
            nc.sbuf_tensor("f2_wu", [128, 2, CPG, NCH, 128], BF16) as wgt_u,
            nc.sbuf_tensor("f2_wd", [128, 2, CPG, 1024], BF16) as wgt_d,
            nc.sbuf_tensor("f2_bgu", [128, 2, CPG, 2], F32) as bgu_t,
            nc.sbuf_tensor("f2_xn", [128, 2, NCH, TG], BF16) as xn_g,
            nc.sbuf_tensor("f2_tg", [128, 512], BF16) as t_g,
            nc.sbuf_tensor("f2_mid", [128, CPG, TG], BF16) as mid,
            nc.psum_tensor("f2_pg", [128, 2, 512], F32) as pg,
            nc.psum_tensor("f2_pu", [128, 2, 512], F32) as pu,
            nc.psum_tensor("f2_po", [128, 2, 512], F32) as po,
        ):
            s = Sched(p, "f2a")
            s.add("sync", lambda sy: sy.dma_start(out=acc[:], in_=esc3),
                  label="lacc", dma="dxi")
            s.add("gpsimd", lambda gp: gp.dma_start(out=bd_t[:], in_=bd2, allow_slow_non_contiguous=True),
                  label="lbd", dma="dwt")

            for cg in range(NCG):
                waits = []
                if cg >= 2:
                    waits.append(f"wdone{cg - 2}")

                def lw_f(gp, cg=cg):
                    pa = cg % 2
                    out = []
                    for k in range(NCH):
                        out.append(gp.dma_start(
                            out=wgt_g[:, pa, :, k, :],
                            in_=wg4[:, k, cg * 1024:(cg + 1) * 1024]))
                        out.append(gp.dma_start(
                            out=wgt_u[:, pa, :, k, :],
                            in_=wu4[:, k, cg * 1024:(cg + 1) * 1024]))
                    out.append(gp.dma_start(
                        out=wgt_d[:, pa],
                        in_=wd3[:, cg * CPG:(cg + 1) * CPG]))
                    out.append(gp.dma_start(
                        out=bgu_t[:, pa],
                        in_=bgu3[:, cg * CPG:(cg + 1) * CPG]))
                    return out
                s.add("gpsimd", lw_f, waits=waits, label=f"lwg{cg}",
                      dma="dwt", n_dma=18)

            for cg in range(NCG):
                pa = cg % 2
                for gi in range(NTG):
                    gpa = gi % 2
                    waits = []
                    if cg > 0 or gi >= 2:
                        waits.append(f"mid{cg}_{gi - 2}_7" if gi >= 2 else
                                     f"mid{cg - 1}_{6 + gi}_7")
                    s.add("sync",
                          lambda sy, cg=cg, gi=gi, gpa=gpa: sy.dma_start(
                              out=xn_g[:, gpa],
                              in_=xnc3[:, :, gi * TG:(gi + 1) * TG]),
                          waits=waits, label=f"f2x{cg}_{gi}", dma="dxi")

                    for ci in range(CPG):
                        waits = [f"f2x{cg}_{gi}", f"lwg{cg}"]
                        if not (cg == 0 and gi == 0 and ci < 2):
                            waits.append(
                                f"mid{cg}_{gi}_{ci - 2}" if ci >= 2 else
                                (f"mid{cg}_{gi - 1}_{6 + ci}" if gi > 0 else
                                 f"mid{cg - 1}_7_{6 + ci}"))

                        def ag_f(te, cg=cg, pa=pa, gi=gi, gpa=gpa, ci=ci):
                            for k in range(NCH):
                                te.matmul(pg[:, ci % 2],
                                          wgt_g[:, pa, ci, k],
                                          xn_g[:, gpa, k], start=(k == 0),
                                          stop=(k == NCH - 1))
                            for k in range(NCH):
                                te.matmul(pu[:, ci % 2],
                                          wgt_u[:, pa, ci, k],
                                          xn_g[:, gpa, k], start=(k == 0),
                                          stop=(k == NCH - 1))
                        s.add("tensor", ag_f, waits=waits,
                              label=f"agu{cg}_{gi}_{ci}")

                        wtg = [f"agu{cg}_{gi}_{ci}"]
                        if not (cg == 0 and gi == 0 and ci == 0):
                            wtg.append(
                                f"mid{cg}_{gi}_{ci - 1}" if ci > 0 else
                                (f"mid{cg}_{gi - 1}_7" if gi > 0 else
                                 f"mid{cg - 1}_7_7"))
                        s.add("scalar",
                              lambda se, pa=pa, ci=ci: se.activation(
                                  out=t_g[:], in_=pg[:, ci % 2],
                                  func=AF.Silu,
                                  bias=bgu_t[:, pa, ci, 0:1], scale=1.0),
                              waits=wtg, label=f"tg{cg}_{gi}_{ci}")

                        s.add("vector",
                              lambda ve, pa=pa, ci=ci:
                              ve.scalar_tensor_tensor(
                                  mid[:, ci], pu[:, ci % 2],
                                  bgu_t[:, pa, ci, 1:2], t_g[:],
                                  OP.add, OP.mult),
                              waits=[f"tg{cg}_{gi}_{ci}"],
                              label=f"mid{cg}_{gi}_{ci}")

                    for oc in range(8):
                        waits = [f"mid{cg}_{gi}_7"]
                        if not (cg == 0 and gi == 0 and oc < 2):
                            waits.append(
                                f"oacc{cg}_{gi}_{oc - 2}" if oc >= 2 else
                                (f"oacc{cg}_{gi - 1}_{6 + oc}" if gi > 0
                                 else f"oacc{cg - 1}_7_{6 + oc}"))

                        def wd_f(te, cg=cg, pa=pa, gi=gi, oc=oc):
                            for ci in range(CPG):
                                te.matmul(po[:, oc % 2],
                                          wgt_d[:, pa, ci,
                                                oc * 128:(oc + 1) * 128],
                                          mid[:, ci], start=(ci == 0),
                                          stop=(ci == CPG - 1))
                        s.add("tensor", wd_f, waits=waits,
                              label=f"wd{cg}_{gi}_{oc}")

                        wacc = [f"wd{cg}_{gi}_{oc}"]
                        if cg == 0 and gi == 0 and oc == 0:
                            wacc.append("lacc")

                        def acc_f(ve, gi=gi, oc=oc):
                            ve.tensor_add(acc[:, oc, gi * TG:(gi + 1) * TG],
                                          acc[:, oc, gi * TG:(gi + 1) * TG],
                                          po[:, oc % 2])
                        s.add("vector", acc_f, waits=wacc,
                              label=f"oacc{cg}_{gi}_{oc}")
                s.labels[f"wdone{cg}"] = s.labels[f"oacc{cg}_7_7"]
            s.emit()
        p.barrier()

        if "dbg_acc" in dbg:
            dump_fm(p, dbg["dbg_acc"], acc[:])

        with (
            nc.sbuf_tensor("f2_x", [128, 2, NCH, TG], F32) as x_t,
            nc.sbuf_tensor("f2_s", [128, 2, NCH, TG], BF16) as s_t,
            nc.sbuf_tensor("f2_gk", [128, TG], BF16) as gk,
            nc.sbuf_tensor("f2_o", [128, 2, NCH, TG], F32) as o_t,
        ):
            s = Sched(p, "f2b")
            for gi in range(NTG):
                gpa = gi % 2
                waits = []
                if gi >= 2:
                    waits.append(f"cmb{gi - 2}")
                    waits.append(f"co{gi - 2}")

                def lx_f(sy, gi=gi, gpa=gpa):
                    return [
                        sy.dma_start(out=x_t[:, gpa],
                                     in_=x3[:, :, gi * TG:(gi + 1) * TG]),
                        sy.dma_start(out=s_t[:, gpa],
                                     in_=sc3[:, :, gi * TG:(gi + 1) * TG]),
                    ]
                s.add("sync", lx_f, waits=waits, label=f"cx{gi}",
                      dma="dxi", n_dma=2)

                for k in range(NCH):
                    wgk = [f"cx{gi}"]
                    if k > 0:
                        wgk.append(f"cmk{gi}_{k - 1}")
                    elif gi > 0:
                        wgk.append(f"cmk{gi - 1}_7")
                    s.add("scalar",
                          lambda se, gi=gi, k=k: se.activation(
                              out=gk[:],
                              in_=acc[:, k, gi * TG:(gi + 1) * TG],
                              func=AF.Silu, bias=bd_t[:, k, 0:1],
                              scale=1.0),
                          waits=wgk, label=f"cgk{gi}_{k}")

                    def cm_f(ve, gi=gi, gpa=gpa, k=k):
                        ve.tensor_mul(gk[:], s_t[:, gpa, k], gk[:])
                        ve.tensor_add(o_t[:, gpa, k], x_t[:, gpa, k], gk[:])
                    s.add("vector", cm_f, waits=[f"cgk{gi}_{k}"],
                          label=f"cmk{gi}_{k}")
                s.labels[f"cmb{gi}"] = s.labels[f"cmk{gi}_7"]

                s.add("sync",
                      lambda sy, gi=gi, gpa=gpa: sy.dma_start(
                          out=o3[:, :, gi * TG:(gi + 1) * TG],
                          in_=o_t[:, gpa]),
                      waits=[f"cmb{gi}"], label=f"co{gi}", dma="dout")

            def f2fin(sy):
                sy.wait_ge(p.sem["dout"], p.cnt["dout"])
            s.add("sync", f2fin, label="f2fin")
            s.emit()



def stage_inputs(inputs):
    x = np.asarray(inputs["L0_attn_x"])
    filt = np.ascontiguousarray(np.concatenate(
        [np.asarray(inputs["L0_attn_h_filter"]).T,
         np.asarray(inputs["L0_attn_g_filter"]).T,
         np.asarray(inputs["L0_attn_h_synth"]).T,
         np.asarray(inputs["L0_attn_g_synth"]).T], axis=1)).astype(np.float32)
    gb = np.ascontiguousarray(np.stack(
        [np.asarray(inputs["L0_attn_norm_gamma"]),
         np.asarray(inputs["L0_attn_norm_beta"])], axis=1)).astype(np.float32)
    keysT = np.ascontiguousarray(
        np.transpose(np.asarray(inputs["L0_ffn_keys"]), (2, 0, 3, 1))
    ).astype(np.float32)
    dnT = np.ascontiguousarray(
        np.asarray(inputs["L0_ffn_down_embed"]).T).astype(np.float32)
    bgu = np.ascontiguousarray(np.stack(
        [np.asarray(inputs["L0_ffn_bg"]),
         np.asarray(inputs["L0_ffn_bu"])], axis=1)).astype(np.float32)
    shared = {
        "filt": filt, "gb": gb,
        "wq": np.ascontiguousarray(np.asarray(inputs["L0_ffn_Wq"],
                                              np.float32)),
        "keysT": keysT, "dnT": dnT,
        "ue": np.ascontiguousarray(np.asarray(inputs["L0_ffn_up_embed"],
                                              np.float32)),
        "wg": np.ascontiguousarray(np.asarray(inputs["L0_ffn_Wg"],
                                              np.float32)),
        "wu": np.ascontiguousarray(np.asarray(inputs["L0_ffn_Wu"],
                                              np.float32)),
        "wd": np.ascontiguousarray(np.asarray(inputs["L0_ffn_Wd"],
                                              np.float32)),
        "bgu": bgu,
        "bd": np.ascontiguousarray(
            np.asarray(inputs["L0_ffn_bd"], np.float32)[:, None]),
    }
    per_core = []
    for b in range(8):
        m = dict(shared)
        m["x"] = np.ascontiguousarray(x[b].T).astype(np.float32)
        per_core.append(m)
    return per_core


def kernel(**inputs):
    nc = build_program()
    in_maps = stage_inputs(inputs)
    res = run_bass_kernel_spmd(nc, in_maps, core_ids=list(range(8)))
    out = np.stack([np.asarray(r["out"]).T for r in res.results])
    return out.astype(np.float32)


# version 08 (final)
# speedup vs baseline: 1.1070x; 1.0479x over previous
import numpy as np
from contextlib import ExitStack
import concourse.bass as bass
import concourse.mybir as mybir
from concourse.bass_utils import run_bass_kernel_spmd

F32 = mybir.dt.float32
BF16 = mybir.dt.bfloat16
I32 = mybir.dt.int32
U32 = mybir.dt.uint32
AF = mybir.ActivationFunctionType
OP = mybir.AluOpType

C = 1024
T = 4096
NCH = 8
TG = 512
NTG = T // TG
LN_EPS = 1e-5
ATTN_EPS = 1e-6
NEGBIG = -1e30

RQ0, RQ1, RQ2 = 0.7765257306659091, -0.17066219872211436, 0.011377491746806564
RSQRT_C = 1.8369135901471134e-20

ENGINES = ("tensor", "vector", "scalar", "gpsimd", "sync")


class Prog:
    def __init__(self, nc):
        self.nc = nc
        self._ctx = []
        self.sem = {}
        self.cnt = {}
        self.bar_round = 0

    def keep(self, cm):
        v = cm.__enter__()
        self._ctx.append(cm)
        return v

    def add_sems(self, *names):
        for n in names:
            self.sem[n] = self.keep(self.nc.semaphore(n))
            self.cnt[n] = 0

    def close(self):
        for cm in reversed(self._ctx):
            cm.__exit__(None, None, None)

    def barrier(self):
        self.bar_round += 1
        target = self.bar_round * len(ENGINES)
        bar = self.sem["bar"]
        dummy = self.bar_dummy
        ones = self.bar_ones
        with self.nc.Block(f"bar{self.bar_round}") as blk:
            @blk.tensor
            def _(te):
                te.ldweights(ones[:, 0:1])
                te.drain().then_inc(bar, 1)
                te.wait_ge(bar, target)

            @blk.vector
            def _(ve):
                ve.memset(dummy[:, 0:1], 0.0)
                ve.drain().then_inc(bar, 1)
                ve.wait_ge(bar, target)

            @blk.scalar
            def _(se):
                se.activation(out=dummy[:, 1:2], in_=dummy[:, 1:2],
                              func=AF.Copy, bias=0.0, scale=1.0)
                se.drain().then_inc(bar, 1)
                se.wait_ge(bar, target)

            @blk.gpsimd
            def _(gp):
                gp.memset(dummy[:, 2:3], 0.0)
                gp.drain().then_inc(bar, 1)
                gp.wait_ge(bar, target)

            @blk.sync
            def _(sy):
                sy.sem_inc(bar, 1)
                sy.wait_ge(bar, target)
        self.cnt["bar"] = target


class _Rec:

    def __init__(self, eng):
        object.__setattr__(self, "_eng", eng)
        object.__setattr__(self, "last", None)

    def __getattr__(self, n):
        a = getattr(object.__getattribute__(self, "_eng"), n)
        if callable(a):
            def call(*args, **kw):
                r = a(*args, **kw)
                if hasattr(r, "then_inc"):
                    object.__setattr__(self, "last", r)
                return r
            return call
        return a


class Sched:

    def __init__(self, p, name):
        self.p = p
        self.name = name
        self.steps = {e: [] for e in ENGINES}
        self.labels = {}

    def add(self, eng, fn, waits=(), label=None, dma=None, n_dma=1):
        p = self.p
        sem_name = dma if dma else "e_" + eng
        inc = 16 * n_dma if dma else 1
        p.cnt[sem_name] += inc
        tgt = p.cnt[sem_name]
        if label is not None:
            assert label not in self.labels, label
            self.labels[label] = (sem_name, tgt)
        self.steps[eng].append((fn, tuple(waits), sem_name, tgt, dma, n_dma))

    def emit(self):
        p = self.p
        with p.nc.Block(self.name) as blk:
            for en in ENGINES:
                if not self.steps[en]:
                    continue

                def _mk(en):
                    def _(eng):
                        for fn, waits, sem_name, tgt, dma, n_dma in \
                                self.steps[en]:
                            for w in waits:
                                ws, wv = self.labels[w]
                                eng.wait_ge(p.sem[ws], wv)
                            if dma is not None:
                                instrs = fn(eng)
                                if not isinstance(instrs, (list, tuple)):
                                    instrs = [instrs]
                                assert len(instrs) == n_dma, \
                                    (self.name, n_dma, len(instrs))
                                for ins in instrs:
                                    ins.then_inc(p.sem[dma], 16)
                            elif en == "sync":
                                rec = _Rec(eng)
                                fn(rec)
                                last = object.__getattribute__(rec, "last")
                                if last is not None:
                                    last.then_inc(p.sem[sem_name], 1)
                                else:
                                    eng.sem_inc(p.sem[sem_name], 1)
                            else:
                                fn(eng)
                                eng.drain().then_inc(p.sem[sem_name], 1)
                    return _
                getattr(blk, en)(_mk(en))


def _width(ap):
    n = 1
    for d in ap.shape[1:]:
        n *= d
    return n


def emit_rsqrt(ve, out, v, z, a, xor_31, iters=3):
    gap = (lambda: ve.drain()) if _width(v) < 64 else (lambda: 0)
    ve.tensor_single_scalar(z.bitcast(I32), v.bitcast(I32), 1,
                            OP.logical_shift_right)
    gap()
    ve.tensor_scalar(z.bitcast(I32), z.bitcast(I32), xor_31, None,
                     OP.bitwise_xor)
    gap()
    ve.tensor_scalar_mul(z, z, RSQRT_C)
    for i in range(iters):
        last = i == iters - 1
        gap()
        ve.tensor_mul(a, z, z)
        gap()
        ve.tensor_mul(a, a, v)
        gap()
        ve.tensor_scalar(a, a, 3.0, -0.5, OP.subtract, OP.mult)
        gap()
        ve.tensor_mul(out if last else z, z, a)


def emit_recip(ve, out, x, r, u, xor_exp, iters=3):
    gap = (lambda: ve.drain()) if _width(x) < 64 else (lambda: 0)
    ve.tensor_scalar(r.bitcast(I32), x.bitcast(I32), xor_exp, None,
                     OP.bitwise_xor)
    gap()
    ve.tensor_mul(u, r, x)
    gap()
    ve.tensor_scalar(out, u, RQ2, RQ1, OP.mult, OP.add)
    gap()
    ve.tensor_mul(out, out, u)
    gap()
    ve.tensor_scalar_add(out, out, RQ0)
    gap()
    ve.tensor_mul(r, out, r)
    for i in range(iters):
        last = i == iters - 1
        gap()
        ve.tensor_mul(u, r, x)
        gap()
        ve.tensor_scalar(u, u, 2.0, -1.0, OP.subtract, OP.mult)
        gap()
        ve.tensor_mul(out if last else r, r, u)



def build_program(debug=(), phases=("ln", "attn", "f1", "f2")):
    nc = bass.Bass("TRN2", target_bir_lowering=False,
                   detect_race_conditions=False)
    p = Prog(nc)
    p.add_sems("bar", "e_tensor", "e_vector", "e_scalar", "e_gpsimd",
               "e_sync", "dxi", "dxn", "dwt", "dout", "dtr")

    d = {}

    def din(name, shape, dtype=F32):
        d[name] = nc.dram_tensor(name, shape, dtype, kind="ExternalInput").ap()

    din("x", [C, T])
    din("filt", [C, 16])
    din("gb", [C, 2])
    din("wq", [C, 512])
    din("keysT", [2, 4, 64, 64])
    din("dnT", [C, 4096])
    din("ue", [4096, C])
    din("wg", [C, 4096])
    din("wu", [C, 4096])
    din("wd", [4096, C])
    din("bgu", [4096, 2])
    din("bd", [C, 1])

    d["xn_c"] = nc.dram_tensor("xn_c", [C, T], BF16, kind="Internal").ap()
    d["s_c"] = nc.dram_tensor("s_c", [C, T], BF16, kind="Internal").ap()
    d["es_c"] = nc.dram_tensor("es_c", [C, T], BF16, kind="Internal").ap()
    d["out"] = nc.dram_tensor("out", [C, T], F32, kind="ExternalOutput").ap()

    dbg = {}
    for name, shape, dt_ in [
        ("dbg_xn", [C, T], BF16), ("dbg_h0a", [C, T // 2], BF16),
        ("dbg_g0", [C, T // 2], BF16), ("dbg_l1", [C, T // 4], BF16),
        ("dbg_g1", [C, T // 4], BF16), ("dbg_att", [C, T // 4], BF16),
        ("dbg_low1", [C, T // 2], BF16), ("dbg_s", [C, T], BF16),
        ("dbg_q", [C, T // 4], BF16), ("dbg_es", [C, T], BF16),
        ("dbg_kvz", [128, 16 * 65], BF16), ("dbg_attm", [128, 8 * 1024], BF16),
        ("dbg_dens", [128, 8 * 16], F32), ("dbg_qtm", [128, 8 * 1024], BF16),
        ("dbg_S", [128, 4096], BF16), ("dbg_siluP", [128, 4096], BF16),
        ("dbg_qret", [512, T], BF16), ("dbg_acc", [C, T], BF16),
        ("dbg_pnz", [128, 520], F32), ("dbg_rcp", [128, 128], F32),
    ]:
        if name in debug:
            dbg[name] = nc.dram_tensor(name, shape, dt_,
                                       kind="ExternalOutput").ap()

    p.bar_dummy = p.keep(nc.sbuf_tensor("bar_dummy", [1, 4], F32))
    consts = p.keep(nc.sbuf_tensor("consts", [128, 8], F32))
    ones128 = p.keep(nc.sbuf_tensor("ones128", [128, 128], BF16))
    ident = p.keep(nc.sbuf_tensor("ident", [128, 128], BF16))
    identi = p.keep(nc.sbuf_tensor("identi", [128, 128], I32))
    negbig64 = p.keep(nc.sbuf_tensor("negbig64", [128, 64], F32))
    onecol = p.keep(nc.sbuf_tensor("onecol", [128, 1], BF16))

    g = dict(consts=consts, ones128=ones128, ident=ident,
             negbig64=negbig64, onecol=onecol,
             XOR_EXP=consts.bitcast(I32)[:, 0:1],
             XOR_31=consts.bitcast(I32)[:, 1:2])

    with nc.Block("init") as blk:
        @blk.vector
        def _(ve):
            ve.memset(consts.bitcast(U32)[:, 0:1], 0x7F800000)
            ve.memset(consts.bitcast(U32)[:, 1:2], 0x7FFFFFFF)
            ve.memset(consts[:, 2:3], NEGBIG)
            ve.memset(ones128[:], 1.0)
            ve.memset(negbig64[:], NEGBIG)
            ve.memset(onecol[:], 1.0)
            ve.memset(p.bar_dummy[:], 0.0)
            ve.wait_ge(p.sem["e_gpsimd"], 1)
            ve.tensor_single_scalar(ident[:], identi[:], 0.0, OP.is_equal)

        @blk.gpsimd
        def _(gp):
            gp.iota(identi[:], [[1, 128]], channel_multiplier=-1,
                    allow_small_or_imprecise_dtypes=True
                    ).then_inc(p.sem["e_gpsimd"], 1)
    p.cnt["e_gpsimd"] += 1
    p.bar_ones = ones128
    p.barrier()

    xn_cm = nc.sbuf_tensor("xn_sb", [128, NCH * T], BF16)
    xn_sb = xn_cm.__enter__()
    xn3 = xn_sb[:].rearrange("p (k t) -> p k t", k=NCH)

    if "ln" in phases:
        emit_ln(p, d, g, xn3)
        p.barrier()
        if "dbg_xn" in dbg:
            dump_fm(p, dbg["dbg_xn"], xn3)

    if "attn" in phases:
        emit_attn(p, d, g, xn3, dbg)
        p.barrier()

    xn_cm.__exit__(None, None, None)

    if "f1" in phases:
        emit_f1(p, d, g, dbg)
        p.barrier()

    if "f2" in phases:
        emit_f2(p, d, g, dbg)
        p.barrier()

    p.close()
    return nc


def dump_fm(p, dram, sb3):
    nc = p.nc
    dr = dram.rearrange("(k p) t -> p k t", p=128)
    p.cnt["dtr"] += 16
    tgt = p.cnt["dtr"]
    p.dump_n = getattr(p, "dump_n", 0) + 1
    with nc.Block(f"dump{p.dump_n}") as blk:
        @blk.sync
        def _(sy):
            sy.dma_start(out=dr, in_=sb3).then_inc(p.sem["dtr"], 16)
            sy.wait_ge(p.sem["dtr"], tgt)
    p.barrier()


def dump_flat(p, dram, sb):
    nc = p.nc
    p.cnt["dtr"] += 16
    tgt = p.cnt["dtr"]
    p.dump_n = getattr(p, "dump_n", 0) + 1
    with nc.Block(f"dumpf{p.dump_n}") as blk:
        @blk.sync
        def _(sy):
            sy.dma_start(out=dram, in_=sb).then_inc(p.sem["dtr"], 16)
            sy.wait_ge(p.sem["dtr"], tgt)
    p.barrier()



def emit_ln(p, d, g, xn3):
    nc = p.nc
    x3 = d["x"].rearrange("(k p) t -> p k t", p=128)
    xnc3 = d["xn_c"].rearrange("(k p) t -> p k t", p=128)

    with (
        nc.sbuf_tensor("ln_x", [128, 2, NCH, TG], F32) as x_t,
        nc.sbuf_tensor("ln_xbf", [128, 2, NCH, TG], BF16) as xbf,
        nc.sbuf_tensor("ln_xsq", [128, 2, NCH, TG], BF16) as xsq,
        nc.sbuf_tensor("ln_mu", [128, 2, TG], F32) as mu_t,
        nc.sbuf_tensor("ln_rs", [128, TG], F32) as rs_t,
        nc.sbuf_tensor("ln_v", [128, TG], F32) as v_t,
        nc.sbuf_tensor("ln_s1", [128, TG], F32) as s1_t,
        nc.sbuf_tensor("ln_s2", [128, TG], F32) as s2_t,
        nc.psum_tensor("ln_ps", [128, 2, 2, TG], F32) as ps,
    ):
        s = Sched(p, "ln")
        for gi in range(NTG):
            pa = gi % 2

            def loadx(sy, gi=gi, pa=pa):
                return sy.dma_start(out=x_t[:, pa],
                                    in_=x3[:, :, gi * TG:(gi + 1) * TG])
            w = [f"ap{gi - 2}"] if gi >= 2 else []
            s.add("sync", loadx, waits=w, label=f"lx{gi}", dma="dxi")

            def xbf_f(ve, gi=gi, pa=pa):
                ve.tensor_copy(xbf[:, pa], x_t[:, pa])
            s.add("vector", xbf_f, waits=[f"lx{gi}"], label=f"xb{gi}")

            def xsq_f(se, gi=gi, pa=pa):
                se.activation(out=xsq[:, pa], in_=x_t[:, pa], func=AF.Square,
                              bias=0.0, scale=1.0)
            wq_ = [f"lx{gi}"] + ([f"st{gi - 2}"] if gi >= 2 else [])
            s.add("scalar", xsq_f, waits=wq_, label=f"xq{gi}")

            def stats(te, gi=gi, pa=pa):
                for k in range(NCH):
                    te.matmul(ps[:, pa, 0], g["ones128"][:], xbf[:, pa, k],
                              start=(k == 0), stop=(k == NCH - 1))
                for k in range(NCH):
                    te.matmul(ps[:, pa, 1], g["ones128"][:], xsq[:, pa, k],
                              start=(k == 0), stop=(k == NCH - 1))
            w = [f"xb{gi}", f"xq{gi}"] + ([f"mo{gi - 2}"] if gi >= 2 else [])
            s.add("tensor", stats, waits=w, label=f"st{gi}")

            def moments(ve, gi=gi, pa=pa):
                ve.tensor_scalar_mul(mu_t[:, pa], ps[:, pa, 0], 1.0 / C)
                ve.tensor_mul(v_t[:], mu_t[:, pa], mu_t[:, pa])
                ve.scalar_tensor_tensor(v_t[:], ps[:, pa, 1], 1.0 / C,
                                        v_t[:], OP.mult, OP.subtract)
                ve.tensor_scalar_add(v_t[:], v_t[:], LN_EPS)
            s.add("vector", moments, waits=[f"st{gi}"], label=f"mo{gi}")

            def apply(ve, gi=gi, pa=pa):
                emit_rsqrt(ve, rs_t[:], v_t[:], s1_t[:], s2_t[:],
                           g["XOR_31"])
                for k in range(NCH):
                    ve.tensor_sub(s1_t[:], x_t[:, pa, k], mu_t[:, pa])
                    ve.tensor_mul(xn3[:, k, gi * TG:(gi + 1) * TG],
                                  s1_t[:], rs_t[:])
            s.add("vector", apply, label=f"ap{gi}")

            def store(sy, gi=gi):
                return sy.dma_start(out=xnc3[:, :, gi * TG:(gi + 1) * TG],
                                    in_=xn3[:, :, gi * TG:(gi + 1) * TG])
            s.add("sync", store, waits=[f"ap{gi}"], label=f"sx{gi}",
                  dma="dxn")
        def lnfin(sy):
            sy.wait_ge(p.sem["dxn"], p.cnt["dxn"])
        s.add("sync", lnfin, label="lnfin")
        s.emit()



def emit_attn(p, d, g, xn3, dbg):
    nc = p.nc
    NT2, NT4 = T // 2, T // 4

    with (
        nc.sbuf_tensor("at_filt", [128, NCH, 16], F32) as filt,
        nc.sbuf_tensor("at_gb", [128, NCH, 2], F32) as gb,
        nc.sbuf_tensor("at_g0", [128, NCH, NT2], BF16) as g0,
        nc.sbuf_tensor("at_g1", [128, NCH, NT4], BF16) as g1,
        nc.sbuf_tensor("at_att", [128, NCH, NT4], BF16) as att,
    ):
        with nc.sbuf_tensor("at_l1", [128, NCH, NT4], BF16) as l1:
            with (
                nc.sbuf_tensor("at_h0a", [128, NCH, NT2], BF16) as h0a,
                nc.sbuf_tensor("at_scr", [128, NT2], F32) as scr,
            ):
                s = Sched(p, "attnA")

                def loadf(sy):
                    return [sy.dma_start(
                        out=filt[:], in_=d["filt"].rearrange(
                            "(k p) i -> p k i", p=128)),
                        sy.dma_start(out=gb[:], in_=d["gb"].rearrange(
                            "(k p) i -> p k i", p=128))]
                s.add("sync", loadf, label="lf", dma="dxi", n_dma=2)

                def conv(ve, out3, in3, fcol, Lin, k):
                    Lo = Lin // 2
                    zi = in3[:, k]
                    f = lambda i: filt[:, k, fcol + i:fcol + i + 1]
                    ve.tensor_scalar(scr[:, 0:Lo], zi[:, 0:Lin:2], f(1),
                                     None, OP.mult)
                    ve.scalar_tensor_tensor(scr[:, 0:Lo], zi[:, 1:Lin:2],
                                            f(2), scr[:, 0:Lo], OP.mult,
                                            OP.add)
                    ve.scalar_tensor_tensor(scr[:, 1:Lo], zi[:, 1:Lin - 2:2],
                                            f(0), scr[:, 1:Lo], OP.mult,
                                            OP.add)
                    ve.scalar_tensor_tensor(scr[:, 0:Lo - 1], zi[:, 2:Lin:2],
                                            f(3), scr[:, 0:Lo - 1], OP.mult,
                                            OP.add)
                    ve.tensor_copy(out3[:, k], scr[:, 0:Lo])

                for k in range(NCH):
                    s.add("vector",
                          lambda ve, k=k: conv(ve, h0a, xn3, 0, T, k),
                          waits=["lf"], label=f"h0a{k}")
                    s.add("vector",
                          lambda ve, k=k: conv(ve, g0, xn3, 4, T, k),
                          label=f"g0{k}")
                for k in range(NCH):
                    s.add("vector",
                          lambda ve, k=k: conv(ve, l1, h0a, 0, NT2, k),
                          label=f"l1{k}")
                    s.add("vector",
                          lambda ve, k=k: conv(ve, g1, h0a, 4, NT2, k),
                          label=f"g1{k}")
                s.emit()
                p.barrier()

                if "dbg_h0a" in dbg:
                    dump_fm(p, dbg["dbg_h0a"], h0a[:])
                if "dbg_g0" in dbg:
                    dump_fm(p, dbg["dbg_g0"], g0[:])
                if "dbg_l1" in dbg:
                    dump_fm(p, dbg["dbg_l1"], l1[:])
                if "dbg_g1" in dbg:
                    dump_fm(p, dbg["dbg_g1"], g1[:])

            emit_linattn(p, d, g, l1, att, dbg)
            p.barrier()
            if "dbg_att" in dbg:
                dump_fm(p, dbg["dbg_att"], att[:])

        emit_up_sln(p, d, g, xn3, att, g0, g1, gb, filt, dbg)
        p.barrier()
        if "dbg_s" in dbg:
            dump_fm(p, dbg["dbg_s"], xn3)


def emit_linattn(p, d, g, l1, att, dbg):
    nc = p.nc
    N = T // 4
    NTC = N // 128

    with (
        nc.sbuf_tensor("la_q", [128, NCH, N], BF16) as q,
        nc.sbuf_tensor("la_qtm", [128, NTC, C], BF16) as q_tm,
        nc.sbuf_tensor("la_vtm", [128, NTC, C], BF16) as v_tm,
        nc.sbuf_tensor("la_kvz", [128, 8, 130], BF16) as kvz,
        nc.sbuf_tensor("la_dens", [128, 8, 16], F32) as dens,
        nc.sbuf_tensor("la_rcp", [128, 8, 16], F32) as rcp,
        nc.sbuf_tensor("la_r1", [128, 8, 16], F32) as r1,
        nc.sbuf_tensor("la_r2", [128, 8, 16], F32) as r2,
        nc.sbuf_tensor("la_scA", [128, N], F32) as scA,
        nc.sbuf_tensor("la_scB", [128, N], F32) as scB,
        nc.psum_tensor("la_pst", [128, 2, 1024], BF16) as pst,
        nc.psum_tensor("la_kv", [64, 2, 512], F32) as pkv,
        nc.psum_tensor("la_nz", [128, 4, 512], F32) as pnz,
    ):
        att_tm = v_tm
        s = Sched(p, "linattn")

        def zkvz(ve):
            ve.memset(kvz[:], 0.0)
        s.add("vector", zkvz, label="zkvz")

        for k in range(NCH):
            s.add("vector",
                  lambda ve, k=k: ve.tensor_scalar_min(scA[:], l1[:, k], 0.0),
                  label=f"qm{k}", waits=[f"qq{k - 1}"] if k else [])
            s.add("scalar",
                  lambda se, k=k: se.activation(out=scB[:], in_=scA[:],
                                                func=AF.Exp, bias=0.0,
                                                scale=1.0),
                  waits=[f"qm{k}"], label=f"qe{k}")
            s.add("vector",
                  lambda ve, k=k: ve.scalar_tensor_tensor(
                      q[:, k], l1[:, k], 0.0, scB[:], OP.max, OP.add),
                  waits=[f"qe{k}"], label=f"qq{k}")

        w = 0
        for src, dst, nm in ((q, q_tm, "q"), (l1, v_tm, "v")):
            for tc in range(NTC):
                for k in range(NCH):
                    w += 1
                    waits = []
                    if nm == "q":
                        waits.append(f"qq{k}")
                    if w > 2:
                        waits.append(f"tev{w - 2}")
                    s.add("tensor",
                          lambda te, src=src, tc=tc, k=k, w=w: te.transpose(
                              pst[:, w % 2, 0:128],
                              src[:, k, tc * 128:(tc + 1) * 128],
                              g["ident"][:]),
                          waits=waits, label=f"tt{w}")
                    s.add("vector",
                          lambda ve, dst=dst, tc=tc, k=k, w=w: ve.tensor_copy(
                              dst[:, tc, k * 128:(k + 1) * 128],
                              pst[:, w % 2, 0:128]),
                          waits=[f"tt{w}"], label=f"tev{w}")
        ntr = w

        for h in range(16):
            pa = h % 2
            waits = [f"tev{ntr}"] if h == 0 else []
            if h >= 2:
                waits.append(f"kve{h - 2}")
            def kv_f(te, h=h, pa=pa):
                for tc in range(NTC):
                    te.matmul(pkv[:, pa, 0:64], q_tm[:, tc, h * 64:(h + 1) * 64],
                              v_tm[:, tc, h * 64:(h + 1) * 64],
                              start=(tc == 0), stop=(tc == NTC - 1))
                for tc in range(NTC):
                    te.matmul(pkv[:, pa, 64:65], q_tm[:, tc, h * 64:(h + 1) * 64],
                              g["onecol"][:], start=(tc == 0),
                              stop=(tc == NTC - 1))
            s.add("tensor", kv_f, waits=waits, label=f"kv{h}")

            def kve_f(ve, h=h, pa=pa):
                j, half = h // 2, h % 2
                ve.tensor_copy(kvz[half * 64:(half + 1) * 64, j,
                                   half * 65:half * 65 + 64],
                               pkv[:, pa, 0:64])
                ve.tensor_copy(kvz[half * 64:(half + 1) * 64, j,
                                   half * 65 + 64:half * 65 + 65],
                               pkv[:, pa, 64:65])
            s.add("vector", kve_f, waits=[f"kv{h}"], label=f"kve{h}")

        nn = 0
        for tc in range(NTC):
            for j in range(8):
                nn += 1
                waits = [f"kve{2 * j + 1}"]
                if nn > 4:
                    waits.append(f"nze{nn - 4}")

                def nz_f(te, tc=tc, j=j, nn=nn):
                    te.matmul(pnz[:, nn % 4, 0:130],
                              q[:, j, tc * 128:(tc + 1) * 128],
                              kvz[:, j], start=True, stop=True)
                s.add("tensor", nz_f, waits=waits, label=f"nz{nn}")

                def nze_f(ve, tc=tc, j=j, nn=nn):
                    ve.tensor_scalar_add(dens[:, tc, 2 * j:2 * j + 1],
                                         pnz[:, nn % 4, 64:65], ATTN_EPS)
                    ve.tensor_scalar_add(dens[:, tc, 2 * j + 1:2 * j + 2],
                                         pnz[:, nn % 4, 129:130], ATTN_EPS)
                    ve.tensor_scalar_add(att_tm[:, tc, 2 * j * 64:
                                                 (2 * j + 1) * 64],
                                         pnz[:, nn % 4, 0:64], 0.0)
                    ve.tensor_scalar_add(att_tm[:, tc, (2 * j + 1) * 64:
                                                 (2 * j + 2) * 64],
                                         pnz[:, nn % 4, 65:129], 0.0)
                    if "dbg_pnz" in dbg and nn in (1, 5):
                        ix = 0 if nn == 1 else 1
                        ve.tensor_copy(scA[:, ix * 260:ix * 260 + 130],
                                       pnz[:, nn % 4, 0:130])
                        ve.tensor_copy(
                            scA[:, ix * 260 + 130:ix * 260 + 258],
                            att_tm[:, tc, 2 * j * 64:(2 * j + 2) * 64])
                s.add("vector", nze_f, waits=[f"nz{nn}"],
                      label=f"nze{nn}")

        s.add("vector",
              lambda ve: emit_recip(ve, rcp[:], dens[:], r1[:], r2[:],
                                    g["XOR_EXP"]),
              label="rcpa")
        for tc in range(NTC):
            def norm_f(ve, tc=tc):
                for h in range(16):
                    ve.tensor_scalar(att_tm[:, tc, h * 64:(h + 1) * 64],
                                     att_tm[:, tc, h * 64:(h + 1) * 64],
                                     rcp[:, tc, h:h + 1], None, OP.mult)
            s.add("vector", norm_f, waits=["rcpa"], label=f"nrm{tc}")

        w2 = 0
        for tc in range(NTC):
            for k in range(NCH):
                w2 += 1
                waits = [f"nrm{tc}"]
                if w2 > 2:
                    waits.append(f"tae{w2 - 2}")
                s.add("tensor",
                      lambda te, tc=tc, k=k, w2=w2: te.transpose(
                          pst[:, w2 % 2, 0:128],
                          att_tm[:, tc, k * 128:(k + 1) * 128],
                          g["ident"][:]),
                      waits=waits, label=f"ta{w2}")
                s.add("vector",
                      lambda ve, tc=tc, k=k, w2=w2: ve.tensor_copy(
                          att[:, k, tc * 128:(tc + 1) * 128],
                          pst[:, w2 % 2, 0:128]),
                      waits=[f"ta{w2}"], label=f"tae{w2}")
        if "dbg_kvz" in dbg:
            s.add("sync", lambda sy: sy.dma_start(
                out=dbg["dbg_kvz"].rearrange("p (h e) -> p h e", h=8),
                in_=kvz[:]), waits=["kve15"], label="dkvz", dma="dtr")
        if "dbg_q" in dbg:
            s.add("sync", lambda sy: sy.dma_start(
                out=dbg["dbg_q"].rearrange("(k p) t -> p k t", p=128),
                in_=q[:]), waits=["qq7"], label="dq", dma="dtr")
        if "dbg_attm" in dbg:
            s.add("sync", lambda sy: sy.dma_start(
                out=dbg["dbg_attm"].rearrange("p (a b) -> p a b", a=8),
                in_=att_tm[:]), waits=[f"nrm{NTC - 1}"], label="dattm",
                dma="dtr")
        if "dbg_rcp" in dbg:
            s.add("sync", lambda sy: sy.dma_start(
                out=dbg["dbg_rcp"].rearrange("p (a b) -> p a b", a=8),
                in_=rcp[:]),
                waits=[f"nrm{NTC - 1}"], label="drcp", dma="dtr")
        if "dbg_pnz" in dbg:
            s.add("sync", lambda sy: sy.dma_start(
                out=dbg["dbg_pnz"], in_=scA[:, 0:520]),
                waits=["nze5"], label="dpnz", dma="dtr")
        if "dbg_dens" in dbg:
            s.add("sync", lambda sy: sy.dma_start(
                out=dbg["dbg_dens"].rearrange("p (a b) -> p a b", a=8),
                in_=dens[:]),
                waits=[f"nrm{NTC - 1}"], label="ddens", dma="dtr")
        if "dbg_qtm" in dbg:
            s.add("sync", lambda sy: sy.dma_start(
                out=dbg["dbg_qtm"].rearrange("p (a b) -> p a b", a=8),
                in_=q_tm[:]), waits=["tev128"], label="dqtm", dma="dtr")
        s.emit()


def emit_up_sln(p, d, g, xn3, att, g0, g1, gb, filt, dbg):
    nc = p.nc
    NT2, NT4 = T // 2, T // 4
    sc3 = d["s_c"].rearrange("(k p) t -> p k t", p=128)

    with (
        nc.sbuf_tensor("up_comb", [128, NCH, 10], F32) as comb,
        nc.sbuf_tensor("up_diag", [128, 2, 10, 128], BF16) as diag,
        nc.sbuf_tensor("up_low1", [128, NCH, NT2], BF16) as low1,
        nc.sbuf_tensor("up_rsq", [128, 2, NCH, TG], BF16) as rsq,
        nc.sbuf_tensor("up_mu", [128, 2, TG], F32) as mu_t,
        nc.sbuf_tensor("up_rs", [128, TG], F32) as rs_t,
        nc.sbuf_tensor("up_s1", [128, TG], F32) as s1_t,
        nc.sbuf_tensor("up_s2", [128, TG], F32) as s2_t,
        nc.sbuf_tensor("up_v", [128, TG], F32) as v_t,
        nc.psum_tensor("up_ps", [128, 4, 512], F32) as ps,
        nc.psum_tensor("up_st", [128, 2, 2, TG], F32) as pstat,
    ):
        s = Sched(p, "upconv")

        def combs(ve):
            for k in range(NCH):
                for base, col in ((8, 0), (12, 5)):
                    f = lambda i: filt[:, k, base + i:base + i + 1]
                    cs = lambda j: comb[:, k, col + j:col + j + 1]
                    ve.tensor_copy(cs(0), f(0))
                    ve.tensor_add(cs(1), f(1), f(2))
                    ve.tensor_copy(cs(2), f(3))
                    ve.tensor_add(cs(3), f(0), f(1))
                    ve.tensor_add(cs(4), f(2), f(3))
            ve.drain()
        s.add("vector", combs, label="cmb")

        def add_diag(tag, k, waits):
            def f(ve, k=k):
                for j in range(10):
                    ve.tensor_scalar(diag[:, k % 2, j], g["ident"][:],
                                     comb[:, k, j:j + 1], None, OP.mult)
            s.add("vector", f, waits=waits, label=tag)

        def up_pass(passname, vin1, vin2, Lh, nslots_used, evac_eng,
                    evac_fn, prev_diag_user):
            ngr = Lh // 512
            wv = 0
            for k in range(NCH):
                dwaits = ["cmb"]
                if k >= 2:
                    dwaits.append(f"{passname}mm{k - 2}_last")
                add_diag(f"{passname}dg{k}", k, dwaits)
                for ph in range(2):
                    for tg2 in range(ngr):
                        wv += 1
                        lo = tg2 * 512
                        slot = (wv - 1) % 4
                        waits = [f"{passname}dg{k}"]
                        if wv > 4:
                            waits.append(f"{passname}ev{wv - 4}")

                        def mm_f(te, k=k, ph=ph, lo=lo, slot=slot,
                                 vin1=vin1, vin2=vin2, Lh=Lh):
                            seq = []
                            for (src, base) in ((vin1, 0), (vin2, 5)):
                                if ph == 0:
                                    seq += [(src, base + 1, 0),
                                            (src, base + 0, -1),
                                            (src, base + 2, +1)]
                                else:
                                    seq += [(src, base + 3, 0),
                                            (src, base + 4, +1)]
                            for idx, (src, tap, off) in enumerate(seq):
                                a, b = lo + off, lo + off + 512
                                pslice = ps[:, slot, 0:512]
                                if a < 0:
                                    pslice = ps[:, slot, 1:512]
                                    a, b = 0, 511
                                elif b > Lh:
                                    pslice = ps[:, slot, 0:512 - (b - Lh)]
                                    b = Lh
                                te.matmul(pslice, diag[:, k % 2, tap],
                                          src[:, k, a:b],
                                          start=(idx == 0),
                                          stop=(idx == len(seq) - 1))
                        lbl = f"{passname}mm{wv}"
                        s.add("tensor", mm_f, waits=waits, label=lbl)
                        if ph == 1 and tg2 == ngr - 1:
                            s.labels[f"{passname}mm{k}_last"] = \
                                s.labels[lbl]
                        s.add(evac_eng,
                              lambda e, k=k, ph=ph, tg2=tg2, slot=slot:
                              evac_fn(e, k, ph, tg2, slot),
                              waits=[lbl], label=f"{passname}ev{wv}")
            return wv

        def ev1(se, k, ph, tg2, slot):
            se.activation(out=low1[:, k, tg2 * 1024 + ph:
                                   (tg2 + 1) * 1024:2],
                          in_=ps[:, slot, 0:512], func=AF.Copy, bias=0.0,
                          scale=1.0)
        n1 = up_pass("u1", att, g1, NT4, 4, "scalar", ev1, None)

        def ev0(ve, k, ph, tg2, slot):
            ve.tensor_add(xn3[:, k, tg2 * 1024 + ph:(tg2 + 1) * 1024:2],
                          ps[:, slot, 0:512],
                          xn3[:, k, tg2 * 1024 + ph:(tg2 + 1) * 1024:2])
        def up0():
            ngr = NT2 // 512
            wv = 0
            for k in range(NCH):
                dwaits = [f"u1ev{n1}"] if k == 0 else []
                dwaits.append("cmb")
                if k >= 2:
                    dwaits.append(f"u0mm{k - 2}_last")
                add_diag(f"u0dg{k}", k, dwaits)
                for ph in range(2):
                    for tg2 in range(ngr):
                        wv += 1
                        lo = tg2 * 512
                        slot = (wv - 1) % 4
                        waits = [f"u0dg{k}"]
                        if wv > 4:
                            waits.append(f"u0ev{wv - 4}")

                        def mm_f(te, k=k, ph=ph, lo=lo, slot=slot):
                            seq = []
                            for (src, base) in ((low1, 0), (g0, 5)):
                                if ph == 0:
                                    seq += [(src, base + 1, 0),
                                            (src, base + 0, -1),
                                            (src, base + 2, +1)]
                                else:
                                    seq += [(src, base + 3, 0),
                                            (src, base + 4, +1)]
                            for idx, (src, tap, off) in enumerate(seq):
                                a, b = lo + off, lo + off + 512
                                pslice = ps[:, slot, 0:512]
                                if a < 0:
                                    pslice = ps[:, slot, 1:512]
                                    a, b = 0, 511
                                elif b > NT2:
                                    pslice = ps[:, slot, 0:512 - (b - NT2)]
                                    b = NT2
                                te.matmul(pslice, diag[:, k % 2, tap],
                                          src[:, k, a:b],
                                          start=(idx == 0),
                                          stop=(idx == len(seq) - 1))
                        lbl = f"u0mm{wv}"
                        s.add("tensor", mm_f, waits=waits, label=lbl)
                        if ph == 1 and tg2 == ngr - 1:
                            s.labels[f"u0mm{k}_last"] = s.labels[lbl]
                        s.add("vector",
                              lambda ve, k=k, ph=ph, tg2=tg2, slot=slot:
                              ev0(ve, k, ph, tg2, slot),
                              waits=[lbl], label=f"u0ev{wv}")
            return wv
        n0 = up0()

        for gi in range(NTG):
            pa = gi % 2

            def sq_f(se, gi=gi, pa=pa):
                for k in range(NCH):
                    se.activation(out=rsq[:, pa, k],
                                  in_=xn3[:, k, gi * TG:(gi + 1) * TG],
                                  func=AF.Square, bias=0.0, scale=1.0)
            waits = [f"u0ev{n0}"] if gi == 0 else []
            if gi >= 2:
                waits.append(f"sap{gi - 2}")
            s.add("scalar", sq_f, waits=waits, label=f"ssq{gi}")

            def sstat(te, gi=gi, pa=pa):
                for k in range(NCH):
                    te.matmul(pstat[:, pa, 0], g["ones128"][:],
                              xn3[:, k, gi * TG:(gi + 1) * TG],
                              start=(k == 0), stop=(k == NCH - 1))
                for k in range(NCH):
                    te.matmul(pstat[:, pa, 1], g["ones128"][:],
                              rsq[:, pa, k], start=(k == 0),
                              stop=(k == NCH - 1))
            w = [f"ssq{gi}"] + ([f"smo{gi - 2}"] if gi >= 2 else [])
            s.add("tensor", sstat, waits=w, label=f"sst{gi}")

            def smom(ve, gi=gi, pa=pa):
                ve.tensor_scalar_mul(mu_t[:, pa], pstat[:, pa, 0], 1.0 / C)
                ve.tensor_mul(v_t[:], mu_t[:, pa], mu_t[:, pa])
                ve.scalar_tensor_tensor(v_t[:], pstat[:, pa, 1], 1.0 / C,
                                        v_t[:], OP.mult, OP.subtract)
                ve.tensor_scalar_add(v_t[:], v_t[:], LN_EPS)
            s.add("vector", smom, waits=[f"sst{gi}"], label=f"smo{gi}")

            def sap(ve, gi=gi, pa=pa):
                emit_rsqrt(ve, rs_t[:], v_t[:], s1_t[:], s2_t[:],
                           g["XOR_31"])
                for k in range(NCH):
                    ve.tensor_sub(s1_t[:], xn3[:, k, gi * TG:(gi + 1) * TG],
                                  mu_t[:, pa])
                    ve.tensor_mul(s1_t[:], s1_t[:], rs_t[:])
                    ve.tensor_scalar(xn3[:, k, gi * TG:(gi + 1) * TG],
                                     s1_t[:], gb[:, k, 0:1], gb[:, k, 1:2],
                                     OP.mult, OP.add)
            s.add("vector", sap, label=f"sap{gi}")

            s.add("sync",
                  lambda sy, gi=gi: sy.dma_start(
                      out=sc3[:, :, gi * TG:(gi + 1) * TG],
                      in_=xn3[:, :, gi * TG:(gi + 1) * TG]),
                  waits=[f"sap{gi}"], label=f"ssx{gi}", dma="dxn")
        s.add("sync", lambda sy: sy.wait_ge(p.sem["dxn"], p.cnt["dxn"]),
              label="upfin")
        if "dbg_low1" in dbg:
            s.add("sync", lambda sy: sy.dma_start(
                out=dbg["dbg_low1"].rearrange("(k p) t -> p k t", p=128),
                in_=low1[:]), waits=[f"u1ev{n1}"], label="dlow1", dma="dtr")
        s.emit()



def emit_f1(p, d, g, dbg):
    nc = p.nc
    xnc3 = d["xn_c"].rearrange("(k p) t -> p k t", p=128)
    esc3 = d["es_c"].rearrange("(k p) t -> p k t", p=128)
    ue3 = d["ue"].rearrange("(ec p) dd -> p ec dd", p=128)

    es_ = ExitStack()
    def sb(name, shape, dt_):
        return es_.enter_context(nc.sbuf_tensor(name, shape, dt_))
    def psb(name, shape, dt_):
        return es_.enter_context(nc.psum_tensor(name, shape, dt_))
    dn = sb("f1_dn", [128, NCH, 4096], BF16)
    keys = sb("f1_keys", [128, 4, 128], BF16)
    wq = sb("f1_wq", [128, NCH, 512], BF16)
    xn_g = sb("f1_xn", [128, 2, NCH, TG], BF16)
    qret = sb("f1_qret", [128, 4, TG], BF16)
    Bm = sb("f1_Bm", [128, 4, 8, 64], F32)
    mx = sb("f1_mx", [128, 4, 8, 8], F32)
    tnew = sb("f1_tnew", [128, 4, 64], F32)
    msk64 = sb("f1_msk", [128, 4, 64], mybir.dt.uint8)
    grid_a = sb("f1_grid", [128, 4, 64], F32)
    g8a = sb("f1_g8", [128, 16, 8], F32)
    negm = sb("f1_nm", [128, 16], F32)
    theta = sb("f1_th", [128, 16], F32)
    e8a = sb("f1_e8", [128, 16, 8], F32)
    Zb = sb("f1_Z", [128, 64], F32)
    rZ = sb("f1_rZ", [128, 64], F32)
    D = sb("f1_D", [128, 2, 4096], F32)
    W = sb("f1_W", [128, 4096], BF16)
    M = sb("f1_M", [128, 4096], BF16)
    S = sb("f1_S", [128, 4096], BF16)
    sP = sb("f1_sP", [128, 4096], BF16)
    Sfm = sb("f1_Sfm", [128, 32, TG], BF16)
    ue_t = sb("f1_ue", [128, 2, 512], BF16)
    es_t = sb("f1_es", [128, 2, TG], BF16)
    psim = psb("f1_psim", [128, 2, 512], F32)
    ppd = psb("f1_pd", [128, 2, 512], F32)
    pes = psb("f1_pes", [128, 4, 512], F32)
    if True:
        D3 = [D[:, sl].rearrange("p (i j) -> p i j", i=64)
              for sl in range(2)]
        s = Sched(p, "f1")

        kr = d["keysT"].rearrange("q h n k -> n (q h) k")

        def zkeys(ve):
            ve.memset(keys[:], 0.0)
            ve.memset(Zb[:], 1.0)
        s.add("vector", zkeys, label="zk")

        def loadw(gp):
            out = [
                gp.dma_start(out=dn[:], in_=d["dnT"].rearrange(
                    "(k p) e -> p k e", p=128)),
                gp.dma_start(out=wq[:], in_=d["wq"].rearrange(
                    "(k p) r -> p k r", p=128)),
            ]
            for rc in range(4):
                out.append(gp.dma_start(out=keys[0:64, rc, 0:64],
                                        in_=kr[:, 2 * rc]))
                out.append(gp.dma_start(out=keys[64:128, rc, 64:128],
                                        in_=kr[:, 2 * rc + 1]))
            return out
        s.add("gpsimd", loadw, waits=["zk"], label="lw", dma="dwt",
              n_dma=10)

        for gi in range(NTG):
            pa = gi % 2

            waits = ["lw"] if gi == 0 else [f"pdn{gi - 1}_3_7"]
            s.add("sync",
                  lambda sy, gi=gi, pa=pa: sy.dma_start(
                      out=xn_g[:, pa], in_=xnc3[:, :, gi * TG:(gi + 1) * TG]),
                  waits=waits, label=f"lxn{gi}", dma="dxi")

            for rc in range(4):
                waits = [f"lxn{gi}"]
                if gi > 0:
                    waits.append(f"slu{gi - 1}_3_{6 + rc % 2}")
                if rc >= 2:
                    waits.append(f"qe{gi}_{rc - 2}")

                def qr_f(te, gi=gi, pa=pa, rc=rc):
                    for k in range(NCH):
                        te.matmul(ppd[:, rc % 2],
                                  wq[:, k, rc * 128:(rc + 1) * 128],
                                  xn_g[:, pa, k], start=(k == 0),
                                  stop=(k == NCH - 1))
                s.add("tensor", qr_f, waits=waits, label=f"qr{gi}_{rc}")
                s.add("vector",
                      lambda ve, rc=rc: ve.tensor_copy(qret[:, rc],
                                                       ppd[:, rc % 2]),
                      waits=[f"qr{gi}_{rc}"], label=f"qe{gi}_{rc}")

            for tc in range(4):
                waits = [f"qe{gi}_3"]
                if gi > 0 or tc >= 2:
                    waits.append(f"st1{gi}_{tc - 2}" if tc >= 2 else
                                 f"st1{gi - 1}_{2 + tc}")

                def sim_f(te, gi=gi, tc=tc):
                    for rc in range(4):
                        te.matmul(psim[:, tc % 2, rc * 128:(rc + 1) * 128],
                                  qret[:, rc, tc * 128:(tc + 1) * 128],
                                  keys[:, rc], start=True, stop=True)
                s.add("tensor", sim_f, waits=waits, label=f"sim{gi}_{tc}")

                def st1_f(ve, gi=gi, tc=tc):
                    def sl(ph):
                        rc, off = ph // 2, (ph % 2) * 64
                        return psim[:, tc % 2, rc * 128 + off:
                                    rc * 128 + off + 64]
                    for half in range(2):
                        phs = range(half * 4, half * 4 + 4)
                        for ph in phs:
                            ve.max(out=mx[:, tc, ph], in_=sl(ph))
                        for ph in phs:
                            ve.match_replace(out=tnew[:, ph % 4],
                                             in_to_replace=mx[:, tc, ph],
                                             in_values=sl(ph),
                                             imm_value=NEGBIG)
                        for ph in phs:
                            ve.tensor_scalar(msk64[:, ph % 4],
                                             tnew[:, ph % 4], NEGBIG,
                                             None, OP.is_equal)
                        for ph in phs:
                            ve.select(Bm[:, tc, ph], msk64[:, ph % 4],
                                      sl(ph), g["negbig64"][:])
                s.add("vector", st1_f, waits=[f"sim{gi}_{tc}"],
                      label=f"st1{gi}_{tc}")

            if gi > 0:
                emit_up(gi - 1)

            def zg_f(ve, gi=gi):
                for q in range(4):
                    for j in range(4):
                        tc, h = q, j
                        ve.tensor_add(
                            grid_a[:, j].rearrange("p (a b) -> p a b",
                                                   a=8),
                            mx[:, tc, h, :].unsqueeze(2).broadcast_to(
                                [128, 8, 8]),
                            mx[:, tc, 4 + h, :].unsqueeze(1).broadcast_to(
                                [128, 8, 8]))
                    for j in range(4):
                        ve.max(out=g8a[:, q * 4 + j], in_=grid_a[:, j])
                for i in range(16):
                    ve.tensor_scalar(negm[:, i:i + 1], g8a[:, i, 0:1],
                                     -1.0, None, OP.mult)
                for i in range(16):
                    ve.tensor_copy(theta[:, i:i + 1], g8a[:, i, 7:8])
                for i in range(16):
                    ve.tensor_scalar(e8a[:, i], g8a[:, i],
                                     negm[:, i:i + 1], None, OP.add)
            s.add("vector", zg_f, waits=[f"st1{gi}_3"], label=f"zg{gi}")

            s.add("scalar",
                  lambda se: se.activation(
                      out=e8a[:].rearrange("p a b -> p (a b)"),
                      in_=e8a[:].rearrange("p a b -> p (a b)"),
                      func=AF.Exp, bias=0.0, scale=1.0),
                  waits=[f"zg{gi}"], label=f"ze{gi}")

            def zs_f(ve, gi=gi):
                for i in range(16):
                    ve.reduce_sum(Zb[:, i:i + 1], e8a[:, i],
                                  axis=mybir.AxisListType.X)
                ve.drain()
                emit_recip(ve, rZ[:], Zb[:], grid_a[:, 0], grid_a[:, 1],
                           g["XOR_EXP"])
            s.add("vector", zs_f, waits=[f"ze{gi}"], label=f"zs{gi}")

            for tc in range(4):
                def mk_dD(tc, h, waits):
                    def dD_f(ve, tc=tc, h=h):
                        ve.tensor_add(
                            D3[h % 2],
                            Bm[:, tc, h, :].unsqueeze(2).broadcast_to(
                                [128, 64, 64]),
                            Bm[:, tc, 4 + h, :].unsqueeze(1).broadcast_to(
                                [128, 64, 64]))
                    s.add("vector", dD_f, waits=waits,
                          label=f"dD{gi}_{tc}_{h}")

                def mk_sW(tc, h, waits):
                    def sW_f(se, tc=tc, h=h):
                        se.activation(out=W[:], in_=D[:, h % 2],
                                      func=AF.Exp,
                                      bias=negm[:, tc * 4 + h:
                                                tc * 4 + h + 1], scale=1.0)
                    s.add("scalar", sW_f, waits=waits,
                          label=f"sW{gi}_{tc}_{h}")

                def mk_dM(tc, h):
                    def dM_f(ve, tc=tc, h=h):
                        i = tc * 4 + h
                        ve.tensor_scalar(M[:], D[:, h % 2],
                                         theta[:, i:i + 1], None, OP.is_ge)
                        if h == 0:
                            ve.scalar_tensor_tensor(
                                S[:], W[:], rZ[:, i:i + 1], M[:],
                                OP.mult, OP.mult)
                        else:
                            ve.scalar_tensor_tensor(
                                M[:], W[:], rZ[:, i:i + 1], M[:],
                                OP.mult, OP.mult)
                            ve.tensor_add(S[:], S[:], M[:])
                    s.add("vector", dM_f, waits=[f"sW{gi}_{tc}_{h}"],
                          label=f"dm{gi}_{tc}_{h}")

                w0 = [f"st1{gi}_{tc}"]
                pw = []
                if not (gi == 0 and tc == 0):
                    w0.append(f"tr{gi}_{tc - 1}" if tc > 0 else
                              f"tr{gi - 1}_3")
                    pt, pg_ = (tc - 1, gi) if tc > 0 else (3, gi - 1)
                    pw = [f"dm{pg_}_{pt}_3"]
                mk_dD(tc, 0, w0)
                mk_dD(tc, 1, [])
                mk_sW(tc, 0, [f"dD{gi}_{tc}_0"] + pw)
                mk_dM(tc, 0)
                mk_sW(tc, 1, [f"dD{gi}_{tc}_1", f"dm{gi}_{tc}_0"])
                mk_dD(tc, 2, [])
                mk_dM(tc, 1)
                mk_sW(tc, 2, [f"dD{gi}_{tc}_2", f"dm{gi}_{tc}_1"])
                mk_dD(tc, 3, [])
                mk_dM(tc, 2)
                mk_sW(tc, 3, [f"dD{gi}_{tc}_3", f"dm{gi}_{tc}_2"])
                mk_dM(tc, 3)

                for eg in range(8):
                    waits = []
                    if eg >= 2:
                        waits.append(f"slu{gi}_{tc}_{eg - 2}")
                    elif not (gi == 0 and tc == 0):
                        waits.append(f"slu{gi}_{tc - 1}_{6 + eg}" if tc > 0
                                     else f"qe{gi}_{2 + eg}")
                    if eg < 2 and not (gi == 0 and tc == 0) and tc == 0:
                        pass
                    if eg == 0:
                        waits.append(f"lxn{gi}")

                    def pd_f(te, gi=gi, pa=pa, tc=tc, eg=eg):
                        for k in range(NCH):
                            te.matmul(ppd[:, eg % 2],
                                      xn_g[:, pa, k, tc * 128:(tc + 1) * 128],
                                      dn[:, k, eg * 512:(eg + 1) * 512],
                                      start=(k == 0), stop=(k == NCH - 1))
                    s.add("tensor", pd_f, waits=waits,
                          label=f"pdn{gi}_{tc}_{eg}")

                    wsl = [f"pdn{gi}_{tc}_{eg}"]
                    if eg == 0 and not (gi == 0 and tc == 0):
                        wsl.append(f"smul{gi}_{tc - 1}" if tc > 0 else
                                   f"smul{gi - 1}_3")
                    s.add("scalar",
                          lambda se, eg=eg: se.activation(
                              out=sP[:, eg * 512:(eg + 1) * 512],
                              in_=ppd[:, eg % 2], func=AF.Silu, bias=0.0,
                              scale=1.0),
                          waits=wsl, label=f"slu{gi}_{tc}_{eg}")

                s.add("vector",
                      lambda ve: ve.tensor_mul(S[:], S[:], sP[:]),
                      waits=[f"dm{gi}_{tc}_3", f"slu{gi}_{tc}_7"],
                      label=f"smul{gi}_{tc}")

                if "dbg_S" in dbg and gi == 0 and tc == 0:
                    s.add("sync", lambda sy: sy.dma_start(
                        out=dbg["dbg_S"], in_=S[:]),
                        waits=[f"smul{gi}_{tc}"], label="dS", dma="dtr")
                if "dbg_siluP" in dbg and gi == 0 and tc == 0:
                    s.add("sync", lambda sy: sy.dma_start(
                        out=dbg["dbg_siluP"], in_=sP[:]),
                        waits=[f"slu{gi}_{tc}_7"], label="dsP", dma="dtr")

                waits = [f"smul{gi}_{tc}"]
                if gi > 0 and tc == 0:
                    waits.append(f"upm{gi - 1}_1_31")

                def tr_f(sy, gi=gi, tc=tc):
                    out = []
                    for ec in range(32):
                        out.append(sy.dma_start(
                            out=Sfm[:, ec, tc * 128:(tc + 1) * 128],
                            in_=S[:, ec * 128:(ec + 1) * 128],
                            transpose=True))
                    return out
                s.add("sync", tr_f, waits=waits, label=f"tr{gi}_{tc}",
                      dma="dtr", n_dma=32)

                if tc == 3 and gi > 0:
                    emit_evac(gi - 1, 1)

            def emit_evac(gi, pp):
                for d4 in range(4):
                    dc = pp * 4 + d4
                    waits = [f"upm{gi}_{pp}_31"]
                    if dc >= 2:
                        waits.append(f"esd{gi}_{dc - 2}")
                    elif gi > 0:
                        waits.append(f"esd{gi - 1}_{6 + dc}")
                    s.add("vector",
                          lambda ve, dc=dc, d4=d4: ve.tensor_copy(
                              es_t[:, dc % 2], pes[:, d4]),
                          waits=waits, label=f"ese{gi}_{dc}")
                    s.add("sync",
                          lambda sy, gi=gi, dc=dc: sy.dma_start(
                              out=esc3[:, dc, gi * TG:(gi + 1) * TG],
                              in_=es_t[:, dc % 2]),
                          waits=[f"ese{gi}_{dc}"], label=f"esd{gi}_{dc}",
                          dma="dxn")

            def emit_up(gi):
                for pp in range(2):
                    for ec in range(32):
                        waits = []
                        if ec >= 2:
                            waits.append(f"upm{gi}_{pp}_{ec - 2}")
                        elif pp == 1:
                            waits.append(f"upm{gi}_0_{30 + ec}")
                        elif gi > 0:
                            waits.append(f"upm{gi - 1}_1_{30 + ec}")
                        s.add("gpsimd",
                              lambda gp, gi=gi, pp=pp, ec=ec: gp.dma_start(
                                  out=ue_t[:, ec % 2],
                                  in_=ue3[:, ec, pp * 512:(pp + 1) * 512]),
                              waits=waits, label=f"lue{gi}_{pp}_{ec}", dma="dwt")

                        waits = [f"lue{gi}_{pp}_{ec}", f"tr{gi}_3"]
                        if ec == 0:
                            waits.append(f"ese{gi - 1}_7" if (pp == 0 and gi > 0)
                                         else (f"ese{gi}_3" if pp == 1 else
                                               "lw"))

                        def up_f(te, gi=gi, pp=pp, ec=ec):
                            for d4 in range(4):
                                te.matmul(pes[:, d4],
                                          ue_t[:, ec % 2,
                                               d4 * 128:(d4 + 1) * 128],
                                          Sfm[:, ec], start=(ec == 0),
                                          stop=(ec == 31))
                        s.add("tensor", up_f, waits=waits,
                              label=f"upm{gi}_{pp}_{ec}")

                    if pp == 0:
                        emit_evac(gi, 0)

        emit_up(NTG - 1)
        emit_evac(NTG - 1, 1)

        def f1fin(sy):
            sy.wait_ge(p.sem["dxn"], p.cnt["dxn"])
        s.add("sync", f1fin, label="f1fin")
        s.emit()
    es_.close()



def emit_f2(p, d, g, dbg):
    nc = p.nc
    xnc3 = d["xn_c"].rearrange("(k p) t -> p k t", p=128)
    esc3 = d["es_c"].rearrange("(k p) t -> p k t", p=128)
    sc3 = d["s_c"].rearrange("(k p) t -> p k t", p=128)
    x3 = d["x"].rearrange("(k p) t -> p k t", p=128)
    o3 = d["out"].rearrange("(k p) t -> p k t", p=128)
    wg4 = d["wg"].rearrange("(k p) cq -> p k cq", p=128)
    wu4 = d["wu"].rearrange("(k p) cq -> p k cq", p=128)
    wd3 = d["wd"].rearrange("(c p) n -> p c n", p=128)
    bgu3 = d["bgu"].rearrange("(c p) i -> p c i", p=128)
    bd2 = d["bd"].rearrange("(k p) i -> p k i", p=128)

    NCG, CPG = 4, 8

    with (
        nc.sbuf_tensor("f2_acc", [128, NCH, T], BF16) as acc,
        nc.sbuf_tensor("f2_bd", [128, NCH, 1], F32) as bd_t,
    ):
        with (
            nc.sbuf_tensor("f2_wg", [128, 2, CPG, NCH, 128], BF16) as wgt_g,
            nc.sbuf_tensor("f2_wu", [128, 2, CPG, NCH, 128], BF16) as wgt_u,
            nc.sbuf_tensor("f2_wd", [128, 2, CPG, 1024], BF16) as wgt_d,
            nc.sbuf_tensor("f2_bgu", [128, 2, CPG, 2], F32) as bgu_t,
            nc.sbuf_tensor("f2_xn", [128, 2, NCH, TG], BF16) as xn_g,
            nc.sbuf_tensor("f2_tg", [128, 512], BF16) as t_g,
            nc.sbuf_tensor("f2_mid", [128, CPG, TG], BF16) as mid,
            nc.psum_tensor("f2_pg", [128, 2, 512], F32) as pg,
            nc.psum_tensor("f2_pu", [128, 2, 512], F32) as pu,
            nc.psum_tensor("f2_po", [128, 2, 512], F32) as po,
        ):
            s = Sched(p, "f2a")
            s.add("sync", lambda sy: sy.dma_start(out=acc[:], in_=esc3),
                  label="lacc", dma="dxi")
            s.add("gpsimd", lambda gp: gp.dma_start(out=bd_t[:], in_=bd2, allow_slow_non_contiguous=True),
                  label="lbd", dma="dwt")

            for cg in range(NCG):
                waits = []
                if cg >= 2:
                    waits.append(f"wdone{cg - 2}")

                def lw_f(gp, cg=cg):
                    pa = cg % 2
                    out = []
                    for k in range(NCH):
                        out.append(gp.dma_start(
                            out=wgt_g[:, pa, :, k, :],
                            in_=wg4[:, k, cg * 1024:(cg + 1) * 1024]))
                        out.append(gp.dma_start(
                            out=wgt_u[:, pa, :, k, :],
                            in_=wu4[:, k, cg * 1024:(cg + 1) * 1024]))
                    out.append(gp.dma_start(
                        out=wgt_d[:, pa],
                        in_=wd3[:, cg * CPG:(cg + 1) * CPG]))
                    out.append(gp.dma_start(
                        out=bgu_t[:, pa],
                        in_=bgu3[:, cg * CPG:(cg + 1) * CPG]))
                    return out
                s.add("gpsimd", lw_f, waits=waits, label=f"lwg{cg}",
                      dma="dwt", n_dma=18)

            for cg in range(NCG):
                pa = cg % 2
                for gi in range(NTG):
                    gpa = gi % 2
                    waits = []
                    if cg > 0 or gi >= 2:
                        waits.append(f"mid{cg}_{gi - 2}_7" if gi >= 2 else
                                     f"mid{cg - 1}_{6 + gi}_7")
                    s.add("sync",
                          lambda sy, cg=cg, gi=gi, gpa=gpa: sy.dma_start(
                              out=xn_g[:, gpa],
                              in_=xnc3[:, :, gi * TG:(gi + 1) * TG]),
                          waits=waits, label=f"f2x{cg}_{gi}", dma="dxi")

                    for ci in range(CPG):
                        waits = [f"f2x{cg}_{gi}", f"lwg{cg}"]
                        if not (cg == 0 and gi == 0 and ci < 2):
                            waits.append(
                                f"mid{cg}_{gi}_{ci - 2}" if ci >= 2 else
                                (f"mid{cg}_{gi - 1}_{6 + ci}" if gi > 0 else
                                 f"mid{cg - 1}_7_{6 + ci}"))

                        def ag_f(te, cg=cg, pa=pa, gi=gi, gpa=gpa, ci=ci):
                            for k in range(NCH):
                                te.matmul(pg[:, ci % 2],
                                          wgt_g[:, pa, ci, k],
                                          xn_g[:, gpa, k], start=(k == 0),
                                          stop=(k == NCH - 1))
                            for k in range(NCH):
                                te.matmul(pu[:, ci % 2],
                                          wgt_u[:, pa, ci, k],
                                          xn_g[:, gpa, k], start=(k == 0),
                                          stop=(k == NCH - 1))
                        s.add("tensor", ag_f, waits=waits,
                              label=f"agu{cg}_{gi}_{ci}")

                        wtg = [f"agu{cg}_{gi}_{ci}"]
                        if not (cg == 0 and gi == 0 and ci == 0):
                            wtg.append(
                                f"mid{cg}_{gi}_{ci - 1}" if ci > 0 else
                                (f"mid{cg}_{gi - 1}_7" if gi > 0 else
                                 f"mid{cg - 1}_7_7"))
                        s.add("scalar",
                              lambda se, pa=pa, ci=ci: se.activation(
                                  out=t_g[:], in_=pg[:, ci % 2],
                                  func=AF.Silu,
                                  bias=bgu_t[:, pa, ci, 0:1], scale=1.0),
                              waits=wtg, label=f"tg{cg}_{gi}_{ci}")

                        s.add("vector",
                              lambda ve, pa=pa, ci=ci:
                              ve.scalar_tensor_tensor(
                                  mid[:, ci], pu[:, ci % 2],
                                  bgu_t[:, pa, ci, 1:2], t_g[:],
                                  OP.add, OP.mult),
                              waits=[f"tg{cg}_{gi}_{ci}"],
                              label=f"mid{cg}_{gi}_{ci}")

                    for oc in range(8):
                        waits = [f"mid{cg}_{gi}_7"]
                        if not (cg == 0 and gi == 0 and oc < 2):
                            waits.append(
                                f"oacc{cg}_{gi}_{oc - 2}" if oc >= 2 else
                                (f"oacc{cg}_{gi - 1}_{6 + oc}" if gi > 0
                                 else f"oacc{cg - 1}_7_{6 + oc}"))

                        def wd_f(te, cg=cg, pa=pa, gi=gi, oc=oc):
                            for ci in range(CPG):
                                te.matmul(po[:, oc % 2],
                                          wgt_d[:, pa, ci,
                                                oc * 128:(oc + 1) * 128],
                                          mid[:, ci], start=(ci == 0),
                                          stop=(ci == CPG - 1))
                        s.add("tensor", wd_f, waits=waits,
                              label=f"wd{cg}_{gi}_{oc}")

                        wacc = [f"wd{cg}_{gi}_{oc}"]
                        if cg == 0 and gi == 0 and oc == 0:
                            wacc.append("lacc")

                        def acc_f(ve, gi=gi, oc=oc):
                            ve.tensor_add(acc[:, oc, gi * TG:(gi + 1) * TG],
                                          acc[:, oc, gi * TG:(gi + 1) * TG],
                                          po[:, oc % 2])
                        s.add("vector", acc_f, waits=wacc,
                              label=f"oacc{cg}_{gi}_{oc}")
                s.labels[f"wdone{cg}"] = s.labels[f"oacc{cg}_7_7"]
            s.emit()
        p.barrier()

        if "dbg_acc" in dbg:
            dump_fm(p, dbg["dbg_acc"], acc[:])

        with (
            nc.sbuf_tensor("f2_x", [128, 2, NCH, TG], F32) as x_t,
            nc.sbuf_tensor("f2_s", [128, 2, NCH, TG], BF16) as s_t,
            nc.sbuf_tensor("f2_gk", [128, TG], BF16) as gk,
            nc.sbuf_tensor("f2_o", [128, 2, NCH, TG], F32) as o_t,
        ):
            s = Sched(p, "f2b")
            for gi in range(NTG):
                gpa = gi % 2
                waits = []
                if gi >= 2:
                    waits.append(f"cmb{gi - 2}")
                    waits.append(f"co{gi - 2}")

                def lx_f(sy, gi=gi, gpa=gpa):
                    return [
                        sy.dma_start(out=x_t[:, gpa],
                                     in_=x3[:, :, gi * TG:(gi + 1) * TG]),
                        sy.dma_start(out=s_t[:, gpa],
                                     in_=sc3[:, :, gi * TG:(gi + 1) * TG]),
                    ]
                s.add("sync", lx_f, waits=waits, label=f"cx{gi}",
                      dma="dxi", n_dma=2)

                for k in range(NCH):
                    wgk = [f"cx{gi}"]
                    if k > 0:
                        wgk.append(f"cmk{gi}_{k - 1}")
                    elif gi > 0:
                        wgk.append(f"cmk{gi - 1}_7")
                    s.add("scalar",
                          lambda se, gi=gi, k=k: se.activation(
                              out=gk[:],
                              in_=acc[:, k, gi * TG:(gi + 1) * TG],
                              func=AF.Silu, bias=bd_t[:, k, 0:1],
                              scale=1.0),
                          waits=wgk, label=f"cgk{gi}_{k}")

                    def cm_f(ve, gi=gi, gpa=gpa, k=k):
                        ve.tensor_mul(gk[:], s_t[:, gpa, k], gk[:])
                        ve.tensor_add(o_t[:, gpa, k], x_t[:, gpa, k], gk[:])
                    s.add("vector", cm_f, waits=[f"cgk{gi}_{k}"],
                          label=f"cmk{gi}_{k}")
                s.labels[f"cmb{gi}"] = s.labels[f"cmk{gi}_7"]

                s.add("sync",
                      lambda sy, gi=gi, gpa=gpa: sy.dma_start(
                          out=o3[:, :, gi * TG:(gi + 1) * TG],
                          in_=o_t[:, gpa]),
                      waits=[f"cmb{gi}"], label=f"co{gi}", dma="dout")

            def f2fin(sy):
                sy.wait_ge(p.sem["dout"], p.cnt["dout"])
            s.add("sync", f2fin, label="f2fin")
            s.emit()



def stage_inputs(inputs):
    x = np.asarray(inputs["L0_attn_x"])
    filt = np.ascontiguousarray(np.concatenate(
        [np.asarray(inputs["L0_attn_h_filter"]).T,
         np.asarray(inputs["L0_attn_g_filter"]).T,
         np.asarray(inputs["L0_attn_h_synth"]).T,
         np.asarray(inputs["L0_attn_g_synth"]).T], axis=1)).astype(np.float32)
    gb = np.ascontiguousarray(np.stack(
        [np.asarray(inputs["L0_attn_norm_gamma"]),
         np.asarray(inputs["L0_attn_norm_beta"])], axis=1)).astype(np.float32)
    keysT = np.ascontiguousarray(
        np.transpose(np.asarray(inputs["L0_ffn_keys"]), (2, 0, 3, 1))
    ).astype(np.float32)
    dnT = np.ascontiguousarray(
        np.asarray(inputs["L0_ffn_down_embed"]).T).astype(np.float32)
    bgu = np.ascontiguousarray(np.stack(
        [np.asarray(inputs["L0_ffn_bg"]),
         np.asarray(inputs["L0_ffn_bu"])], axis=1)).astype(np.float32)
    shared = {
        "filt": filt, "gb": gb,
        "wq": np.ascontiguousarray(np.asarray(inputs["L0_ffn_Wq"],
                                              np.float32)),
        "keysT": keysT, "dnT": dnT,
        "ue": np.ascontiguousarray(np.asarray(inputs["L0_ffn_up_embed"],
                                              np.float32)),
        "wg": np.ascontiguousarray(np.asarray(inputs["L0_ffn_Wg"],
                                              np.float32)),
        "wu": np.ascontiguousarray(np.asarray(inputs["L0_ffn_Wu"],
                                              np.float32)),
        "wd": np.ascontiguousarray(np.asarray(inputs["L0_ffn_Wd"],
                                              np.float32)),
        "bgu": bgu,
        "bd": np.ascontiguousarray(
            np.asarray(inputs["L0_ffn_bd"], np.float32)[:, None]),
    }
    per_core = []
    for b in range(8):
        m = dict(shared)
        m["x"] = np.ascontiguousarray(x[b].T).astype(np.float32)
        per_core.append(m)
    return per_core


def kernel(**inputs):
    nc = build_program()
    in_maps = stage_inputs(inputs)
    res = run_bass_kernel_spmd(nc, in_maps, core_ids=list(range(8)))
    out = np.stack([np.asarray(r["out"]).T for r in res.results])
    return out.astype(np.float32)
